# version 34
# baseline (speedup 1.0000x reference)
import numpy as np
import ml_dtypes
from contextlib import ExitStack

import concourse.bass as bass
import concourse.mybir as mybir
from concourse.bass_utils import run_bass_kernel_spmd

F32 = mybir.dt.float32
BF16 = mybir.dt.bfloat16
AF = mybir.ActivationFunctionType
ALU = mybir.AluOpType
AX = mybir.AxisListType

NCORES = 8
B_LOC = 4
T = 2048
C = 1024
NCC = 8
TC = T + 1
H_ATT = 8
D_ATT = 64
INNER = 512
WINDOW = 512
NW = T // WINDOW
HID = 2730
HIDP = 2816
NHC = HIDP // 128
QK_SCALE = 8.0
LN_EPS = 1e-5

ENG = ("t", "v", "s", "g", "y")
ENG_FULL = {"t": "tensor", "v": "vector", "s": "scalar", "g": "gpsimd", "y": "sync"}
SELF_WAIT = {"t": False, "v": True, "s": True, "g": True, "y": False}


class _Op:
    __slots__ = ("fn", "waits", "inc")

    def __init__(self, fn):
        self.fn = fn
        self.waits = []
        self.inc = None


class Buf:
    __slots__ = ("h", "name", "shape", "dtype")

    def __init__(self, h, name, shape, dtype):
        self.h = h
        self.name = name
        self.shape = shape
        self.dtype = dtype

    def __getitem__(self, key):
        return self.h.ap()[key]

    def ap(self):
        return self.h.ap()


class Prog:
    def __init__(self, nc, ctx):
        self.nc = nc
        self.ctx = ctx
        self.ops = {e: [] for e in ENG}
        self.sem = {e: ctx.enter_context(nc.semaphore(f"c_{e}")) for e in ENG}
        self.dsem = {e: ctx.enter_context(nc.semaphore(f"d_{e}")) for e in ("y", "g", "s")}
        self.sem_n = {e: 0 for e in ENG}
        self.dsem_n = {e: 0 for e in ENG}
        self.waited = {}
        self.dma_barrier = {}
        self.regions = {}
        self.out_dma_tokens = []
        self.bufs = {}
        self._names = {}

    def uname(self, base):
        n = self._names.get(base, 0)
        self._names[base] = n + 1
        return base if n == 0 else f"{base}_{n}"

    def sbuf(self, name, shape, dtype):
        h = self.ctx.enter_context(self.nc.sbuf_tensor(self.uname(name), list(shape), dtype))
        b = Buf(h, name, tuple(shape), dtype)
        self.bufs[name] = b
        return b

    def psum(self, name, shape, dtype):
        h = self.ctx.enter_context(self.nc.psum_tensor(self.uname(name), list(shape), dtype))
        b = Buf(h, name, tuple(shape), dtype)
        self.bufs[name] = b
        return b

    def _ensure_inc(self, e, pos, kind):
        lst = self.ops[e]
        if kind == "d":
            op = lst[pos]
            assert op.inc is not None
            self.dma_barrier[e] = self.dsem_n[e]
            return (op.inc[0], 16 * self.dsem_n[e])
        for i in range(pos, len(lst)):
            inc = lst[i].inc
            if inc is not None and inc[0] is self.sem[e]:
                return (inc[0], inc[2])
        for i in range(len(lst) - 1, pos - 1, -1):
            if lst[i].inc is None:
                self.sem_n[e] += 1
                lst[i].inc = (self.sem[e], 1, self.sem_n[e])
                return (self.sem[e], self.sem_n[e])
        raise RuntimeError(f"no inc slot on engine {e} at pos>={pos}")

    def op_slow_dma(self, e, fn, writes=()):
        self.op(e, fn, reads=(), writes=writes, dma=True)

    def op(self, e, fn, reads=(), writes=(), dma=False):
        o = _Op(fn)
        lst = self.ops[e]
        pos = len(lst)
        need = {}
        for (buf, p0, p1, f0, f1) in reads:
            for a in self.regions.get(buf.name, ()):
                if a[0] == "w" and a[3] < p1 and p0 < a[4] and a[5] < f1 and f0 < a[6]:
                    need[(a[1], a[2], a[7])] = True
        for (buf, p0, p1, f0, f1) in writes:
            for a in self.regions.get(buf.name, ()):
                if a[3] < p1 and p0 < a[4] and a[5] < f1 and f0 < a[6]:
                    need[(a[1], a[2], a[7])] = True
        for (ae, apos, akind) in sorted(need):
            if ae == e and akind != "d" and not SELF_WAIT[e]:
                continue
            sem, val = self._ensure_inc(ae, apos, akind)
            key = (e, sem.name)
            if self.waited.get(key, -1) < val:
                o.waits.append((sem, val))
                self.waited[key] = val
        if dma:
            if lst and lst[-1].inc is not None and lst[-1].inc[0] is self.dsem[e]:
                lst.append(_Op(lambda eng: eng.nop()))
            self.dsem_n[e] += 1
            o.inc = (self.dsem[e], 16, 16 * self.dsem_n[e])
        kind = "d" if dma else "c"
        pos = len(lst)
        lst.append(o)
        for (buf, p0, p1, f0, f1) in writes:
            rl = self.regions.setdefault(buf.name, [])
            rl[:] = [a for a in rl if not (p0 <= a[3] and a[4] <= p1 and f0 <= a[5] and a[6] <= f1)]
            rl.append(("w", e, pos, p0, p1, f0, f1, kind))
        for (buf, p0, p1, f0, f1) in reads:
            rl = self.regions.setdefault(buf.name, [])
            rl[:] = [a for a in rl
                     if not (a[0] == "r" and a[1] == e and a[3] == p0 and a[4] == p1
                             and a[5] == f0 and a[6] == f1)]
            rl.append(("r", e, pos, p0, p1, f0, f1, kind))
        return (e, pos, kind)

    def phase_barrier(self):
        pairs = {}
        for oe in ENG:
            lst = self.ops[oe]
            if not lst:
                continue
            last = lst[-1]
            if last.inc is not None:
                pairs[oe] = (last.inc[0], last.inc[2])
            else:
                pairs[oe] = self._ensure_inc(oe, len(lst) - 1, "c")
        for e in ENG:
            mypairs = [v for k, v in pairs.items() if k != e]

            def fn(eng, mypairs=tuple(mypairs)):
                last = None
                for s, v in mypairs:
                    last = eng.wait_ge(s, v)
                return last
            o = _Op(fn)
            self.ops[e].append(o)
            for s, v in mypairs:
                key = (e, s.name)
                if self.waited.get(key, -1) < v:
                    self.waited[key] = v
        self.regions.clear()

    def wait_tokens(self, e, toks):
        pairs = [self._ensure_inc(*t) for t in toks]
        o = _Op(None)

        def fn(eng, pairs=tuple(pairs)):
            last = None
            for s, v in pairs:
                last = eng.wait_ge(s, v)
            return last
        o.fn = fn
        self.ops[e].append(o)

    def finalize_into_block(self, block):
        for e in ENG:
            section = getattr(block, ENG_FULL[e])
            ops = self.ops[e]

            def make(ops):
                def body(eng):
                    for o in ops:
                        for sem, val in o.waits:
                            eng.wait_ge(sem, val)
                        inst = o.fn(eng)
                        if o.inc is not None:
                            inst.then_inc(o.inc[0], o.inc[1])

                return body
            section(make(ops))


def r(buf, p0, p1, f0, f1):
    return (buf, p0, p1, f0, f1)


def rfull(buf):
    return (buf, 0, buf.shape[0], 0, buf.shape[1])





def build_program(nbatch=B_LOC, stages=("rwkv", "att", "mlp"), dumps=(), rwkv_sub="full"):
    nc = bass.Bass("TRN2", target_bir_lowering=False)
    dt_in = {}

    def din(name, shape):
        t = nc.dram_tensor(name, list(shape), F32, kind="ExternalInput")
        dt_in[name] = t
        return t

    x_in = din("x", (B_LOC * T, C))
    out_t = nc.dram_tensor("out", [B_LOC * T, C], F32, kind="ExternalOutput")

    din("wkT", (C, C)); din("wvT", (C, C)); din("wrT", (C, C)); din("woT", (C, C))
    din("mixk", (C,)); din("mixv", (C,)); din("mixr", (C,))
    din("tdec", (16,)); din("tfirst", (16,))
    for l in range(2):
        din(f"wqkv{l}", (C, 3 * INNER))
        din(f"qsc{l}", (D_ATT,)); din(f"ksc{l}", (D_ATT,))
        din(f"woffn{l}", (INNER, C))
        din(f"lng{l}", (C,)); din(f"lnb{l}", (C,))
        din(f"wa{l}", (C, HIDP)); din(f"wg{l}", (C, HIDP))
        din(f"b1a{l}", (HIDP,)); din(f"b1g{l}", (HIDP,))
        din(f"w2_{l}", (HIDP, C)); din(f"b2_{l}", (C,))

    ctx = ExitStack()
    p = Prog(nc, ctx)
    dump_toks = []

    def dump(buf):
        dt = nc.dram_tensor(f"dump_{buf.name}", list(buf.shape), buf.dtype,
                            kind="ExternalOutput")
        tok = p.op("y", lambda e, buf=buf, dt=dt: e.dma_start(
            out=dt.ap()[:, :], in_=buf[:, :]), reads=[rfull(buf)], dma=True)
        dump_toks.append(tok)

    with ctx:
        X1 = p.sbuf("X1", (128, NCC * TC), F32)
        Z = p.sbuf("Z", (128, NCC * TC), F32)
        IDNF = p.sbuf("IDNF", (128, 128), F32)
        IDNB = p.sbuf("IDNB", (128, 128), BF16)
        ONE1B = p.sbuf("ONE1B", (128, 1), BF16)
        ONE1F = p.sbuf("ONE1F", (1, 128), F32)
        ONE1BR = p.sbuf("ONE1BR", (1, 128), BF16)
        ONES2 = p.sbuf("ONES2", (128, 2), BF16)
        SEL2 = p.sbuf("SEL2", (2, 128), BF16)
        SEL2A = p.sbuf("SEL2A", (1, 128), BF16)
        SEL2B = p.sbuf("SEL2B", (1, 128), BF16)
        MCAT = p.sbuf("MCAT", (128, 256), BF16)
        MIXC = p.sbuf("MIXC", (128, 24), F32)
        EWC = p.sbuf("EWC", (128, 8), F32)
        EUC = p.sbuf("EUC", (128, 8), F32)
        EW2 = p.sbuf("EW2", (2, 8), F32)
        EU2 = p.sbuf("EU2", (2, 8), F32)
        SA = p.sbuf("SA", (128, 8), F32)
        SB2 = p.sbuf("SB2", (2, 8), F32)
        STATB = p.sbuf("STATB", (128, 512), BF16)
        SUMT = p.sbuf("SUMT", (128, 512), F32)
        MUC = p.sbuf("MUC", (128, 4), F32)
        KEPS = p.sbuf("KEPS", (128, 1), F32)
        PQC = p.sbuf("PQC", (128, 128), F32)
        PQS = p.sbuf("PQS", (128, 128), F32)
        PKC = p.sbuf("PKC", (128, 128), F32)
        PKS = p.sbuf("PKS", (128, 128), F32)
        RSTC = p.sbuf("RSTC", (128, 256), F32)

        PB = [p.psum(f"PB{i}", (128, 512), F32) for i in range(4)]
        PTRN = p.psum("PTRN", (128, 512), F32)
        PTRB = p.psum("PTRB", (128, 512), BF16)
        PROW = p.psum("PROW", (128, 512), F32)
        PAV = p.psum("PAV", (65, 512), F32)

        def tcol(cc, t):
            return cc * TC + 1 + t

        def aff(buf, sl, cmp, fill, base, pattern, cm):
            p.op("g", lambda eng: eng.affine_select(out=sl, in_=sl, compare_op=cmp,
                                                    fill=fill, base=base, pattern=pattern,
                                                    channel_multiplier=cm),
                 reads=[rfull(buf)], writes=[rfull(buf)])

        def init_consts():
            for buf in (IDNF, IDNB):
                p.op("g", lambda e, b=buf: e.memset(b[:, :], 0.0), writes=[rfull(buf)])
                aff(buf, buf[:, :], ALU.not_equal, 1.0, 0, [[-1, 128]], 1)
            p.op("g", lambda e: e.memset(ONE1B[:, :], 1.0), writes=[rfull(ONE1B)])
            p.op("g", lambda e: e.memset(ONE1F[:, :], 1.0), writes=[rfull(ONE1F)])
            p.op("g", lambda e: e.memset(ONE1BR[:, :], 1.0), writes=[rfull(ONE1BR)])
            p.op("g", lambda e: e.memset(ONES2[:, :], 0.0), writes=[rfull(ONES2)])
            aff(ONES2, ONES2[:, 0:1], ALU.is_ge, 1.0, -64, [[0, 1]], 1)
            aff(ONES2, ONES2[:, 1:2], ALU.is_gt, 1.0, 64, [[0, 1]], -1)
            p.op("g", lambda e: e.memset(SEL2[:, :], 0.0), writes=[rfull(SEL2)])
            p.op("g", lambda e: e.memset(SEL2A[:, :], 0.0), writes=[rfull(SEL2A)])
            aff(SEL2A, SEL2A[:, :], ALU.is_ge, 1.0, -64, [[1, 128]], 0)
            p.op("g", lambda e: e.memset(SEL2B[:, :], 0.0), writes=[rfull(SEL2B)])
            aff(SEL2B, SEL2B[:, :], ALU.is_gt, 1.0, 64, [[-1, 128]], 0)
            aff(SEL2, SEL2[:, :], ALU.is_ge, 1.0, -64, [[1, 128]], 999)
            aff(SEL2, SEL2[:, :], ALU.is_gt, 1.0, 1063, [[-1, 128]], -999)
            p.op("g", lambda e: e.memset(MCAT[:, :], 0.0), writes=[rfull(MCAT)])
            aff(MCAT, MCAT[:, 0:128], ALU.is_gt, 1.0, 0, [[-1, 128]], 1)
            aff(MCAT, MCAT[:, 128:256], ALU.is_gt, 1.0, 0, [[1, 128]], -1)
            for i, nm in enumerate(("mixk", "mixv", "mixr")):
                def fn(e, nm=nm, i=i):
                    src = bass.AP(dt_in[nm], 0, [[1, 128], [128, 8]])
                    return e.dma_start(out=MIXC[:, 8 * i:8 * (i + 1)], in_=src,
                                       allow_slow_non_contiguous=True)
                p.op_slow_dma("g", fn, writes=[r(MIXC, 0, 128, 8 * i, 8 * (i + 1))])

            def rep(dst, nm, rows):
                if rows == 128:
                    for hh in range(2):
                        def fn(e, dst=dst, nm=nm, hh=hh):
                            ap = bass.AP(dt_in[nm], hh, [[0, 64], [2, 8]])
                            return e.dma_start(out=dst[64 * hh:64 * (hh + 1), :], in_=ap,
                                               allow_slow_non_contiguous=True)
                        p.op_slow_dma("g", fn, writes=[r(dst, 64 * hh, 64 * (hh + 1), 0, 8)])
                else:
                    def fn(e, dst=dst, nm=nm):
                        ap = bass.AP(dt_in[nm], 0, [[1, 2], [2, 8]])
                        return e.dma_start(out=dst[:, :], in_=ap,
                                           allow_slow_non_contiguous=True)
                    p.op_slow_dma("g", fn, writes=[rfull(dst)])
            rep(EWC, "tdec", 128); rep(EUC, "tfirst", 128)
            rep(EW2, "tdec", 2); rep(EU2, "tfirst", 2)
            for buf in (EWC, EW2):
                p.op("s", lambda e, b=buf: e.activation(out=b[:, :], in_=b[:, :], func=AF.Exp),
                     reads=[rfull(buf)], writes=[rfull(buf)])
                p.op("s", lambda e, b=buf: e.activation(out=b[:, :], in_=b[:, :], func=AF.Exp,
                                                        scale=-1.0),
                     reads=[rfull(buf)], writes=[rfull(buf)])
            for buf in (EUC, EU2):
                p.op("s", lambda e, b=buf: e.activation(out=b[:, :], in_=b[:, :], func=AF.Exp),
                     reads=[rfull(buf)], writes=[rfull(buf)])
            p.op("v", lambda e: e.memset(KEPS[:, :], LN_EPS), writes=[rfull(KEPS)])
            for buf in (X1, Z):
                def zc(e, b=buf):
                    ap = bass.AP(b.h, 0, [[NCC * TC, 128], [TC, 8], [1, 1]])
                    return e.memset(ap, 0.0)
                p.op("v", zc, writes=[r(buf, 0, 128, 0, buf.shape[1])])

        def init_rotary():
          with ExitStack() as ph:
            TWO_PI = 2.0 * np.pi
            mk = lambda name, shape, dt: Buf(
                ph.enter_context(nc.sbuf_tensor(p.uname(name), list(shape), dt)),
                name, tuple(shape), dt)
            IVFC = mk("IVFC", (128, 32), F32)
            LNBC = mk("LNBC", (128, 32), F32)
            BKIC = mk("BKIC", (128, 32), mybir.dt.int32)
            for d2 in range(32):
                ivf = 1.0 / 10000.0 ** (2.0 * d2 / 64.0)
                lnb = float(np.log((2.0 * d2 + 0.4 * 64.0) / (1.4 * 64.0)))
                bsq = float(np.exp(2.0 * lnb))
                c5, s5 = float(np.cos(512.0 * ivf)), float(np.sin(512.0 * ivf))
                cb, sb = float(np.cos(1536.0 * ivf)), float(-np.sin(1536.0 * ivf))
                b3 = float(np.exp(-6.0 * lnb))
                for k4, v in enumerate((c5 * bsq, s5 * bsq, c5 / bsq, s5 / bsq,
                                        cb * b3, sb * b3, cb / b3, sb / b3)):
                    col = 32 * k4 + d2
                    p.op("g", lambda e, c=float(np.float32(v)), col=col: e.memset(
                        RSTC[:, col:col + 1], c),
                        writes=[r(RSTC, 0, 128, col, col + 1)])
                p.op("g", lambda e, c=float(np.float32(ivf)), d2=d2: e.memset(
                    IVFC[:, d2:d2 + 1], c), writes=[r(IVFC, 0, 128, d2, d2 + 1)])
                p.op("g", lambda e, c=float(np.float32(lnb)), d2=d2: e.memset(
                    LNBC[:, d2:d2 + 1], c), writes=[r(LNBC, 0, 128, d2, d2 + 1)])
            S = SUMT
            for tt in range(4):
                c0 = 32 * tt
                p.op("g", lambda e: e.iota(S[:, 288:320], pattern=[[0, 32]], base=0,
                                           channel_multiplier=1,
                                           allow_small_or_imprecise_dtypes=True),
                     writes=[r(S, 0, 128, 288, 320)])
                p.op("v", lambda e, tt=tt: e.scalar_tensor_tensor(
                    out=S[:, 0:32], in0=S[:, 288:320], scalar=float(128.0 * tt),
                    in1=IVFC[:, :], op0=ALU.add, op1=ALU.mult),
                    reads=[r(S, 0, 128, 288, 320), rfull(IVFC)],
                    writes=[r(S, 0, 128, 0, 32)])
                for shift, dstc in ((0.0, 192), (float(np.pi / 2), 160)):
                    p.op("v", lambda e, shift=shift: e.tensor_scalar(
                        out=S[:, 32:64], in0=S[:, 0:32], scalar1=float(1.0 / TWO_PI),
                        scalar2=float(shift / TWO_PI), op0=ALU.mult, op1=ALU.add),
                        reads=[r(S, 0, 128, 0, 32)], writes=[r(S, 0, 128, 32, 64)])
                    p.op("v", lambda e: e.tensor_copy(out=BKIC[:, :], in_=S[:, 32:64]),
                         reads=[r(S, 0, 128, 32, 64)], writes=[rfull(BKIC)])
                    p.op("v", lambda e: e.tensor_copy(out=S[:, 64:96], in_=BKIC[:, :]),
                         reads=[rfull(BKIC)], writes=[r(S, 0, 128, 64, 96)])
                    p.op("v", lambda e: e.tensor_sub(S[:, 96:128], S[:, 32:64], S[:, 64:96]),
                         reads=[r(S, 0, 128, 32, 96)], writes=[r(S, 0, 128, 96, 128)])
                    p.op("v", lambda e: e.tensor_scalar(
                        out=S[:, 96:128], in0=S[:, 96:128], scalar1=0.5, scalar2=None,
                        op0=ALU.is_ge),
                        reads=[r(S, 0, 128, 96, 128)], writes=[r(S, 0, 128, 96, 128)])
                    p.op("v", lambda e: e.tensor_add(S[:, 64:96], S[:, 64:96], S[:, 96:128]),
                         reads=[r(S, 0, 128, 64, 128)], writes=[r(S, 0, 128, 64, 96)])
                    p.op("v", lambda e, shift=shift: e.tensor_scalar(
                        out=S[:, 128:160], in0=S[:, 64:96], scalar1=float(-TWO_PI),
                        scalar2=float(shift), op0=ALU.mult, op1=ALU.add),
                        reads=[r(S, 0, 128, 64, 96)], writes=[r(S, 0, 128, 128, 160)])
                    p.op("v", lambda e: e.tensor_add(S[:, 128:160], S[:, 128:160], S[:, 0:32]),
                         reads=[r(S, 0, 128, 0, 32), r(S, 0, 128, 128, 160)],
                         writes=[r(S, 0, 128, 128, 160)])
                    p.op("s", lambda e, dstc=dstc: e.activation(
                        out=S[:, dstc:dstc + 32], in_=S[:, 128:160], func=AF.Sin),
                        reads=[r(S, 0, 128, 128, 160)],
                        writes=[r(S, 0, 128, dstc, dstc + 32)])
                p.op("v", lambda e, tt=tt: e.tensor_scalar(
                    out=S[:, 320:321], in0=S[:, 288:289], scalar1=float(1.0 / 256.0),
                    scalar2=float((128.0 * tt - 1024.0) / 256.0), op0=ALU.mult,
                    op1=ALU.add),
                    reads=[r(S, 0, 128, 288, 289)], writes=[r(S, 0, 128, 320, 321)])
                p.op("v", lambda e: e.tensor_scalar(
                    out=S[:, 32:64], in0=LNBC[:, :], scalar1=S[:, 320:321], scalar2=None,
                    op0=ALU.mult),
                    reads=[rfull(LNBC), r(S, 0, 128, 320, 321)],
                    writes=[r(S, 0, 128, 32, 64)])
                p.op("s", lambda e: e.activation(out=S[:, 224:256], in_=S[:, 32:64],
                                                 func=AF.Exp),
                     reads=[r(S, 0, 128, 32, 64)], writes=[r(S, 0, 128, 224, 256)])
                p.op("s", lambda e: e.activation(out=S[:, 256:288], in_=S[:, 32:64],
                                                 func=AF.Exp, scale=-1.0),
                     reads=[r(S, 0, 128, 32, 64)], writes=[r(S, 0, 128, 256, 288)])
                for dst, trig, sc in ((PQC, 160, 224), (PQS, 192, 224),
                                      (PKC, 160, 256), (PKS, 192, 256)):
                    p.op("v", lambda e, dst=dst, trig=trig, sc=sc, c0=c0: e.tensor_mul(
                        dst[:, c0:c0 + 32], S[:, trig:trig + 32], S[:, sc:sc + 32]),
                        reads=[r(S, 0, 128, trig, trig + 32), r(S, 0, 128, sc, sc + 32)],
                        writes=[r(dst, 0, 128, c0, c0 + 32)])
            p.phase_barrier()

        def load_input_batch(b):
          with ExitStack() as ph:
            XTMr = [Buf(ph.enter_context(nc.sbuf_tensor(p.uname("XTM"), [128, 1024], F32)),
                        "XTM", (128, 1024), F32) for _ in range(2)]
            for tt in range(T // 128):
                t0 = tt * 128
                XTM = XTMr[tt % 2]
                PTX = (PTRN, PB[0])[tt % 2]
                p.op("y", lambda e, b=b, t0=t0, XTM=XTM: e.dma_start(
                    out=XTM[:, :], in_=x_in.ap()[b * T + t0: b * T + t0 + 128, :]),
                    writes=[rfull(XTM)], dma=True)
                for half in range(2):
                    for k in range(4):
                        cc = half * 4 + k
                        p.op("t", lambda e, cc=cc, k=k, XTM=XTM, PTX=PTX: e.transpose(
                            PTX[:, 128 * k:128 * (k + 1)],
                            XTM[:, 128 * cc:128 * (cc + 1)], IDNF[:, :]),
                            reads=[r(XTM, 0, 128, 128 * cc, 128 * (cc + 1)), rfull(IDNF)],
                            writes=[r(PTX, 0, 128, 128 * k, 128 * (k + 1))])
                    cc0 = half * 4

                    def ev(e, cc0=cc0, t0=t0, PTX=PTX):
                        dst = bass.AP(Z.h, tcol(cc0, t0), [[NCC * TC, 128], [TC, 4], [1, 128]])
                        return e.tensor_copy(out=dst, in_=PTX[:, :])
                    p.op("v", ev, reads=[rfull(PTX)],
                         writes=[r(Z, 0, 128, tcol(cc0, t0), tcol(cc0 + 3, t0) + 128)])
            p.phase_barrier()

        def ln_rows_mid():
            (MB, mr, mc), (RB, rr, rc) = p.bufs["LNROWV"]
            MU = MB[mr:mr + 1, mc:mc + 512]
            RS = RB[rr:rr + 1, rc:rc + 512]
            mreg = r(MB, mr, mr + 1, mc, mc + 512)
            rreg = r(RB, rr, rr + 1, rc, rc + 512)
            p.op("v", lambda e: e.tensor_scalar_mul(MU, PROW[0:1, :], 1.0 / C),
                 reads=[r(PROW, 0, 1, 0, 512)], writes=[mreg])
            p.op("v", lambda e: e.tensor_mul(PROW[0:1, :], MU, MU),
                 reads=[mreg, r(PROW, 0, 1, 0, 512)],
                 writes=[r(PROW, 0, 1, 0, 512)])
            p.op("v", lambda e: e.tensor_scalar_mul(RS, PAV[0:1, :], 1.0 / C),
                 reads=[r(PAV, 0, 1, 0, 512)], writes=[rreg])
            p.op("v", lambda e: e.tensor_sub(RS, RS, PROW[0:1, :]),
                 reads=[rreg, r(PROW, 0, 1, 0, 512)], writes=[rreg])
            p.op("s", lambda e: e.activation(out=RS, in_=RS,
                                             func=AF.Ln, bias=KEPS[0:1, 0:1]),
                 reads=[rreg, rfull(KEPS)], writes=[rreg])
            p.op("s", lambda e: e.activation(out=RS, in_=RS, func=AF.Exp, scale=-0.5),
                 reads=[rreg], writes=[rreg])
            p.op("t", lambda e: e.matmul(PROW[:, :], ONE1F[:, :], RS,
                                         start=True, stop=True),
                 reads=[rfull(ONE1F), rreg], writes=[rfull(PROW)])
            p.op("v", lambda e: e.scalar_tensor_tensor(
                out=MU, in0=MU, scalar=-1.0, in1=RS, op0=ALU.mult, op1=ALU.mult),
                reads=[mreg, rreg], writes=[mreg])
            p.op("t", lambda e: e.matmul(PTRN[:, :], ONE1F[:, :], MU,
                                         start=True, stop=True),
                 reads=[rfull(ONE1F), mreg], writes=[rfull(PTRN)])

        def ln_channel_major(src, dst, t0):
            for cc in range(NCC):
                sl = r(src, 0, 128, tcol(cc, t0), tcol(cc, t0) + 512)
                p.op("s", lambda e, cc=cc, t0=t0, src=src: e.copy(
                    out=STATB[:, 0:512], in_=src[:, tcol(cc, t0):tcol(cc, t0) + 512]),
                    reads=[sl], writes=[r(STATB, 0, 128, 0, 512)])
                p.op("t", lambda e, cc=cc: e.matmul(
                    PROW[0:1, :], ONE1B[:, :], STATB[:, 0:512],
                    start=(cc == 0), stop=(cc == NCC - 1)),
                    reads=[rfull(ONE1B), r(STATB, 0, 128, 0, 512)],
                    writes=[r(PROW, 0, 1, 0, 512)])
                p.op("s", lambda e, cc=cc, t0=t0, src=src: e.activation(
                    out=STATB[:, 0:512], in_=src[:, tcol(cc, t0):tcol(cc, t0) + 512],
                    func=AF.Square),
                    reads=[sl], writes=[r(STATB, 0, 128, 0, 512)])
                p.op("t", lambda e, cc=cc: e.matmul(
                    PAV[0:1, :], ONE1B[:, :], STATB[:, 0:512],
                    start=(cc == 0), stop=(cc == NCC - 1)),
                    reads=[rfull(ONE1B), r(STATB, 0, 128, 0, 512)],
                    writes=[r(PAV, 0, 1, 0, 512)])
            ln_rows_mid()
            for cc in range(NCC):
                sl = r(src, 0, 128, tcol(cc, t0), tcol(cc, t0) + 512)
                dl = r(dst, 0, 128, tcol(cc, t0), tcol(cc, t0) + 512)
                p.op("v", lambda e, cc=cc, t0=t0, src=src, dst=dst: e.tensor_mul(
                    dst[:, tcol(cc, t0):tcol(cc, t0) + 512],
                    src[:, tcol(cc, t0):tcol(cc, t0) + 512], PROW[:, :]),
                    reads=[sl, rfull(PROW)], writes=[dl])
                p.op("v", lambda e, cc=cc, t0=t0, dst=dst: e.tensor_add(
                    dst[:, tcol(cc, t0):tcol(cc, t0) + 512],
                    dst[:, tcol(cc, t0):tcol(cc, t0) + 512], PTRN[:, :]),
                    reads=[dl, rfull(PTRN)], writes=[dl])

        def rwkv_batch(b):
            with ExitStack() as ph:
                sb = lambda name, shape, dt: Buf(
                    ph.enter_context(nc.sbuf_tensor(p.uname(name), list(shape), dt)),
                    name, tuple(shape), dt)
                WSLAB = sb("WSLAB", (128, NCC * 1024), BF16)
                LNROW = sb("LNROW", (1, 1024), F32)
                p.bufs["LNROWV"] = ((LNROW, 0, 0), (LNROW, 0, 512))
                KH = sb("KH", (128, NCC * 1024), BF16)
                VH = sb("VH", (128, NCC * 1024), BF16)
                XKB = sb("XKB", (128, NCC * 512), BF16)
                DDr = [sb("DDr", (128, 512), F32)] * 2
                EKVQ = SUMT
                SCN = sb("SCN", (128, 513), F32)
                EKS = sb("EKS", (2, 512), F32)
                SBB = sb("SBB", (2, 513), F32)
                RSIGr = [sb("RSIGr", (128, 512), F32)] * 2
                for nm, bb in (("WSLAB", WSLAB), ("KH", KH), ("VH", VH), ("XKB", XKB)):
                    p.bufs[nm] = bb

                def load_wslab(nm):
                    def fn(e, nm=nm):
                        src = bass.AP(dt_in[nm], 0, [[1024, 128], [131072, 8], [1, 1024]])
                        return e.dma_start(out=WSLAB[:, :].rearrange("p (cc co) -> p cc co", cc=8),
                                           in_=src)
                    return p.op("g", fn, writes=[rfull(WSLAB)], dma=True)

                def mix_tiles(t0, mixoff):
                    for cc in range(NCC):
                        DD = DDr[cc % 2]
                        xc = r(Z, 0, 128, tcol(cc, t0) - 1, tcol(cc, t0) + 512)
                        p.op("v", lambda e, DD=DD, cc=cc, t0=t0: e.tensor_sub(
                            DD[:, :], Z[:, tcol(cc, t0):tcol(cc, t0) + 512],
                            Z[:, tcol(cc, t0) - 1:tcol(cc, t0) + 511]),
                            reads=[xc], writes=[rfull(DD)])
                        p.op("v", lambda e, DD=DD, cc=cc, t0=t0, mo=mixoff: e.scalar_tensor_tensor(
                            out=XKB[:, 512 * cc:512 * (cc + 1)], in0=DD[:, :],
                            scalar=MIXC[:, mo + cc:mo + cc + 1],
                            in1=Z[:, tcol(cc, t0) - 1:tcol(cc, t0) + 511],
                            op0=ALU.mult, op1=ALU.add),
                            reads=[rfull(DD), xc, r(MIXC, 0, 128, mixoff + cc, mixoff + cc + 1)],
                            writes=[r(XKB, 0, 128, 512 * cc, 512 * (cc + 1))])

                def proj_block(dstbuf, hl0, act=None):
                    for co in range(NCC):
                        ps = PB[co % 4]
                        for cc in range(NCC):
                            p.op("t", lambda e, ps=ps, co=co, cc=cc: e.matmul(
                                ps[:, :],
                                WSLAB[:, 1024 * cc + 128 * co:1024 * cc + 128 * (co + 1)],
                                XKB[:, 512 * cc:512 * (cc + 1)],
                                start=(cc == 0), stop=(cc == NCC - 1)),
                                reads=[r(WSLAB, 0, 128, 1024 * cc + 128 * co,
                                         1024 * cc + 128 * (co + 1)),
                                       r(XKB, 0, 128, 512 * cc, 512 * (cc + 1))],
                                writes=[rfull(ps)])
                        p.op("s", lambda e, ps=ps, co=co, hl0=hl0, dstbuf=dstbuf: e.copy(
                            out=dstbuf[:, 1024 * co + hl0:1024 * co + hl0 + 512], in_=ps[:, :]),
                            reads=[rfull(ps)],
                            writes=[r(dstbuf, 0, 128, 1024 * co + hl0, 1024 * co + hl0 + 512)])

                def wkv_quarter(cc, hl0):
                    par = (cc + hl0 // 512) % 2
                    EKB0 = 1536 * par
                    RCP0 = 512 + 1536 * par
                    ksl = r(KH, 0, 128, 1024 * cc + hl0, 1024 * cc + hl0 + 512)
                    vsl = r(VH, 0, 128, 1024 * cc + hl0, 1024 * cc + hl0 + 512)
                    p.op("s", lambda e: e.activation(
                        out=XKB[:, EKB0:EKB0 + 512], in_=KH[:, 1024 * cc + hl0:1024 * cc + hl0 + 512],
                        func=AF.Exp),
                        reads=[ksl], writes=[r(XKB, 0, 128, EKB0, EKB0 + 512)])
                    p.op("v", lambda e: e.tensor_mul(
                        EKVQ[:, :], XKB[:, EKB0:EKB0 + 512], VH[:, 1024 * cc + hl0:1024 * cc + hl0 + 512]),
                        reads=[r(XKB, 0, 128, EKB0, EKB0 + 512), vsl], writes=[rfull(EKVQ)])
                    p.op("t", lambda e: e.matmul(PROW[0:2, :], ONES2[:, :], XKB[:, EKB0:EKB0 + 512],
                                                 start=True, stop=True),
                         reads=[rfull(ONES2), r(XKB, 0, 128, EKB0, EKB0 + 512)], writes=[rfull(PROW)])
                    p.op("v", lambda e: e.tensor_copy(out=EKS[:, :], in_=PROW[0:2, :]),
                         reads=[rfull(PROW)], writes=[rfull(EKS)])
                    p.op("v", lambda e: e.tensor_copy(out=SCN[:, 0:1], in_=SA[:, cc:cc + 1]),
                         reads=[r(SA, 0, 128, cc, cc + 1)], writes=[r(SCN, 0, 128, 0, 1)])
                    p.op("v", lambda e: e.tensor_tensor_scan(
                        out=SCN[:, 1:513], data0=EWC[:, cc:cc + 1].broadcast_to([128, 512]),
                        data1=EKVQ[:, :], initial=SA[:, cc:cc + 1], op0=ALU.mult, op1=ALU.add),
                        reads=[rfull(EKVQ), r(EWC, 0, 128, cc, cc + 1), r(SA, 0, 128, cc, cc + 1)],
                        writes=[r(SCN, 0, 128, 1, 513)])
                    p.op("v", lambda e: e.tensor_copy(out=SA[:, cc:cc + 1], in_=SCN[:, 512:513]),
                         reads=[r(SCN, 0, 128, 512, 513)], writes=[r(SA, 0, 128, cc, cc + 1)])
                    p.op("v", lambda e: e.scalar_tensor_tensor(
                        out=SCN[:, 0:512], in0=EKVQ[:, :], scalar=EUC[:, cc:cc + 1],
                        in1=SCN[:, 0:512], op0=ALU.mult, op1=ALU.add),
                        reads=[rfull(EKVQ), r(EUC, 0, 128, cc, cc + 1), r(SCN, 0, 128, 0, 512)],
                        writes=[r(SCN, 0, 128, 0, 512)])
                    p.op("v", lambda e: e.tensor_copy(out=SBB[:, 0:1], in_=SB2[:, cc:cc + 1]),
                         reads=[r(SB2, 0, 2, cc, cc + 1)], writes=[r(SBB, 0, 2, 0, 1)])
                    p.op("v", lambda e: e.tensor_tensor_scan(
                        out=SBB[:, 1:513], data0=EW2[:, cc:cc + 1].broadcast_to([2, 512]),
                        data1=EKS[:, :], initial=SB2[:, cc:cc + 1], op0=ALU.mult, op1=ALU.add),
                        reads=[rfull(EKS), r(EW2, 0, 2, cc, cc + 1), r(SB2, 0, 2, cc, cc + 1)],
                        writes=[r(SBB, 0, 2, 1, 513)])
                    p.op("v", lambda e: e.tensor_copy(out=SB2[:, cc:cc + 1], in_=SBB[:, 512:513]),
                         reads=[r(SBB, 0, 2, 512, 513)], writes=[r(SB2, 0, 2, cc, cc + 1)])
                    p.op("v", lambda e: e.tensor_scalar(
                        out=EKS[:, :], in0=EKS[:, :], scalar1=EU2[:, cc:cc + 1], scalar2=1e-6,
                        op0=ALU.mult, op1=ALU.max),
                        reads=[rfull(EKS), r(EU2, 0, 2, cc, cc + 1)], writes=[rfull(EKS)])
                    p.op("v", lambda e: e.tensor_add(EKS[:, :], EKS[:, :], SBB[:, 0:512]),
                         reads=[rfull(EKS), r(SBB, 0, 2, 0, 512)], writes=[rfull(EKS)])
                    p.op("s", lambda e: e.activation(out=EKS[:, :], in_=EKS[:, :], func=AF.Ln),
                         reads=[rfull(EKS)], writes=[rfull(EKS)])
                    p.op("s", lambda e: e.activation(out=XKB[0:2, RCP0:RCP0 + 512],
                                                     in_=EKS[:, :], func=AF.Exp, scale=-1.0),
                         reads=[rfull(EKS)], writes=[r(XKB, 0, 2, RCP0, RCP0 + 512)])
                    p.op("t", lambda e: e.matmul(PROW[:, :], SEL2[:, :], XKB[0:2, RCP0:RCP0 + 512],
                                                 start=True, stop=True),
                         reads=[rfull(SEL2), r(XKB, 0, 2, RCP0, RCP0 + 512)], writes=[rfull(PROW)])
                    p.op("v", lambda e: e.tensor_mul(
                        KH[:, 1024 * cc + hl0:1024 * cc + hl0 + 512], SCN[:, 0:512], PROW[:, :]),
                        reads=[r(SCN, 0, 128, 0, 512), rfull(PROW)],
                        writes=[ksl])

                def rwkv_r_block(hl0):
                    for co in range(NCC):
                        ps = PB[co % 4]
                        for cc in range(NCC):
                            p.op("t", lambda e, ps=ps, co=co, cc=cc: e.matmul(
                                ps[:, :],
                                WSLAB[:, 1024 * cc + 128 * co:1024 * cc + 128 * (co + 1)],
                                XKB[:, 512 * cc:512 * (cc + 1)],
                                start=(cc == 0), stop=(cc == NCC - 1)),
                                reads=[r(WSLAB, 0, 128, 1024 * cc + 128 * co,
                                         1024 * cc + 128 * (co + 1)),
                                       r(XKB, 0, 128, 512 * cc, 512 * (cc + 1))],
                                writes=[rfull(ps)])
                        RSIG = RSIGr[co % 2]
                        p.op("s", lambda e, ps=ps, RSIG=RSIG: e.activation(
                            out=RSIG[:, :], in_=ps[:, :], func=AF.Sigmoid),
                            reads=[rfull(ps)], writes=[rfull(RSIG)])
                        p.op("v", lambda e, co=co, hl0=hl0, RSIG=RSIG: e.tensor_mul(
                            VH[:, 1024 * co + hl0:1024 * co + hl0 + 512], RSIG[:, :],
                            KH[:, 1024 * co + hl0:1024 * co + hl0 + 512]),
                            reads=[rfull(RSIG),
                                   r(KH, 0, 128, 1024 * co + hl0, 1024 * co + hl0 + 512)],
                            writes=[r(VH, 0, 128, 1024 * co + hl0, 1024 * co + hl0 + 512)])

                def rwkv_out_block(t0, hl0):
                    for oc in range(NCC):
                        ps = PB[oc % 4]
                        for ic in range(NCC):
                            p.op("t", lambda e, ps=ps, oc=oc, ic=ic, hl0=hl0: e.matmul(
                                ps[:, :],
                                WSLAB[:, 1024 * ic + 128 * oc:1024 * ic + 128 * (oc + 1)],
                                VH[:, 1024 * ic + hl0:1024 * ic + hl0 + 512],
                                start=(ic == 0), stop=(ic == NCC - 1)),
                                reads=[r(WSLAB, 0, 128, 1024 * ic + 128 * oc,
                                         1024 * ic + 128 * (oc + 1)),
                                       r(VH, 0, 128, 1024 * ic + hl0, 1024 * ic + hl0 + 512)],
                                writes=[rfull(ps)])
                        p.op("v", lambda e, ps=ps, oc=oc, t0=t0: e.tensor_add(
                            X1[:, tcol(oc, t0):tcol(oc, t0) + 512],
                            Z[:, tcol(oc, t0):tcol(oc, t0) + 512], ps[:, :]),
                            reads=[r(Z, 0, 128, tcol(oc, t0), tcol(oc, t0) + 512), rfull(ps)],
                            writes=[r(X1, 0, 128, tcol(oc, t0), tcol(oc, t0) + 512)])
                    ln_channel_major(X1, X1, t0)

                p.op("v", lambda e: e.memset(SA[:, :], 0.0), writes=[rfull(SA)])
                p.op("v", lambda e: e.memset(SB2[:, :], 0.0), writes=[rfull(SB2)])
                for half in range(2):
                    th0 = half * 1024
                    for nm, dstbuf, mixoff in (("wkT", KH, 0), ("wvT", VH, 8)):
                        load_wslab(nm)
                        for tb in range(2):
                            mix_tiles(th0 + tb * 512, mixoff)
                            proj_block(dstbuf, tb * 512)
                    if rwkv_sub == "kv":
                        continue
                    for cc in range(NCC):
                        for q in range(2):
                            wkv_quarter(cc, q * 512)
                    if rwkv_sub == "wkv":
                        if half == 1 and b in dump_batches:
                            p.op("v", lambda e: e.tensor_copy(out=SUMT[:, :], in_=PROW[:, :]),
                                 reads=[rfull(PROW)], writes=[rfull(SUMT)])
                            for nm, bb in (("SCN", SCN), ("EKS", EKS), ("SBB", SBB),
                                           ("SUMT", SUMT), ("XKB", XKB)):
                                if nm in dump_stages:
                                    dump(bb)
                        continue
                    load_wslab("wrT")
                    for tb in range(2):
                        mix_tiles(th0 + tb * 512, 16)
                        rwkv_r_block(tb * 512)
                    if rwkv_sub == "r":
                        continue
                    load_wslab("woT")
                    for tb in range(2):
                        rwkv_out_block(th0 + tb * 512, tb * 512)
                if b in dump_batches:
                    for nm in ("KH", "VH"):
                        if nm in dump_stages:
                            dump(p.bufs[nm])
                p.phase_barrier()


        def ffn_att_batch(b, l, R):
            with ExitStack() as ph:
                sb = lambda name, shape, dt: Buf(
                    ph.enter_context(nc.sbuf_tensor(p.uname(name), list(shape), dt)),
                    name, tuple(shape), dt)
                WQKVB = sb("WQKVB", (128, NCC * 512), BF16)
                WOFr = [sb("WOF", (128, 2 * 512), BF16) for _ in range(2)]
                SHr = [sb("SH", (128, NCC * 129), BF16) for _ in range(2)]
                QT32r = [sb("QT32", (128, 512), F32) for _ in range(2)]
                QKB2r = [sb("QKB2", (128, 512), BF16) for _ in range(2)]
                T3Br = [sb("T3B", (128, 512), BF16)] * 2
                QHATr = [sb("QHAT", (128, 512), F32)] * 2
                QKTE = sb("QTE", (64, 8 * 512), BF16)
                KTE0 = sb("KTE0", (64, 8 * 512), BF16)
                KTE1 = sb("KTE1", (64, 8 * 512), BF16)
                QTEv = (QKTE, 0)
                KTEv = [(KTE0, 0), (KTE1, 0)]
                VE = [sb("VE0", (128, 8 * 4 * 65), BF16), sb("VE1", (128, 8 * 4 * 65), BF16)]
                ATTTr = [sb("ATTT", (128, 512), BF16) for _ in range(4)]
                OT = sb("OT", (128, 4 * 512), BF16)
                RC2B = sb("RC2B", (1, 512), BF16)
                RCBS = sb("RCBS", (128, 512), BF16)
                RC2A = RCBS
                RSQ = sb("RSQ", (128, 20), F32)
                SVCOL = sb("SVCOL", (128, 8), F32)
                CSQr = [sb("CSQ", (128, 256), BF16)] * 2
                SSQr = [sb("SSQ", (128, 256), BF16)] * 2
                CSKr = [sb("CSK", (128, 256), BF16)] * 2
                SSKr = [sb("SSK", (128, 256), BF16)] * 2
                TR1 = QT32r[0]
                TR2 = QT32r[1]
                QSB = sb("QSB", (128, 256), BF16)

                def ldw(nm, dst, nicc, ncols):
                    def fn(e, nm=nm, dst=dst, nicc=nicc, ncols=ncols):
                        src = bass.AP(dt_in[nm], 0,
                                      [[ncols, 128], [128 * ncols, nicc], [1, ncols]])
                        dap = bass.AP(dst.h, 0,
                                      [[dst.shape[1], 128], [ncols, nicc], [1, ncols]])
                        return e.dma_start(out=dap, in_=src)
                    return p.op("g", fn, writes=[rfull(dst)], dma=True)


                def load_wof(qh):
                    WOF = WOFr[qh % 2]

                    def fn(e, qh=qh, WOF=WOF):
                        src = bass.AP(dt_in[f"woffn{l}"], qh * 256,
                                      [[1024, 128], [131072, 4], [1, 256]])
                        dap = bass.AP(WOF.h, 0, [[1024, 128], [256, 4], [1, 256]])
                        return e.dma_start(out=dap, in_=src)
                    return p.op("g", fn, writes=[rfull(WOF)], dma=True)

                def load_qkv_slab(cob):
                    def fn(e, cob=cob):
                        src = bass.AP(dt_in[f"wqkv{l}"], cob * 512,
                                      [[1536, 128], [128 * 1536, 8], [1, 512]])
                        dap = bass.AP(WQKVB.h, 0, [[NCC * 512, 128], [512, NCC], [1, 512]])
                        return e.dma_start(out=dap, in_=src)
                    return p.op("g", fn, writes=[rfull(WQKVB)], dma=True)
                for i, nm in enumerate((f"qsc{l}", f"ksc{l}")):
                    p.op("y", lambda e, nm=nm: e.dma_start(
                        out=TR2[0:1, 0:64],
                        in_=bass.AP(dt_in[nm], 0, [[64, 1], [1, 64]])),
                        writes=[r(TR2, 0, 1, 0, 64)], dma=True)
                    p.op("t", lambda e: e.matmul(PROW[:, 0:64], ONE1F[:, :], TR2[0:1, 0:64],
                                                 start=True, stop=True),
                         reads=[rfull(ONE1F), r(TR2, 0, 1, 0, 64)],
                         writes=[r(PROW, 0, 128, 0, 64)])
                    p.op("v", lambda e, i=i: e.tensor_scalar_mul(
                        QSB[:, 128 * i:128 * i + 64], PROW[:, 0:64],
                        8.0 if i == 0 else 1.0),
                        reads=[r(PROW, 0, 128, 0, 64)],
                        writes=[r(QSB, 0, 128, 128 * i, 128 * i + 64)])
                    p.op("v", lambda e, i=i: e.tensor_scalar_mul(
                        QSB[:, 128 * i + 64:128 * i + 96], QSB[:, 128 * i + 32:128 * i + 64], -1.0),
                        reads=[r(QSB, 0, 128, 128 * i, 128 * i + 64)],
                        writes=[r(QSB, 0, 128, 128 * i + 64, 128 * i + 96)])
                    p.op("v", lambda e, i=i: e.tensor_copy(
                        out=QSB[:, 128 * i + 96:128 * i + 128], in_=QSB[:, 128 * i:128 * i + 32]),
                        reads=[r(QSB, 0, 128, 128 * i, 128 * i + 32)],
                        writes=[r(QSB, 0, 128, 128 * i + 96, 128 * i + 128)])
                for vb in VE:
                    def vones(e, vb=vb):
                        ap = bass.AP(vb.h, 64, [[2080, 128], [65, 32], [1, 1]])
                        return e.memset(ap, 1.0)
                    p.op("v", vones, writes=[r(vb, 0, 128, 0, vb.shape[1])])

                def emit_tables(w):
                    for tt in range(4):
                        c0t = tt * 64
                        for dst, mast, qs_off in (
                                (CSQr[0], PQC, 0), (SSQr[0], PQS, 64),
                                (CSKr[0], PKC, 128), (SSKr[0], PKS, 192)):
                            def f1(e, dst=dst, mast=mast, qs_off=qs_off, c0t=c0t, tt=tt):
                                i0 = bass.AP(mast.h, 32 * tt, [[128, 128], [0, 2], [1, 32]])
                                i1 = QSB[:, qs_off:qs_off + 64].rearrange(
                                    "p (a f) -> p a f", a=2)
                                o0 = dst[:, c0t:c0t + 64].rearrange("p (a f) -> p a f", a=2)
                                return e.tensor_tensor(out=o0, in0=i0, in1=i1, op=ALU.mult)
                            p.op("g", f1,
                                 reads=[rfull(mast), r(QSB, 0, 128, qs_off, qs_off + 64)],
                                 writes=[r(dst, 0, 128, c0t, c0t + 64)])

                def advance(back):
                    co = 128 if back else 0
                    r3 = lambda buf: bass.AP(buf.h, 0,
                                             [[buf.shape[1], 128], [32, 4], [1, 32]])
                    r128 = lambda buf: r(buf, 0, 128, 0, 128)
                    for pc, ps, cc0, sc0 in ((PQC, PQS, co + 0, co + 32),
                                             (PKC, PKS, co + 64, co + 96)):
                        CA = bass.AP(RSTC.h, cc0, [[256, 128], [0, 4], [1, 32]])
                        SA_ = bass.AP(RSTC.h, sc0, [[256, 128], [0, 4], [1, 32]])
                        steps = (
                            (TR1, pc, CA), (TR2, pc, SA_), (pc, ps, SA_))
                        for dstb, in0b, capb in steps:
                            p.op("g", lambda e, dstb=dstb, in0b=in0b, capb=capb:
                                 e.tensor_tensor(
                                out=r3(dstb), in0=r3(in0b), in1=capb, op=ALU.mult),
                                reads=[r128(in0b), rfull(RSTC)], writes=[r128(dstb)])
                        p.op("g", lambda e, pc=pc: e.tensor_sub(
                            pc[:, 0:128], TR1[:, 0:128], pc[:, 0:128]),
                            reads=[r128(TR1), r128(pc)], writes=[r128(pc)])
                        p.op("g", lambda e, ps=ps, CA=CA: e.tensor_tensor(
                            out=r3(TR1), in0=r3(ps), in1=CA, op=ALU.mult),
                            reads=[r128(ps), rfull(RSTC)], writes=[r128(TR1)])
                        p.op("g", lambda e, ps=ps: e.tensor_add(
                            ps[:, 0:128], TR1[:, 0:128], TR2[:, 0:128]),
                            reads=[r128(TR1), r128(TR2)], writes=[r128(ps)])

                def sh_build(w, tt):
                    SH = SHr[tt % 2]
                    t0 = w * 512 + tt * 128
                    for cc in range(NCC):
                        p.op("s", lambda e, SH=SH, cc=cc, t0=t0: e.copy(
                            out=SH[:, 129 * cc:129 * (cc + 1)],
                            in_=R[:, tcol(cc, t0) - 1:tcol(cc, t0) + 128]),
                            reads=[r(R, 0, 128, tcol(cc, t0) - 1, tcol(cc, t0) + 128)],
                            writes=[r(SH, 0, 128, 129 * cc, 129 * (cc + 1))])
                    if l == 1 and w > 0 and tt == 0:
                        def fix(e, SH=SH):
                            dst = bass.AP(SH.h, 0, [[NCC * 129, 128], [129, 8], [1, 1]])
                            return e.tensor_copy(out=dst, in_=SVCOL[:, :])
                        p.op("v", fix, reads=[rfull(SVCOL)],
                             writes=[r(SH, 0, 128, 0, SH.shape[1])])

                def lhs_sh(cc, tt):
                    SH = SHr[tt % 2]
                    off = 129 * cc
                    if cc < 4:
                        return SH[:, off + 1:off + 129], r(SH, 0, 128, off + 1, off + 129)
                    return SH[:, off:off + 128], r(SH, 0, 128, off, off + 128)

                def qkv_cob(w, cob, tt):
                    ps = PB[tt % 4]
                    for cc in range(NCC):
                        ap, rd = lhs_sh(cc, tt)
                        p.op("t", lambda e, ps=ps, ap=ap, cc=cc: e.matmul(
                            ps[:, :], ap, WQKVB[:, 512 * cc:512 * (cc + 1)],
                            start=(cc == 0), stop=(cc == NCC - 1)),
                            reads=[rd, r(WQKVB, 0, 128, 512 * cc, 512 * (cc + 1))],
                            writes=[rfull(ps)])
                    return ps

                def v_evac(w, tt, ps):
                    wb = w % 2

                    def vev(e, wb=wb, tt=tt, ps=ps):
                        dst = bass.AP(VE[wb].h, tt * 65, [[2080, 128], [260, 8], [1, 64]])
                        return e.copy(out=dst, in_=ps[:, :])
                    p.op("s", vev, reads=[rfull(ps)],
                         writes=[r(VE[wb], 0, 128, 0, VE[wb].shape[1])])

                def qk_process(w, iq, tt, ps):
                    wb = w % 2
                    QT32 = QT32r[tt % 2]
                    QKB2 = QKB2r[tt % 2]
                    T3B = T3Br[tt % 2]
                    QHAT = QHATr[tt % 2]
                    wr2 = w % 2
                    CST, SST = (CSQr[wr2], SSQr[wr2]) if iq == 0 else (CSKr[wr2], SSKr[wr2])
                    p.op("s", lambda e, QHAT=QHAT, ps=ps: e.activation(out=QHAT[:, :], in_=ps[:, :],
                                                     func=AF.Square),
                         reads=[rfull(ps)], writes=[rfull(QHAT)])
                    p.op("v", lambda e, QHAT=QHAT: e.tensor_reduce(
                        RSQ[:, 0:8], QHAT.ap().rearrange("p (h d) -> p h d", h=8),
                        axis=AX.X, op=ALU.add),
                        reads=[rfull(QHAT)], writes=[r(RSQ, 0, 128, 0, 8)])
                    p.op("v", lambda e: e.tensor_scalar_max(RSQ[:, 8:16], RSQ[:, 0:8], 1e-24),
                         reads=[r(RSQ, 0, 128, 0, 8)], writes=[r(RSQ, 0, 128, 8, 16)])
                    p.op("s", lambda e: e.activation(out=RSQ[:, 8:16], in_=RSQ[:, 8:16],
                                                     func=AF.Ln),
                         reads=[r(RSQ, 0, 128, 8, 16)], writes=[r(RSQ, 0, 128, 8, 16)])
                    p.op("s", lambda e: e.activation(out=RSQ[:, 8:16], in_=RSQ[:, 8:16],
                                                     func=AF.Exp, scale=-0.5),
                         reads=[r(RSQ, 0, 128, 8, 16)], writes=[r(RSQ, 0, 128, 8, 16)])

                    def qhat(e, QHAT=QHAT, ps=ps):
                        i1 = bass.AP(RSQ.h, 8, [[20, 128], [1, 8], [0, 64]])
                        return e.tensor_tensor(out=QHAT[:, :], in0=ps[:, :], in1=i1,
                                               op=ALU.mult)
                    p.op("v", qhat, reads=[rfull(ps), r(RSQ, 0, 128, 8, 16)],
                         writes=[rfull(QHAT)])

                    def t2f(e, CST=CST, tt=tt, QT32=QT32, QHAT=QHAT):
                        i1 = bass.AP(CST.h, tt * 64, [[256, 128], [0, 8], [1, 64]])
                        return e.tensor_tensor(out=QT32[:, :], in0=QHAT[:, :], in1=i1,
                                               op=ALU.mult)
                    p.op("v", t2f, reads=[rfull(QHAT), rfull(CST)], writes=[rfull(QT32)])

                    def t3a(e, SST=SST, tt=tt, QHAT=QHAT, T3B=T3B):
                        i0 = bass.AP(QHAT.h, 32, [[512, 128], [64, 8], [1, 32]])
                        i1 = bass.AP(SST.h, tt * 64, [[256, 128], [0, 8], [1, 32]])
                        o0 = bass.AP(T3B.h, 0, [[512, 128], [64, 8], [1, 32]])
                        return e.tensor_tensor(out=o0, in0=i0, in1=i1, op=ALU.mult)
                    p.op("g", t3a, reads=[rfull(QHAT), rfull(SST)],
                         writes=[r(T3B, 0, 128, 0, 512)])

                    def t3b(e, SST=SST, tt=tt, QHAT=QHAT, T3B=T3B):
                        i0 = bass.AP(QHAT.h, 0, [[512, 128], [64, 8], [1, 32]])
                        i1 = bass.AP(SST.h, tt * 64 + 32, [[256, 128], [0, 8], [1, 32]])
                        o0 = bass.AP(T3B.h, 32, [[512, 128], [64, 8], [1, 32]])
                        return e.tensor_tensor(out=o0, in0=i0, in1=i1, op=ALU.mult)
                    p.op("g", t3b, reads=[rfull(QHAT), rfull(SST)],
                         writes=[r(T3B, 0, 128, 0, 512)])
                    p.op("g", lambda e, QKB2=QKB2, QT32=QT32, T3B=T3B: e.tensor_add(QKB2[:, :], QT32[:, :], T3B[:, 0:512]),
                         reads=[rfull(QT32), r(T3B, 0, 128, 0, 512)], writes=[rfull(QKB2)])
                    for hp in range(4):
                        p.op("t", lambda e, hp=hp, QKB2=QKB2: e.transpose(
                            PTRB[:, 128 * hp:128 * (hp + 1)], QKB2[:, 128 * hp:128 * (hp + 1)],
                            IDNB[:, :]),
                            reads=[r(QKB2, 0, 128, 128 * hp, 128 * (hp + 1)), rfull(IDNB)],
                            writes=[r(PTRB, 0, 128, 128 * hp, 128 * (hp + 1))])
                    dbuf, dp0 = QTEv if iq == 0 else KTEv[wb]
                    for par in range(2):
                        def tev(e, dbuf=dbuf, dp0=dp0, par=par, tt=tt):
                            dst = bass.AP(dbuf.h, dp0 * 4096 + 512 * par + tt * 128,
                                          [[8 * 512, 64], [1024, 4], [1, 128]])
                            src = bass.AP(PTRB.h, 64 * par * 512,
                                          [[512, 64], [128, 4], [1, 128]])
                            return e.tensor_copy(out=dst, in_=src)
                        p.op("v", tev, reads=[rfull(PTRB)],
                             writes=[r(dbuf, dp0, dp0 + 64, 0, 4096)])

                def head_attention(w, h, pav):
                    wb, pb = w % 2, (w - 1) % 2
                    jts = ([("p", jl) for jl in range(4)] if w > 0 else []) +                           [("c", jl) for jl in range(4)]
                    njts = len(jts)
                    units = []
                    for jidx, (kind, jl) in enumerate(jts):
                        ATTT = ATTTr[jidx % 4]
                        PP2 = (PTRN, PB[1])[jidx % 2]
                        if kind == "p":
                            kbuf, kp0 = KTEv[pb]
                            i0, i1 = 0, 128 * (jl + 1)
                            bnd0, bmask = 128 * jl, 128
                            vb = VE[pb]
                        else:
                            kbuf, kp0 = KTEv[wb]
                            i0, i1 = 128 * jl, 512
                            bnd0, bmask = 0, 0
                            vb = VE[wb]
                        units.append((jidx, jl, ATTT, PP2, kbuf, kp0, i0, i1,
                                      bnd0, bmask, vb))

                    def emit_sim(u):
                        jidx, jl, ATTT, PP2, kbuf, kp0, i0, i1, bnd0, bmask, vb = u
                        span = i1 - i0
                        p.op("t", lambda e, kbuf=kbuf, kp0=kp0, jl=jl, h=h, i0=i0, i1=i1,
                             PP2=PP2: e.matmul(
                            PP2[:, i0:i1],
                            kbuf[kp0:kp0 + 64, 512 * h + 128 * jl:512 * h + 128 * (jl + 1)],
                            QKTE[0:64, 512 * h + i0:512 * h + i1], start=True, stop=True),
                            reads=[r(kbuf, kp0, kp0 + 64, 512 * h + 128 * jl, 512 * h + 128 * (jl + 1)),
                                   r(QKTE, 0, 64, 512 * h + i0, 512 * h + i1)],
                            writes=[rfull(PP2)])
                        p.op("s", lambda e, i0=i0, span=span, PP2=PP2, ATTT=ATTT: e.activation(
                            out=ATTT[:, 0:span], in_=PP2[:, i0:i0 + span], func=AF.Exp),
                            reads=[rfull(PP2)], writes=[r(ATTT, 0, 128, 0, span)])
                        p.op("v", lambda e, bnd0=bnd0, bmask=bmask, ATTT=ATTT: e.tensor_mul(
                            ATTT[:, bnd0:bnd0 + 128], ATTT[:, bnd0:bnd0 + 128],
                            MCAT[:, bmask:bmask + 128]),
                            reads=[r(ATTT, 0, 128, bnd0, bnd0 + 128),
                                   r(MCAT, 0, 128, bmask, bmask + 128)],
                            writes=[r(ATTT, 0, 128, bnd0, bnd0 + 128)])

                    def emit_av(u):
                        jidx, jl, ATTT, PP2, kbuf, kp0, i0, i1, bnd0, bmask, vb = u
                        span = i1 - i0
                        p.op("t", lambda e, vb=vb, h=h, jl=jl, i0=i0, span=span,
                             first=(jidx == 0), last=(jidx == njts - 1), pav=pav,
                             ATTT=ATTT: e.matmul(
                            pav[0:65, i0:i0 + span], vb[:, 260 * h + 65 * jl:260 * h + 65 * (jl + 1)],
                            ATTT[:, 0:span], start=first, stop=last),
                            reads=[r(vb, 0, 128, 260 * h + 65 * jl, 260 * h + 65 * (jl + 1)),
                                   r(ATTT, 0, 128, 0, span)],
                            writes=[rfull(pav)])

                    for k in range(njts + 2):
                        if k < njts:
                            emit_sim(units[k])
                        if k >= 2:
                            emit_av(units[k - 2])

                def pair_out(w, hp, pava, pavb):
                    for pav, rcdst in ((pava, RC2A), (pavb, RC2B)):
                        p.op("s", lambda e, pav=pav, rcdst=rcdst: e.activation(
                            out=rcdst[0:1, :], in_=pav[64:65, :], func=AF.Ln),
                            reads=[rfull(pav)], writes=[rfull(rcdst)])
                        p.op("s", lambda e, rcdst=rcdst: e.activation(
                            out=rcdst[0:1, :], in_=rcdst[0:1, :], func=AF.Exp, scale=-1.0),
                            reads=[rfull(rcdst)], writes=[rfull(rcdst)])
                    p.op("t", lambda e: e.matmul(PROW[:, :], SEL2A[:, :], RC2A[0:1, :],
                                                 start=True, stop=False),
                         reads=[rfull(SEL2A), rfull(RC2A)], writes=[rfull(PROW)])
                    p.op("t", lambda e: e.matmul(PROW[:, :], SEL2B[:, :], RC2B[0:1, :],
                                                 start=False, stop=True),
                         reads=[rfull(SEL2B), rfull(RC2B)], writes=[rfull(PROW)])
                    p.op("v", lambda e: e.tensor_copy(out=RCBS[:, :], in_=PROW[:, :]),
                         reads=[rfull(PROW)], writes=[rfull(RCBS)])
                    for par, pav in ((0, pava), (1, pavb)):
                        p.op("v", lambda e, par=par, pav=pav, hp=hp: e.tensor_mul(
                            OT[64 * par:64 * (par + 1), 512 * hp:512 * (hp + 1)],
                            pav[0:64, :], RCBS[64 * par:64 * (par + 1), :]),
                            reads=[rfull(pav), r(RCBS, 64 * par, 64 * (par + 1), 0, 512)],
                            writes=[r(OT, 64 * par, 64 * (par + 1), 512 * hp, 512 * (hp + 1))])

                def wo_out(w):
                    t0 = w * 512
                    if l == 1:
                        def sv(e, t0=t0):
                            src = bass.AP(Z.h, tcol(0, t0 + 511), [[NCC * TC, 128], [TC, 8], [1, 1]])
                            return e.tensor_copy(out=SVCOL[:, :], in_=src)
                        p.op("v", sv, reads=[r(Z, 0, 128, 0, Z.shape[1])], writes=[rfull(SVCOL)])
                    for qh in range(4):
                        WOF = WOFr[qh % 2]
                        for j2 in range(2):
                            oc = 2 * qh + j2
                            ps = PB[oc % 4]
                            for icc in range(4):
                                p.op("t", lambda e, ps=ps, WOF=WOF, j2=j2, icc=icc: e.matmul(
                                    ps[:, :],
                                    WOF[:, 256 * icc + 128 * j2:256 * icc + 128 * (j2 + 1)],
                                    OT[:, 512 * icc:512 * (icc + 1)],
                                    start=(icc == 0), stop=(icc == 3)),
                                    reads=[r(WOF, 0, 128, 256 * icc + 128 * j2,
                                             256 * icc + 128 * (j2 + 1)),
                                           r(OT, 0, 128, 512 * icc, 512 * (icc + 1))],
                                    writes=[rfull(ps)])
                            p.op("v", lambda e, ps=ps, oc=oc, t0=t0: e.tensor_add(
                                Z[:, tcol(oc, t0):tcol(oc, t0) + 512],
                                R[:, tcol(oc, t0):tcol(oc, t0) + 512], ps[:, :]),
                                reads=[r(R, 0, 128, tcol(oc, t0), tcol(oc, t0) + 512), rfull(ps)],
                                writes=[r(Z, 0, 128, tcol(oc, t0), tcol(oc, t0) + 512)])
                        if qh + 2 < 4:
                            load_wof(qh + 2)

                for w in range(NW):
                    emit_tables(w)
                    advance(back=(w == NW - 1))
                    for cob in range(3):
                        if not (cob == 0 and w > 0):
                            load_qkv_slab(cob)
                        for tt in range(4):
                            sh_build(w, tt)
                            ps = qkv_cob(w, cob, tt)
                            if cob == 2:
                                v_evac(w, tt, ps)
                            else:
                                qk_process(w, cob, tt, ps)
                    load_wof(0)
                    load_wof(1)
                    for hp in range(4):
                        pava, pavb = ((PAV, PB[0]), (PB[2], PB[3]))[hp % 2]
                        for par in range(2):
                            h = 2 * hp + par
                            head_attention(w, h, pava if par == 0 else pavb)
                        pair_out(w, hp, pava, pavb)
                    if w + 1 < NW:
                        load_qkv_slab(0)
                    wo_out(w)
                if b in dump_batches:
                    dd = {"QTE": QKTE, "KTE0": KTE0, "VE0": VE[0], "OT": OT,
                          "CSQ": CSQr[0], "SSQ": SSQr[0], "CSK": CSKr[0], "SSK": SSKr[0]}
                    for nm in dd:
                        if nm in dump_stages:
                            dump(dd[nm])
                p.phase_barrier()


        HGROUPS = [5, 5, 4, 4, 4]

        def ffn_mlp_batch(b, l):
            with ExitStack() as ph:
                sb = lambda name, shape, dt: Buf(
                    ph.enter_context(nc.sbuf_tensor(p.uname(name), list(shape), dt)),
                    name, tuple(shape), dt)
                LNZ = sb("LNZ", (128, NCC * 2048), BF16)
                MGELr = [sb("MGEL", (128, 512), F32)] * 2
                p.bufs["LNROWV"] = ((SUMT, 0, 0), (MGELr[0], 0, 0))
                WA = sb("WA", (128, 5 * 1024), BF16)
                WG = sb("WG", (128, 5 * 1024), BF16)
                W2P = sb("W2P", (128, 5 * 1024), BF16)
                HTG = sb("HTG", (128, 5 * 512), BF16)
                LNGC = sb("LNGC", (128, 16), F32)
                B1AC = sb("B1AC", (128, NHC), F32)
                B1GC = sb("B1GC", (128, NHC), F32)
                B2C = sb("B2C", (128, 8), F32)

                def cl(dstbuf, c0, nm, ncols):
                    def fn(e, dstbuf=dstbuf, c0=c0, nm=nm, ncols=ncols):
                        src = bass.AP(dt_in[nm], 0, [[1, 128], [128, ncols]])
                        return e.dma_start(out=dstbuf[:, c0:c0 + ncols], in_=src,
                                           allow_slow_non_contiguous=True)
                    p.op_slow_dma("g", fn, writes=[r(dstbuf, 0, 128, c0, c0 + ncols)])
                cl(LNGC, 0, f"lng{l}", 8)
                cl(LNGC, 8, f"lnb{l}", 8)
                cl(B1AC, 0, f"b1a{l}", NHC)
                cl(B1GC, 0, f"b1g{l}", NHC)
                cl(B2C, 0, f"b2_{l}", 8)

                def zsh(cc, t0, n=512):
                    base = tcol(cc, t0) - (1 if cc >= 4 else 0)
                    return Z[:, base:base + n], r(Z, 0, 128, base, base + n)

                for tb in range(4):
                    t0 = tb * 512
                    for cc in range(NCC):
                        ap, rd = zsh(cc, t0)
                        cb, cb0 = (STATB, 0) if cc % 2 == 0 else (HTG, 512)
                        qb, qb0 = (HTG, 0) if cc % 2 == 0 else (HTG, 1024)
                        p.op("s", lambda e, ap=ap, cb=cb, cb0=cb0: e.copy(
                            out=cb[:, cb0:cb0 + 512], in_=ap),
                            reads=[rd], writes=[r(cb, 0, 128, cb0, cb0 + 512)])
                        p.op("t", lambda e, cc=cc, cb=cb, cb0=cb0: e.matmul(
                            PROW[0:1, :], ONE1B[:, :], cb[:, cb0:cb0 + 512],
                            start=(cc == 0), stop=(cc == NCC - 1)),
                            reads=[rfull(ONE1B), r(cb, 0, 128, cb0, cb0 + 512)],
                            writes=[r(PROW, 0, 1, 0, 512)])
                        p.op("s", lambda e, ap=ap, qb=qb, qb0=qb0: e.activation(
                            out=qb[:, qb0:qb0 + 512], in_=ap, func=AF.Square),
                            reads=[rd], writes=[r(qb, 0, 128, qb0, qb0 + 512)])
                        p.op("t", lambda e, cc=cc, qb=qb, qb0=qb0: e.matmul(
                            PAV[0:1, :], ONE1B[:, :], qb[:, qb0:qb0 + 512],
                            start=(cc == 0), stop=(cc == NCC - 1)),
                            reads=[rfull(ONE1B), r(qb, 0, 128, qb0, qb0 + 512)],
                            writes=[r(PAV, 0, 1, 0, 512)])
                    ln_rows_mid()
                    for cc in range(NCC):
                        ap, rd = zsh(cc, t0)
                        TB1 = SUMT
                        p.op("v", lambda e, ap=ap, TB1=TB1: e.tensor_mul(TB1[:, :], ap, PROW[:, :]),
                             reads=[rd, rfull(PROW)], writes=[rfull(TB1)])
                        def b2f(e, cc=cc):
                            i1 = LNGC[:, 8 + cc:9 + cc].broadcast_to([128, 512])
                            return e.scalar_tensor_tensor(
                                out=STATB[:, 0:512], in0=PTRN[:, :],
                                scalar=LNGC[:, cc:cc + 1], in1=i1,
                                op0=ALU.mult, op1=ALU.add)
                        p.op("v", b2f, reads=[rfull(PTRN), r(LNGC, 0, 128, 0, 16)],
                             writes=[r(STATB, 0, 128, 0, 512)])
                        p.op("v", lambda e, cc=cc, t0=t0, TB1=TB1: e.scalar_tensor_tensor(
                            out=LNZ[:, 2048 * cc + t0:2048 * cc + t0 + 512], in0=TB1[:, :],
                            scalar=LNGC[:, cc:cc + 1], in1=STATB[:, 0:512],
                            op0=ALU.mult, op1=ALU.add),
                            reads=[rfull(TB1), r(LNGC, 0, 128, 0, 8),
                                   r(STATB, 0, 128, 0, 512)],
                            writes=[r(LNZ, 0, 128, 2048 * cc + t0, 2048 * cc + t0 + 512)])

                hid0 = 0
                for gi, gsz in enumerate(HGROUPS):
                    for wsl, nm in ((WA, f"wa{l}"), (WG, f"wg{l}")):
                        for cc in range(NCC):
                            def ldp(e, wsl=wsl, nm=nm, hid0=hid0, gsz=gsz, cc=cc):
                                src = bass.AP(dt_in[nm], hid0 * 128 + cc * 128 * HIDP,
                                              [[HIDP, 128], [128, gsz], [1, 128]])
                                dst = bass.AP(wsl.h, cc * 128,
                                              [[wsl.shape[1], 128], [1024, gsz], [1, 128]])
                                return e.dma_start(out=dst, in_=src)
                            p.op("g", ldp, writes=[rfull(wsl)], dma=True)
                    def ldw2(e, hid0=hid0, gsz=gsz):
                        src = bass.AP(dt_in[f"w2_{l}"], hid0 * 128 * 1024,
                                      [[1024, 128], [128 * 1024, gsz], [1, 1024]])
                        dst = bass.AP(W2P.h, 0,
                                      [[W2P.shape[1], 128], [1024, gsz], [1, 1024]])
                        return e.dma_start(out=dst, in_=src)
                    p.op("g", ldw2, writes=[rfull(W2P)], dma=True)
                    for tb in range(4):
                        t0 = tb * 512
                        for j in range(gsz):
                            hidc = hid0 + j
                            MGEL = MGELr[j % 2]
                            psa = (PB[0], PB[2])[j % 2]
                            psg = (PB[1], PB[3])[j % 2]
                            for ps, wsl in ((psa, WA), (psg, WG)):
                                for cc in range(NCC):
                                    p.op("t", lambda e, ps=ps, wsl=wsl, j=j, cc=cc, t0=t0: e.matmul(
                                        ps[:, :], wsl[:, 1024 * j + 128 * cc:1024 * j + 128 * (cc + 1)],
                                        LNZ[:, 2048 * cc + t0:2048 * cc + t0 + 512],
                                        start=(cc == 0), stop=(cc == NCC - 1)),
                                        reads=[r(wsl, 0, 128, 1024 * j + 128 * cc,
                                                 1024 * j + 128 * (cc + 1)),
                                               r(LNZ, 0, 128, 2048 * cc + t0, 2048 * cc + t0 + 512)],
                                        writes=[rfull(ps)])
                            p.op("s", lambda e, hidc=hidc, MGEL=MGEL, psg=psg: e.activation(
                                out=MGEL[:, :], in_=psg[:, :], func=AF.Gelu,
                                bias=B1GC[:, hidc:hidc + 1]),
                                reads=[rfull(psg), r(B1GC, 0, 128, hidc, hidc + 1)],
                                writes=[rfull(MGEL)])
                            p.op("v", lambda e, hidc=hidc, j=j, MGEL=MGEL, psa=psa:
                                 e.scalar_tensor_tensor(
                                out=HTG[:, 512 * j:512 * (j + 1)], in0=psa[:, :],
                                scalar=B1AC[:, hidc:hidc + 1], in1=MGEL[:, :],
                                op0=ALU.add, op1=ALU.mult),
                                reads=[rfull(psa), r(B1AC, 0, 128, hidc, hidc + 1),
                                       rfull(MGEL)],
                                writes=[r(HTG, 0, 128, 512 * j, 512 * (j + 1))])
                        for och in range(2):
                            for oci in range(4):
                                oc = och * 4 + oci
                                ps = (PB[0], PB[1], PTRN, PB[2])[oci]
                                for j in range(gsz):
                                    p.op("t", lambda e, ps=ps, j=j, oc=oc: e.matmul(
                                        ps[:, :], W2P[:, 1024 * j + 128 * oc:1024 * j + 128 * (oc + 1)],
                                        HTG[:, 512 * j:512 * (j + 1)],
                                        start=(j == 0), stop=(j == gsz - 1)),
                                        reads=[r(W2P, 0, 128, 1024 * j + 128 * oc,
                                                 1024 * j + 128 * (oc + 1)),
                                               r(HTG, 0, 128, 512 * j, 512 * (j + 1))],
                                        writes=[rfull(ps)])
                                if gi == 0:
                                    p.op("v", lambda e, ps=ps, oc=oc, t0=t0: e.scalar_tensor_tensor(
                                        out=Z[:, tcol(oc, t0):tcol(oc, t0) + 512], in0=ps[:, :],
                                        scalar=B2C[:, oc:oc + 1],
                                        in1=Z[:, tcol(oc, t0):tcol(oc, t0) + 512],
                                        op0=ALU.add, op1=ALU.add),
                                        reads=[rfull(ps), r(B2C, 0, 128, oc, oc + 1),
                                               r(Z, 0, 128, tcol(oc, t0), tcol(oc, t0) + 512)],
                                        writes=[r(Z, 0, 128, tcol(oc, t0), tcol(oc, t0) + 512)])
                                else:
                                    p.op("v", lambda e, ps=ps, oc=oc, t0=t0: e.tensor_add(
                                        Z[:, tcol(oc, t0):tcol(oc, t0) + 512],
                                        Z[:, tcol(oc, t0):tcol(oc, t0) + 512], ps[:, :]),
                                        reads=[rfull(ps), r(Z, 0, 128, tcol(oc, t0),
                                                            tcol(oc, t0) + 512)],
                                        writes=[r(Z, 0, 128, tcol(oc, t0), tcol(oc, t0) + 512)])
                    hid0 += gsz
                if b in dump_batches and "LNZ" in dump_stages:
                    dump(LNZ)
                p.phase_barrier()

        def final_ln_store(b):
          with ExitStack() as ph:
            XTM = Buf(ph.enter_context(nc.sbuf_tensor(p.uname("XTM"), [128, 1024], F32)),
                      "XTM", (128, 1024), F32)
            for tt in range(T // 128):
                t0 = tt * 128
                for half in range(2):
                    for k in range(4):
                        cc = half * 4 + k
                        p.op("v", lambda e, cc=cc, k=k, t0=t0: e.tensor_add(
                            SUMT[:, 128 * k:128 * (k + 1)],
                            X1[:, tcol(cc, t0):tcol(cc, t0) + 128],
                            Z[:, tcol(cc, t0):tcol(cc, t0) + 128]),
                            reads=[r(X1, 0, 128, tcol(cc, t0), tcol(cc, t0) + 128),
                                   r(Z, 0, 128, tcol(cc, t0), tcol(cc, t0) + 128)],
                            writes=[r(SUMT, 0, 128, 128 * k, 128 * (k + 1))])
                    for k in range(4):
                        p.op("t", lambda e, k=k: e.transpose(
                            PTRN[:, 128 * k:128 * (k + 1)], SUMT[:, 128 * k:128 * (k + 1)],
                            IDNF[:, :]),
                            reads=[r(SUMT, 0, 128, 128 * k, 128 * (k + 1)), rfull(IDNF)],
                            writes=[r(PTRN, 0, 128, 128 * k, 128 * (k + 1))])
                    p.op("s", lambda e, half=half: e.copy(
                        out=XTM[:, 512 * half:512 * (half + 1)], in_=PTRN[:, :]),
                        reads=[rfull(PTRN)],
                        writes=[r(XTM, 0, 128, 512 * half, 512 * (half + 1))])
                p.op("v", lambda e: e.tensor_reduce(MUC[:, 0:1], XTM[:, :], axis=AX.X,
                                                    op=ALU.add),
                     reads=[rfull(XTM)], writes=[r(MUC, 0, 128, 0, 1)])
                p.op("s", lambda e, XTM=XTM: e.activation(
                    out=STATB[:, :], in_=XTM[:, 0:512], func=AF.Square,
                    accum_out=MUC[:, 1:2]),
                    reads=[rfull(XTM)], writes=[rfull(STATB), r(MUC, 0, 128, 1, 2)])
                p.op("s", lambda e, XTM=XTM: e.activation(
                    out=STATB[:, :], in_=XTM[:, 512:1024], func=AF.Square,
                    accum_out=MUC[:, 3:4]),
                    reads=[rfull(XTM)], writes=[rfull(STATB), r(MUC, 0, 128, 3, 4)])
                p.op("v", lambda e: e.tensor_add(MUC[:, 1:2], MUC[:, 1:2], MUC[:, 3:4]),
                     reads=[r(MUC, 0, 128, 1, 4)], writes=[r(MUC, 0, 128, 1, 2)])
                p.op("v", lambda e: e.tensor_scalar_mul(MUC[:, 0:1], MUC[:, 0:1], 1.0 / C),
                     reads=[r(MUC, 0, 128, 0, 1)], writes=[r(MUC, 0, 128, 0, 1)])
                p.op("v", lambda e: e.tensor_scalar_mul(MUC[:, 1:2], MUC[:, 1:2], 1.0 / C),
                     reads=[r(MUC, 0, 128, 1, 2)], writes=[r(MUC, 0, 128, 1, 2)])
                p.op("v", lambda e: e.tensor_mul(MUC[:, 2:3], MUC[:, 0:1], MUC[:, 0:1]),
                     reads=[r(MUC, 0, 128, 0, 1)], writes=[r(MUC, 0, 128, 2, 3)])
                p.op("v", lambda e: e.tensor_sub(MUC[:, 1:2], MUC[:, 1:2], MUC[:, 2:3]),
                     reads=[r(MUC, 0, 128, 1, 3)], writes=[r(MUC, 0, 128, 1, 2)])
                p.op("s", lambda e: e.activation(out=MUC[:, 1:2], in_=MUC[:, 1:2], func=AF.Ln,
                                                 bias=KEPS[:, 0:1]),
                     reads=[r(MUC, 0, 128, 1, 2), rfull(KEPS)], writes=[r(MUC, 0, 128, 1, 2)])
                p.op("s", lambda e: e.activation(out=MUC[:, 1:2], in_=MUC[:, 1:2], func=AF.Exp,
                                                 scale=-0.5),
                     reads=[r(MUC, 0, 128, 1, 2)], writes=[r(MUC, 0, 128, 1, 2)])
                p.op("v", lambda e: e.tensor_scalar(
                    out=XTM[:, :], in0=XTM[:, :], scalar1=MUC[:, 0:1], scalar2=MUC[:, 1:2],
                    op0=ALU.subtract, op1=ALU.mult),
                    reads=[rfull(XTM), r(MUC, 0, 128, 0, 2)], writes=[rfull(XTM)])
                tok = p.op("y", lambda e, b=b, t0=t0: e.dma_start(
                    out=out_t.ap()[b * T + t0:b * T + t0 + 128, :], in_=XTM[:, :]),
                    reads=[rfull(XTM)], dma=True)
                p.out_dma_tokens.append(tok)
            p.phase_barrier()

        dump_batches = {nbatch - 1}
        dump_stages = set(dumps)
        init_consts()
        init_rotary()
        for b in range(nbatch):
            load_input_batch(b)
            if "rwkv" in stages:
                rwkv_batch(b)
            else:
                for cc in range(NCC):
                    p.op("v", lambda e, cc=cc: e.tensor_copy(
                        out=X1[:, cc * TC:(cc + 1) * TC], in_=Z[:, cc * TC:(cc + 1) * TC]),
                        reads=[r(Z, 0, 128, cc * TC, (cc + 1) * TC)],
                        writes=[r(X1, 0, 128, cc * TC, (cc + 1) * TC)])
            for l in range(2):
                if f"att{l}" in stages or "att" in stages:
                    ffn_att_batch(b, l, X1 if l == 0 else Z)
                if f"mlp{l}" in stages or "mlp" in stages:
                    ffn_mlp_batch(b, l)
            final_ln_store(b)
        for nm in dump_stages:
            if nm in ("KH", "VH", "WSLAB", "XKB", "SUMT", "SCN", "EKS", "SBB"):
                continue
            if nm in p.bufs:
                dump(p.bufs[nm])

        p.wait_tokens("y", p.out_dma_tokens + dump_toks)

        block = ctx.enter_context(nc.Block())
        p.finalize_into_block(block)

    return nc


def _prep_weights(inputs):
    f = np.ascontiguousarray
    w = {}
    w["wkT"] = f(inputs["L0_attn_Wk"].T)
    w["wvT"] = f(inputs["L0_attn_Wv"].T)
    w["wrT"] = f(inputs["L0_attn_Wr"].T)
    w["woT"] = f(inputs["L0_attn_Wo"].T)
    w["mixk"] = f(inputs["L0_attn_mix_k"].reshape(-1))
    w["mixv"] = f(inputs["L0_attn_mix_v"].reshape(-1))
    w["mixr"] = f(inputs["L0_attn_mix_r"].reshape(-1))
    w["tdec"] = f(inputs["L0_attn_time_decay"])
    w["tfirst"] = f(inputs["L0_attn_time_first"])
    pad = HIDP - HID
    for l in range(2):
        w[f"wqkv{l}"] = f(inputs["L0_ffn_wqkv"][l])
        w[f"qsc{l}"] = f(inputs["L0_ffn_q_scale"][l])
        w[f"ksc{l}"] = f(inputs["L0_ffn_k_scale"][l])
        w[f"woffn{l}"] = f(inputs["L0_ffn_wo"][l])
        w[f"lng{l}"] = f(inputs["L0_ffn_ln_g"][l])
        w[f"lnb{l}"] = f(inputs["L0_ffn_ln_b"][l])
        w1 = inputs["L0_ffn_w1"][l]
        b1 = inputs["L0_ffn_b1"][l]
        w[f"wa{l}"] = np.pad(w1[:, :HID], ((0, 0), (0, pad)))
        w[f"wg{l}"] = np.pad(w1[:, HID:], ((0, 0), (0, pad)))
        w[f"b1a{l}"] = np.pad(b1[:HID], (0, pad))
        w[f"b1g{l}"] = np.pad(b1[HID:], (0, pad))
        w[f"w2_{l}"] = np.pad(inputs["L0_ffn_w2"][l], ((0, pad), (0, 0)))
        w[f"b2_{l}"] = f(inputs["L0_ffn_b2"][l])
    return {k: np.ascontiguousarray(np.asarray(v, dtype=np.float32)) for k, v in w.items()}


def kernel(**inputs):
    x = np.asarray(inputs["L0_attn_x"], dtype=np.float32)
    w = _prep_weights({k: np.asarray(v, dtype=np.float32) for k, v in inputs.items()})
    nc = build_program()
    in_maps = []
    for c in range(NCORES):
        m = dict(w)
        m["x"] = np.ascontiguousarray(x[c * B_LOC:(c + 1) * B_LOC].reshape(B_LOC * T, C))
        in_maps.append(m)
    res = run_bass_kernel_spmd(nc, in_maps, core_ids=list(range(NCORES)))
    out = np.concatenate([res.results[c]["out"].reshape(B_LOC, T, C) for c in range(NCORES)],
                         axis=0)
    return out


# version 35
# speedup vs baseline: 1.0482x; 1.0482x over previous
import numpy as np
import ml_dtypes
from contextlib import ExitStack

import concourse.bass as bass
import concourse.mybir as mybir
from concourse.bass_utils import run_bass_kernel_spmd

F32 = mybir.dt.float32
BF16 = mybir.dt.bfloat16
AF = mybir.ActivationFunctionType
ALU = mybir.AluOpType
AX = mybir.AxisListType

NCORES = 8
B_LOC = 4
T = 2048
C = 1024
NCC = 8
TC = T + 1
H_ATT = 8
D_ATT = 64
INNER = 512
WINDOW = 512
NW = T // WINDOW
HID = 2730
HIDP = 2816
NHC = HIDP // 128
QK_SCALE = 8.0
LN_EPS = 1e-5

ENG = ("t", "v", "s", "g", "y")
ENG_FULL = {"t": "tensor", "v": "vector", "s": "scalar", "g": "gpsimd", "y": "sync"}
SELF_WAIT = {"t": False, "v": True, "s": True, "g": True, "y": False}


class _Op:
    __slots__ = ("fn", "waits", "inc", "need_inc", "cumval")

    def __init__(self, fn):
        self.fn = fn
        self.waits = []
        self.inc = None
        self.need_inc = False
        self.cumval = None


class Buf:
    __slots__ = ("h", "name", "shape", "dtype")

    def __init__(self, h, name, shape, dtype):
        self.h = h
        self.name = name
        self.shape = shape
        self.dtype = dtype

    def __getitem__(self, key):
        return self.h.ap()[key]

    def ap(self):
        return self.h.ap()


class Prog:
    def __init__(self, nc, ctx):
        self.nc = nc
        self.ctx = ctx
        self.ops = {e: [] for e in ENG}
        self.sem = {e: ctx.enter_context(nc.semaphore(f"c_{e}")) for e in ENG}
        self.dsem = {e: ctx.enter_context(nc.semaphore(f"d_{e}")) for e in ("y", "g", "s")}
        self.sem_n = {e: 0 for e in ENG}
        self.dsem_n = {e: 0 for e in ENG}
        self.waited = {}
        self.dma_barrier = {}
        self.regions = {}
        self.out_dma_tokens = []
        self.bufs = {}
        self._names = {}

    def uname(self, base):
        n = self._names.get(base, 0)
        self._names[base] = n + 1
        return base if n == 0 else f"{base}_{n}"

    def sbuf(self, name, shape, dtype):
        h = self.ctx.enter_context(self.nc.sbuf_tensor(self.uname(name), list(shape), dtype))
        b = Buf(h, name, tuple(shape), dtype)
        self.bufs[name] = b
        return b

    def psum(self, name, shape, dtype):
        h = self.ctx.enter_context(self.nc.psum_tensor(self.uname(name), list(shape), dtype))
        b = Buf(h, name, tuple(shape), dtype)
        self.bufs[name] = b
        return b

    def _ensure_inc(self, e, pos, kind):
        lst = self.ops[e]
        if kind == "d":
            op = lst[pos]
            assert op.inc is not None
            self.dma_barrier[e] = self.dsem_n[e]
            return (op.inc[0], 16 * self.dsem_n[e])
        lst[pos].need_inc = True
        return ("c", e, pos)

    def op_slow_dma(self, e, fn, writes=()):
        self.op(e, fn, reads=(), writes=writes, dma=True)

    def op(self, e, fn, reads=(), writes=(), dma=False):
        o = _Op(fn)
        lst = self.ops[e]
        pos = len(lst)
        need = {}
        for (buf, p0, p1, f0, f1) in reads:
            for a in self.regions.get(buf.name, ()):
                if a[0] == "w" and a[3] < p1 and p0 < a[4] and a[5] < f1 and f0 < a[6]:
                    need[(a[1], a[2], a[7])] = True
        for (buf, p0, p1, f0, f1) in writes:
            for a in self.regions.get(buf.name, ()):
                if a[3] < p1 and p0 < a[4] and a[5] < f1 and f0 < a[6]:
                    need[(a[1], a[2], a[7])] = True
        for (ae, apos, akind) in sorted(need):
            if ae == e and akind != "d" and not SELF_WAIT[e]:
                continue
            w = self._ensure_inc(ae, apos, akind)
            if w[0] == "c":
                key = (e, "c", ae)
                if self.waited.get(key, -1) < w[2]:
                    o.waits.append(w)
                    self.waited[key] = w[2]
            else:
                sem, val = w
                key = (e, sem.name)
                if self.waited.get(key, -1) < val:
                    o.waits.append(w)
                    self.waited[key] = val
        if dma:
            if lst and lst[-1].inc is not None and lst[-1].inc[0] is self.dsem[e]:
                lst.append(_Op(lambda eng: eng.nop()))
            self.dsem_n[e] += 1
            o.inc = (self.dsem[e], 16, 16 * self.dsem_n[e])
        kind = "d" if dma else "c"
        pos = len(lst)
        lst.append(o)
        for (buf, p0, p1, f0, f1) in writes:
            rl = self.regions.setdefault(buf.name, [])
            rl[:] = [a for a in rl if not (p0 <= a[3] and a[4] <= p1 and f0 <= a[5] and a[6] <= f1)]
            rl.append(("w", e, pos, p0, p1, f0, f1, kind))
        for (buf, p0, p1, f0, f1) in reads:
            rl = self.regions.setdefault(buf.name, [])
            rl[:] = [a for a in rl
                     if not (a[0] == "r" and a[1] == e and a[3] == p0 and a[4] == p1
                             and a[5] == f0 and a[6] == f1)]
            rl.append(("r", e, pos, p0, p1, f0, f1, kind))
        return (e, pos, kind)

    def phase_barrier(self):
        pairs = {}
        for oe in ENG:
            lst = self.ops[oe]
            if not lst:
                continue
            last = lst[-1]
            if last.inc is not None:
                pairs[oe] = (last.inc[0], 16 * self.dsem_n[oe])
            else:
                last.need_inc = True
                pairs[oe] = ("c", oe, len(lst) - 1)
        for e in ENG:
            o = _Op(lambda eng: eng.nop())
            for k, v in pairs.items():
                if k == e:
                    continue
                o.waits.append(v)
                if v[0] == "c":
                    key = (e, "c", v[1])
                    if self.waited.get(key, -1) < v[2]:
                        self.waited[key] = v[2]
                else:
                    key = (e, v[0].name)
                    if self.waited.get(key, -1) < v[1]:
                        self.waited[key] = v[1]
            self.ops[e].append(o)
        self.regions.clear()

    def wait_tokens(self, e, toks):
        o = _Op(lambda eng: eng.nop())
        o.waits = [self._ensure_inc(*t) for t in toks]
        self.ops[e].append(o)

    def finalize_into_block(self, block):
        for e in ENG:
            cum = 0
            for o in self.ops[e]:
                if o.need_inc and o.inc is None:
                    o.inc = (self.sem[e], 1, None)
                if o.inc is not None and o.inc[0] is self.sem[e]:
                    cum += 1
                    o.cumval = cum
        for e in ENG:
            for o in self.ops[e]:
                rw = []
                for w in o.waits:
                    if w[0] == "c":
                        _, ae, pos = w
                        rw.append((self.sem[ae], self.ops[ae][pos].cumval))
                    else:
                        rw.append(w)
                o.waits = rw
        for e in ENG:
            section = getattr(block, ENG_FULL[e])
            ops = self.ops[e]

            def make(ops):
                def body(eng):
                    for o in ops:
                        for sem, val in o.waits:
                            eng.wait_ge(sem, val)
                        inst = o.fn(eng)
                        if o.inc is not None:
                            inst.then_inc(o.inc[0], o.inc[1])

                return body
            section(make(ops))


def r(buf, p0, p1, f0, f1):
    return (buf, p0, p1, f0, f1)


def rfull(buf):
    return (buf, 0, buf.shape[0], 0, buf.shape[1])





def build_program(nbatch=B_LOC, stages=("rwkv", "att", "mlp"), dumps=(), rwkv_sub="full"):
    nc = bass.Bass("TRN2", target_bir_lowering=False)
    dt_in = {}

    def din(name, shape):
        t = nc.dram_tensor(name, list(shape), F32, kind="ExternalInput")
        dt_in[name] = t
        return t

    x_in = din("x", (B_LOC * T, C))
    out_t = nc.dram_tensor("out", [B_LOC * T, C], F32, kind="ExternalOutput")

    din("wkT", (C, C)); din("wvT", (C, C)); din("wrT", (C, C)); din("woT", (C, C))
    din("mixk", (C,)); din("mixv", (C,)); din("mixr", (C,))
    din("tdec", (16,)); din("tfirst", (16,))
    for l in range(2):
        din(f"wqkv{l}", (C, 3 * INNER))
        din(f"qsc{l}", (D_ATT,)); din(f"ksc{l}", (D_ATT,))
        din(f"woffn{l}", (INNER, C))
        din(f"lng{l}", (C,)); din(f"lnb{l}", (C,))
        din(f"wa{l}", (C, HIDP)); din(f"wg{l}", (C, HIDP))
        din(f"b1a{l}", (HIDP,)); din(f"b1g{l}", (HIDP,))
        din(f"w2_{l}", (HIDP, C)); din(f"b2_{l}", (C,))

    ctx = ExitStack()
    p = Prog(nc, ctx)
    dump_toks = []

    def dump(buf):
        dt = nc.dram_tensor(f"dump_{buf.name}", list(buf.shape), buf.dtype,
                            kind="ExternalOutput")
        tok = p.op("y", lambda e, buf=buf, dt=dt: e.dma_start(
            out=dt.ap()[:, :], in_=buf[:, :]), reads=[rfull(buf)], dma=True)
        dump_toks.append(tok)

    with ctx:
        X1 = p.sbuf("X1", (128, NCC * TC), F32)
        Z = p.sbuf("Z", (128, NCC * TC), F32)
        IDNF = p.sbuf("IDNF", (128, 128), F32)
        IDNB = p.sbuf("IDNB", (128, 128), BF16)
        ONE1B = p.sbuf("ONE1B", (128, 1), BF16)
        ONE1F = p.sbuf("ONE1F", (1, 128), F32)
        ONE1BR = p.sbuf("ONE1BR", (1, 128), BF16)
        ONES2 = p.sbuf("ONES2", (128, 2), BF16)
        SEL2 = p.sbuf("SEL2", (2, 128), BF16)
        SEL2A = p.sbuf("SEL2A", (1, 128), BF16)
        SEL2B = p.sbuf("SEL2B", (1, 128), BF16)
        MCAT = p.sbuf("MCAT", (128, 256), BF16)
        MIXC = p.sbuf("MIXC", (128, 24), F32)
        EWC = p.sbuf("EWC", (128, 8), F32)
        EUC = p.sbuf("EUC", (128, 8), F32)
        EW2 = p.sbuf("EW2", (2, 8), F32)
        EU2 = p.sbuf("EU2", (2, 8), F32)
        SA = p.sbuf("SA", (128, 8), F32)
        SB2 = p.sbuf("SB2", (2, 8), F32)
        STATB = p.sbuf("STATB", (128, 512), BF16)
        SUMT = p.sbuf("SUMT", (128, 512), F32)
        MUC = p.sbuf("MUC", (128, 4), F32)
        KEPS = p.sbuf("KEPS", (128, 1), F32)
        PQC = p.sbuf("PQC", (128, 128), F32)
        PQS = p.sbuf("PQS", (128, 128), F32)
        PKC = p.sbuf("PKC", (128, 128), F32)
        PKS = p.sbuf("PKS", (128, 128), F32)
        RSTC = p.sbuf("RSTC", (128, 256), F32)

        PB = [p.psum(f"PB{i}", (128, 512), F32) for i in range(4)]
        PTRN = p.psum("PTRN", (128, 512), F32)
        PTRB = p.psum("PTRB", (128, 512), BF16)
        PROW = p.psum("PROW", (128, 512), F32)
        PAV = p.psum("PAV", (65, 512), F32)

        def tcol(cc, t):
            return cc * TC + 1 + t

        def aff(buf, sl, cmp, fill, base, pattern, cm):
            p.op("g", lambda eng: eng.affine_select(out=sl, in_=sl, compare_op=cmp,
                                                    fill=fill, base=base, pattern=pattern,
                                                    channel_multiplier=cm),
                 reads=[rfull(buf)], writes=[rfull(buf)])

        def init_consts():
            for buf in (IDNF, IDNB):
                p.op("g", lambda e, b=buf: e.memset(b[:, :], 0.0), writes=[rfull(buf)])
                aff(buf, buf[:, :], ALU.not_equal, 1.0, 0, [[-1, 128]], 1)
            p.op("g", lambda e: e.memset(ONE1B[:, :], 1.0), writes=[rfull(ONE1B)])
            p.op("g", lambda e: e.memset(ONE1F[:, :], 1.0), writes=[rfull(ONE1F)])
            p.op("g", lambda e: e.memset(ONE1BR[:, :], 1.0), writes=[rfull(ONE1BR)])
            p.op("g", lambda e: e.memset(ONES2[:, :], 0.0), writes=[rfull(ONES2)])
            aff(ONES2, ONES2[:, 0:1], ALU.is_ge, 1.0, -64, [[0, 1]], 1)
            aff(ONES2, ONES2[:, 1:2], ALU.is_gt, 1.0, 64, [[0, 1]], -1)
            p.op("g", lambda e: e.memset(SEL2[:, :], 0.0), writes=[rfull(SEL2)])
            p.op("g", lambda e: e.memset(SEL2A[:, :], 0.0), writes=[rfull(SEL2A)])
            aff(SEL2A, SEL2A[:, :], ALU.is_ge, 1.0, -64, [[1, 128]], 0)
            p.op("g", lambda e: e.memset(SEL2B[:, :], 0.0), writes=[rfull(SEL2B)])
            aff(SEL2B, SEL2B[:, :], ALU.is_gt, 1.0, 64, [[-1, 128]], 0)
            aff(SEL2, SEL2[:, :], ALU.is_ge, 1.0, -64, [[1, 128]], 999)
            aff(SEL2, SEL2[:, :], ALU.is_gt, 1.0, 1063, [[-1, 128]], -999)
            p.op("g", lambda e: e.memset(MCAT[:, :], 0.0), writes=[rfull(MCAT)])
            aff(MCAT, MCAT[:, 0:128], ALU.is_gt, 1.0, 0, [[-1, 128]], 1)
            aff(MCAT, MCAT[:, 128:256], ALU.is_gt, 1.0, 0, [[1, 128]], -1)
            for i, nm in enumerate(("mixk", "mixv", "mixr")):
                def fn(e, nm=nm, i=i):
                    src = bass.AP(dt_in[nm], 0, [[1, 128], [128, 8]])
                    return e.dma_start(out=MIXC[:, 8 * i:8 * (i + 1)], in_=src,
                                       allow_slow_non_contiguous=True)
                p.op_slow_dma("g", fn, writes=[r(MIXC, 0, 128, 8 * i, 8 * (i + 1))])

            def rep(dst, nm, rows):
                if rows == 128:
                    for hh in range(2):
                        def fn(e, dst=dst, nm=nm, hh=hh):
                            ap = bass.AP(dt_in[nm], hh, [[0, 64], [2, 8]])
                            return e.dma_start(out=dst[64 * hh:64 * (hh + 1), :], in_=ap,
                                               allow_slow_non_contiguous=True)
                        p.op_slow_dma("g", fn, writes=[r(dst, 64 * hh, 64 * (hh + 1), 0, 8)])
                else:
                    def fn(e, dst=dst, nm=nm):
                        ap = bass.AP(dt_in[nm], 0, [[1, 2], [2, 8]])
                        return e.dma_start(out=dst[:, :], in_=ap,
                                           allow_slow_non_contiguous=True)
                    p.op_slow_dma("g", fn, writes=[rfull(dst)])
            rep(EWC, "tdec", 128); rep(EUC, "tfirst", 128)
            rep(EW2, "tdec", 2); rep(EU2, "tfirst", 2)
            for buf in (EWC, EW2):
                p.op("s", lambda e, b=buf: e.activation(out=b[:, :], in_=b[:, :], func=AF.Exp),
                     reads=[rfull(buf)], writes=[rfull(buf)])
                p.op("s", lambda e, b=buf: e.activation(out=b[:, :], in_=b[:, :], func=AF.Exp,
                                                        scale=-1.0),
                     reads=[rfull(buf)], writes=[rfull(buf)])
            for buf in (EUC, EU2):
                p.op("s", lambda e, b=buf: e.activation(out=b[:, :], in_=b[:, :], func=AF.Exp),
                     reads=[rfull(buf)], writes=[rfull(buf)])
            p.op("v", lambda e: e.memset(KEPS[:, :], LN_EPS), writes=[rfull(KEPS)])
            for buf in (X1, Z):
                def zc(e, b=buf):
                    ap = bass.AP(b.h, 0, [[NCC * TC, 128], [TC, 8], [1, 1]])
                    return e.memset(ap, 0.0)
                p.op("v", zc, writes=[r(buf, 0, 128, 0, buf.shape[1])])

        def init_rotary():
          with ExitStack() as ph:
            TWO_PI = 2.0 * np.pi
            mk = lambda name, shape, dt: Buf(
                ph.enter_context(nc.sbuf_tensor(p.uname(name), list(shape), dt)),
                name, tuple(shape), dt)
            IVFC = mk("IVFC", (128, 32), F32)
            LNBC = mk("LNBC", (128, 32), F32)
            BKIC = mk("BKIC", (128, 32), mybir.dt.int32)
            for d2 in range(32):
                ivf = 1.0 / 10000.0 ** (2.0 * d2 / 64.0)
                lnb = float(np.log((2.0 * d2 + 0.4 * 64.0) / (1.4 * 64.0)))
                bsq = float(np.exp(2.0 * lnb))
                c5, s5 = float(np.cos(512.0 * ivf)), float(np.sin(512.0 * ivf))
                cb, sb = float(np.cos(1536.0 * ivf)), float(-np.sin(1536.0 * ivf))
                b3 = float(np.exp(-6.0 * lnb))
                for k4, v in enumerate((c5 * bsq, s5 * bsq, c5 / bsq, s5 / bsq,
                                        cb * b3, sb * b3, cb / b3, sb / b3)):
                    col = 32 * k4 + d2
                    p.op("g", lambda e, c=float(np.float32(v)), col=col: e.memset(
                        RSTC[:, col:col + 1], c),
                        writes=[r(RSTC, 0, 128, col, col + 1)])
                p.op("g", lambda e, c=float(np.float32(ivf)), d2=d2: e.memset(
                    IVFC[:, d2:d2 + 1], c), writes=[r(IVFC, 0, 128, d2, d2 + 1)])
                p.op("g", lambda e, c=float(np.float32(lnb)), d2=d2: e.memset(
                    LNBC[:, d2:d2 + 1], c), writes=[r(LNBC, 0, 128, d2, d2 + 1)])
            S = SUMT
            for tt in range(4):
                c0 = 32 * tt
                p.op("g", lambda e: e.iota(S[:, 288:320], pattern=[[0, 32]], base=0,
                                           channel_multiplier=1,
                                           allow_small_or_imprecise_dtypes=True),
                     writes=[r(S, 0, 128, 288, 320)])
                p.op("v", lambda e, tt=tt: e.scalar_tensor_tensor(
                    out=S[:, 0:32], in0=S[:, 288:320], scalar=float(128.0 * tt),
                    in1=IVFC[:, :], op0=ALU.add, op1=ALU.mult),
                    reads=[r(S, 0, 128, 288, 320), rfull(IVFC)],
                    writes=[r(S, 0, 128, 0, 32)])
                for shift, dstc in ((0.0, 192), (float(np.pi / 2), 160)):
                    p.op("v", lambda e, shift=shift: e.tensor_scalar(
                        out=S[:, 32:64], in0=S[:, 0:32], scalar1=float(1.0 / TWO_PI),
                        scalar2=float(shift / TWO_PI), op0=ALU.mult, op1=ALU.add),
                        reads=[r(S, 0, 128, 0, 32)], writes=[r(S, 0, 128, 32, 64)])
                    p.op("v", lambda e: e.tensor_copy(out=BKIC[:, :], in_=S[:, 32:64]),
                         reads=[r(S, 0, 128, 32, 64)], writes=[rfull(BKIC)])
                    p.op("v", lambda e: e.tensor_copy(out=S[:, 64:96], in_=BKIC[:, :]),
                         reads=[rfull(BKIC)], writes=[r(S, 0, 128, 64, 96)])
                    p.op("v", lambda e: e.tensor_sub(S[:, 96:128], S[:, 32:64], S[:, 64:96]),
                         reads=[r(S, 0, 128, 32, 96)], writes=[r(S, 0, 128, 96, 128)])
                    p.op("v", lambda e: e.tensor_scalar(
                        out=S[:, 96:128], in0=S[:, 96:128], scalar1=0.5, scalar2=None,
                        op0=ALU.is_ge),
                        reads=[r(S, 0, 128, 96, 128)], writes=[r(S, 0, 128, 96, 128)])
                    p.op("v", lambda e: e.tensor_add(S[:, 64:96], S[:, 64:96], S[:, 96:128]),
                         reads=[r(S, 0, 128, 64, 128)], writes=[r(S, 0, 128, 64, 96)])
                    p.op("v", lambda e, shift=shift: e.tensor_scalar(
                        out=S[:, 128:160], in0=S[:, 64:96], scalar1=float(-TWO_PI),
                        scalar2=float(shift), op0=ALU.mult, op1=ALU.add),
                        reads=[r(S, 0, 128, 64, 96)], writes=[r(S, 0, 128, 128, 160)])
                    p.op("v", lambda e: e.tensor_add(S[:, 128:160], S[:, 128:160], S[:, 0:32]),
                         reads=[r(S, 0, 128, 0, 32), r(S, 0, 128, 128, 160)],
                         writes=[r(S, 0, 128, 128, 160)])
                    p.op("s", lambda e, dstc=dstc: e.activation(
                        out=S[:, dstc:dstc + 32], in_=S[:, 128:160], func=AF.Sin),
                        reads=[r(S, 0, 128, 128, 160)],
                        writes=[r(S, 0, 128, dstc, dstc + 32)])
                p.op("v", lambda e, tt=tt: e.tensor_scalar(
                    out=S[:, 320:321], in0=S[:, 288:289], scalar1=float(1.0 / 256.0),
                    scalar2=float((128.0 * tt - 1024.0) / 256.0), op0=ALU.mult,
                    op1=ALU.add),
                    reads=[r(S, 0, 128, 288, 289)], writes=[r(S, 0, 128, 320, 321)])
                p.op("v", lambda e: e.tensor_scalar(
                    out=S[:, 32:64], in0=LNBC[:, :], scalar1=S[:, 320:321], scalar2=None,
                    op0=ALU.mult),
                    reads=[rfull(LNBC), r(S, 0, 128, 320, 321)],
                    writes=[r(S, 0, 128, 32, 64)])
                p.op("s", lambda e: e.activation(out=S[:, 224:256], in_=S[:, 32:64],
                                                 func=AF.Exp),
                     reads=[r(S, 0, 128, 32, 64)], writes=[r(S, 0, 128, 224, 256)])
                p.op("s", lambda e: e.activation(out=S[:, 256:288], in_=S[:, 32:64],
                                                 func=AF.Exp, scale=-1.0),
                     reads=[r(S, 0, 128, 32, 64)], writes=[r(S, 0, 128, 256, 288)])
                for dst, trig, sc in ((PQC, 160, 224), (PQS, 192, 224),
                                      (PKC, 160, 256), (PKS, 192, 256)):
                    p.op("v", lambda e, dst=dst, trig=trig, sc=sc, c0=c0: e.tensor_mul(
                        dst[:, c0:c0 + 32], S[:, trig:trig + 32], S[:, sc:sc + 32]),
                        reads=[r(S, 0, 128, trig, trig + 32), r(S, 0, 128, sc, sc + 32)],
                        writes=[r(dst, 0, 128, c0, c0 + 32)])
            p.phase_barrier()

        def load_input_batch(b):
          with ExitStack() as ph:
            XTMr = [Buf(ph.enter_context(nc.sbuf_tensor(p.uname("XTM"), [128, 1024], F32)),
                        "XTM", (128, 1024), F32) for _ in range(2)]
            for tt in range(T // 128):
                t0 = tt * 128
                XTM = XTMr[tt % 2]
                PTX = (PTRN, PB[0])[tt % 2]
                p.op("y", lambda e, b=b, t0=t0, XTM=XTM: e.dma_start(
                    out=XTM[:, :], in_=x_in.ap()[b * T + t0: b * T + t0 + 128, :]),
                    writes=[rfull(XTM)], dma=True)
                for half in range(2):
                    for k in range(4):
                        cc = half * 4 + k
                        p.op("t", lambda e, cc=cc, k=k, XTM=XTM, PTX=PTX: e.transpose(
                            PTX[:, 128 * k:128 * (k + 1)],
                            XTM[:, 128 * cc:128 * (cc + 1)], IDNF[:, :]),
                            reads=[r(XTM, 0, 128, 128 * cc, 128 * (cc + 1)), rfull(IDNF)],
                            writes=[r(PTX, 0, 128, 128 * k, 128 * (k + 1))])
                    cc0 = half * 4

                    def ev(e, cc0=cc0, t0=t0, PTX=PTX):
                        dst = bass.AP(Z.h, tcol(cc0, t0), [[NCC * TC, 128], [TC, 4], [1, 128]])
                        return e.tensor_copy(out=dst, in_=PTX[:, :])
                    p.op("v", ev, reads=[rfull(PTX)],
                         writes=[r(Z, 0, 128, tcol(cc0, t0), tcol(cc0 + 3, t0) + 128)])
            p.phase_barrier()

        def ln_rows_mid():
            (MB, mr, mc), (RB, rr, rc) = p.bufs["LNROWV"]
            MU = MB[mr:mr + 1, mc:mc + 512]
            RS = RB[rr:rr + 1, rc:rc + 512]
            mreg = r(MB, mr, mr + 1, mc, mc + 512)
            rreg = r(RB, rr, rr + 1, rc, rc + 512)
            p.op("v", lambda e: e.tensor_scalar_mul(MU, PROW[0:1, :], 1.0 / C),
                 reads=[r(PROW, 0, 1, 0, 512)], writes=[mreg])
            p.op("v", lambda e: e.tensor_mul(PROW[0:1, :], MU, MU),
                 reads=[mreg, r(PROW, 0, 1, 0, 512)],
                 writes=[r(PROW, 0, 1, 0, 512)])
            p.op("v", lambda e: e.tensor_scalar_mul(RS, PAV[0:1, :], 1.0 / C),
                 reads=[r(PAV, 0, 1, 0, 512)], writes=[rreg])
            p.op("v", lambda e: e.tensor_sub(RS, RS, PROW[0:1, :]),
                 reads=[rreg, r(PROW, 0, 1, 0, 512)], writes=[rreg])
            p.op("s", lambda e: e.activation(out=RS, in_=RS,
                                             func=AF.Ln, bias=KEPS[0:1, 0:1]),
                 reads=[rreg, rfull(KEPS)], writes=[rreg])
            p.op("s", lambda e: e.activation(out=RS, in_=RS, func=AF.Exp, scale=-0.5),
                 reads=[rreg], writes=[rreg])
            p.op("t", lambda e: e.matmul(PROW[:, :], ONE1F[:, :], RS,
                                         start=True, stop=True),
                 reads=[rfull(ONE1F), rreg], writes=[rfull(PROW)])
            p.op("v", lambda e: e.scalar_tensor_tensor(
                out=MU, in0=MU, scalar=-1.0, in1=RS, op0=ALU.mult, op1=ALU.mult),
                reads=[mreg, rreg], writes=[mreg])
            p.op("t", lambda e: e.matmul(PTRN[:, :], ONE1F[:, :], MU,
                                         start=True, stop=True),
                 reads=[rfull(ONE1F), mreg], writes=[rfull(PTRN)])

        def ln_channel_major(src, dst, t0):
            for cc in range(NCC):
                sl = r(src, 0, 128, tcol(cc, t0), tcol(cc, t0) + 512)
                p.op("s", lambda e, cc=cc, t0=t0, src=src: e.copy(
                    out=STATB[:, 0:512], in_=src[:, tcol(cc, t0):tcol(cc, t0) + 512]),
                    reads=[sl], writes=[r(STATB, 0, 128, 0, 512)])
                p.op("t", lambda e, cc=cc: e.matmul(
                    PROW[0:1, :], ONE1B[:, :], STATB[:, 0:512],
                    start=(cc == 0), stop=(cc == NCC - 1)),
                    reads=[rfull(ONE1B), r(STATB, 0, 128, 0, 512)],
                    writes=[r(PROW, 0, 1, 0, 512)])
                p.op("s", lambda e, cc=cc, t0=t0, src=src: e.activation(
                    out=STATB[:, 0:512], in_=src[:, tcol(cc, t0):tcol(cc, t0) + 512],
                    func=AF.Square),
                    reads=[sl], writes=[r(STATB, 0, 128, 0, 512)])
                p.op("t", lambda e, cc=cc: e.matmul(
                    PAV[0:1, :], ONE1B[:, :], STATB[:, 0:512],
                    start=(cc == 0), stop=(cc == NCC - 1)),
                    reads=[rfull(ONE1B), r(STATB, 0, 128, 0, 512)],
                    writes=[r(PAV, 0, 1, 0, 512)])
            ln_rows_mid()
            for cc in range(NCC):
                sl = r(src, 0, 128, tcol(cc, t0), tcol(cc, t0) + 512)
                dl = r(dst, 0, 128, tcol(cc, t0), tcol(cc, t0) + 512)
                p.op("v", lambda e, cc=cc, t0=t0, src=src, dst=dst: e.tensor_mul(
                    dst[:, tcol(cc, t0):tcol(cc, t0) + 512],
                    src[:, tcol(cc, t0):tcol(cc, t0) + 512], PROW[:, :]),
                    reads=[sl, rfull(PROW)], writes=[dl])
                p.op("v", lambda e, cc=cc, t0=t0, dst=dst: e.tensor_add(
                    dst[:, tcol(cc, t0):tcol(cc, t0) + 512],
                    dst[:, tcol(cc, t0):tcol(cc, t0) + 512], PTRN[:, :]),
                    reads=[dl, rfull(PTRN)], writes=[dl])

        def rwkv_batch(b):
            with ExitStack() as ph:
                sb = lambda name, shape, dt: Buf(
                    ph.enter_context(nc.sbuf_tensor(p.uname(name), list(shape), dt)),
                    name, tuple(shape), dt)
                WSLAB = sb("WSLAB", (128, NCC * 1024), BF16)
                LNROW = sb("LNROW", (1, 1024), F32)
                p.bufs["LNROWV"] = ((LNROW, 0, 0), (LNROW, 0, 512))
                KH = sb("KH", (128, NCC * 1024), BF16)
                VH = sb("VH", (128, NCC * 1024), BF16)
                XKB = sb("XKB", (128, NCC * 512), BF16)
                DDr = [sb("DDr", (128, 512), F32)] * 2
                EKVQ = SUMT
                SCN = sb("SCN", (128, 513), F32)
                EKS = sb("EKS", (2, 512), F32)
                SBB = sb("SBB", (2, 513), F32)
                RSIGr = [sb("RSIGr", (128, 512), F32)] * 2
                for nm, bb in (("WSLAB", WSLAB), ("KH", KH), ("VH", VH), ("XKB", XKB)):
                    p.bufs[nm] = bb

                def load_wslab(nm):
                    def fn(e, nm=nm):
                        src = bass.AP(dt_in[nm], 0, [[1024, 128], [131072, 8], [1, 1024]])
                        return e.dma_start(out=WSLAB[:, :].rearrange("p (cc co) -> p cc co", cc=8),
                                           in_=src)
                    return p.op("g", fn, writes=[rfull(WSLAB)], dma=True)

                def mix_tiles(t0, mixoff):
                    for cc in range(NCC):
                        DD = DDr[cc % 2]
                        xc = r(Z, 0, 128, tcol(cc, t0) - 1, tcol(cc, t0) + 512)
                        p.op("v", lambda e, DD=DD, cc=cc, t0=t0: e.tensor_sub(
                            DD[:, :], Z[:, tcol(cc, t0):tcol(cc, t0) + 512],
                            Z[:, tcol(cc, t0) - 1:tcol(cc, t0) + 511]),
                            reads=[xc], writes=[rfull(DD)])
                        p.op("v", lambda e, DD=DD, cc=cc, t0=t0, mo=mixoff: e.scalar_tensor_tensor(
                            out=XKB[:, 512 * cc:512 * (cc + 1)], in0=DD[:, :],
                            scalar=MIXC[:, mo + cc:mo + cc + 1],
                            in1=Z[:, tcol(cc, t0) - 1:tcol(cc, t0) + 511],
                            op0=ALU.mult, op1=ALU.add),
                            reads=[rfull(DD), xc, r(MIXC, 0, 128, mixoff + cc, mixoff + cc + 1)],
                            writes=[r(XKB, 0, 128, 512 * cc, 512 * (cc + 1))])

                def proj_block(dstbuf, hl0, act=None):
                    for co in range(NCC):
                        ps = PB[co % 4]
                        for cc in range(NCC):
                            p.op("t", lambda e, ps=ps, co=co, cc=cc: e.matmul(
                                ps[:, :],
                                WSLAB[:, 1024 * cc + 128 * co:1024 * cc + 128 * (co + 1)],
                                XKB[:, 512 * cc:512 * (cc + 1)],
                                start=(cc == 0), stop=(cc == NCC - 1)),
                                reads=[r(WSLAB, 0, 128, 1024 * cc + 128 * co,
                                         1024 * cc + 128 * (co + 1)),
                                       r(XKB, 0, 128, 512 * cc, 512 * (cc + 1))],
                                writes=[rfull(ps)])
                        p.op("s", lambda e, ps=ps, co=co, hl0=hl0, dstbuf=dstbuf: e.copy(
                            out=dstbuf[:, 1024 * co + hl0:1024 * co + hl0 + 512], in_=ps[:, :]),
                            reads=[rfull(ps)],
                            writes=[r(dstbuf, 0, 128, 1024 * co + hl0, 1024 * co + hl0 + 512)])

                def wkv_quarter(cc, hl0):
                    par = (cc + hl0 // 512) % 2
                    EKB0 = 1536 * par
                    RCP0 = 512 + 1536 * par
                    ksl = r(KH, 0, 128, 1024 * cc + hl0, 1024 * cc + hl0 + 512)
                    vsl = r(VH, 0, 128, 1024 * cc + hl0, 1024 * cc + hl0 + 512)
                    p.op("s", lambda e: e.activation(
                        out=XKB[:, EKB0:EKB0 + 512], in_=KH[:, 1024 * cc + hl0:1024 * cc + hl0 + 512],
                        func=AF.Exp),
                        reads=[ksl], writes=[r(XKB, 0, 128, EKB0, EKB0 + 512)])
                    p.op("v", lambda e: e.tensor_mul(
                        EKVQ[:, :], XKB[:, EKB0:EKB0 + 512], VH[:, 1024 * cc + hl0:1024 * cc + hl0 + 512]),
                        reads=[r(XKB, 0, 128, EKB0, EKB0 + 512), vsl], writes=[rfull(EKVQ)])
                    p.op("t", lambda e: e.matmul(PROW[0:2, :], ONES2[:, :], XKB[:, EKB0:EKB0 + 512],
                                                 start=True, stop=True),
                         reads=[rfull(ONES2), r(XKB, 0, 128, EKB0, EKB0 + 512)], writes=[rfull(PROW)])
                    p.op("v", lambda e: e.tensor_copy(out=EKS[:, :], in_=PROW[0:2, :]),
                         reads=[rfull(PROW)], writes=[rfull(EKS)])
                    p.op("v", lambda e: e.tensor_copy(out=SCN[:, 0:1], in_=SA[:, cc:cc + 1]),
                         reads=[r(SA, 0, 128, cc, cc + 1)], writes=[r(SCN, 0, 128, 0, 1)])
                    p.op("v", lambda e: e.tensor_tensor_scan(
                        out=SCN[:, 1:513], data0=EWC[:, cc:cc + 1].broadcast_to([128, 512]),
                        data1=EKVQ[:, :], initial=SA[:, cc:cc + 1], op0=ALU.mult, op1=ALU.add),
                        reads=[rfull(EKVQ), r(EWC, 0, 128, cc, cc + 1), r(SA, 0, 128, cc, cc + 1)],
                        writes=[r(SCN, 0, 128, 1, 513)])
                    p.op("v", lambda e: e.tensor_copy(out=SA[:, cc:cc + 1], in_=SCN[:, 512:513]),
                         reads=[r(SCN, 0, 128, 512, 513)], writes=[r(SA, 0, 128, cc, cc + 1)])
                    p.op("v", lambda e: e.scalar_tensor_tensor(
                        out=SCN[:, 0:512], in0=EKVQ[:, :], scalar=EUC[:, cc:cc + 1],
                        in1=SCN[:, 0:512], op0=ALU.mult, op1=ALU.add),
                        reads=[rfull(EKVQ), r(EUC, 0, 128, cc, cc + 1), r(SCN, 0, 128, 0, 512)],
                        writes=[r(SCN, 0, 128, 0, 512)])
                    p.op("v", lambda e: e.tensor_copy(out=SBB[:, 0:1], in_=SB2[:, cc:cc + 1]),
                         reads=[r(SB2, 0, 2, cc, cc + 1)], writes=[r(SBB, 0, 2, 0, 1)])
                    p.op("v", lambda e: e.tensor_tensor_scan(
                        out=SBB[:, 1:513], data0=EW2[:, cc:cc + 1].broadcast_to([2, 512]),
                        data1=EKS[:, :], initial=SB2[:, cc:cc + 1], op0=ALU.mult, op1=ALU.add),
                        reads=[rfull(EKS), r(EW2, 0, 2, cc, cc + 1), r(SB2, 0, 2, cc, cc + 1)],
                        writes=[r(SBB, 0, 2, 1, 513)])
                    p.op("v", lambda e: e.tensor_copy(out=SB2[:, cc:cc + 1], in_=SBB[:, 512:513]),
                         reads=[r(SBB, 0, 2, 512, 513)], writes=[r(SB2, 0, 2, cc, cc + 1)])
                    p.op("v", lambda e: e.tensor_scalar(
                        out=EKS[:, :], in0=EKS[:, :], scalar1=EU2[:, cc:cc + 1], scalar2=1e-6,
                        op0=ALU.mult, op1=ALU.max),
                        reads=[rfull(EKS), r(EU2, 0, 2, cc, cc + 1)], writes=[rfull(EKS)])
                    p.op("v", lambda e: e.tensor_add(EKS[:, :], EKS[:, :], SBB[:, 0:512]),
                         reads=[rfull(EKS), r(SBB, 0, 2, 0, 512)], writes=[rfull(EKS)])
                    p.op("s", lambda e: e.activation(out=EKS[:, :], in_=EKS[:, :], func=AF.Ln),
                         reads=[rfull(EKS)], writes=[rfull(EKS)])
                    p.op("s", lambda e: e.activation(out=XKB[0:2, RCP0:RCP0 + 512],
                                                     in_=EKS[:, :], func=AF.Exp, scale=-1.0),
                         reads=[rfull(EKS)], writes=[r(XKB, 0, 2, RCP0, RCP0 + 512)])
                    p.op("t", lambda e: e.matmul(PROW[:, :], SEL2[:, :], XKB[0:2, RCP0:RCP0 + 512],
                                                 start=True, stop=True),
                         reads=[rfull(SEL2), r(XKB, 0, 2, RCP0, RCP0 + 512)], writes=[rfull(PROW)])
                    p.op("v", lambda e: e.tensor_mul(
                        KH[:, 1024 * cc + hl0:1024 * cc + hl0 + 512], SCN[:, 0:512], PROW[:, :]),
                        reads=[r(SCN, 0, 128, 0, 512), rfull(PROW)],
                        writes=[ksl])

                def rwkv_r_block(hl0):
                    for co in range(NCC):
                        ps = PB[co % 4]
                        for cc in range(NCC):
                            p.op("t", lambda e, ps=ps, co=co, cc=cc: e.matmul(
                                ps[:, :],
                                WSLAB[:, 1024 * cc + 128 * co:1024 * cc + 128 * (co + 1)],
                                XKB[:, 512 * cc:512 * (cc + 1)],
                                start=(cc == 0), stop=(cc == NCC - 1)),
                                reads=[r(WSLAB, 0, 128, 1024 * cc + 128 * co,
                                         1024 * cc + 128 * (co + 1)),
                                       r(XKB, 0, 128, 512 * cc, 512 * (cc + 1))],
                                writes=[rfull(ps)])
                        RSIG = RSIGr[co % 2]
                        p.op("s", lambda e, ps=ps, RSIG=RSIG: e.activation(
                            out=RSIG[:, :], in_=ps[:, :], func=AF.Sigmoid),
                            reads=[rfull(ps)], writes=[rfull(RSIG)])
                        p.op("v", lambda e, co=co, hl0=hl0, RSIG=RSIG: e.tensor_mul(
                            VH[:, 1024 * co + hl0:1024 * co + hl0 + 512], RSIG[:, :],
                            KH[:, 1024 * co + hl0:1024 * co + hl0 + 512]),
                            reads=[rfull(RSIG),
                                   r(KH, 0, 128, 1024 * co + hl0, 1024 * co + hl0 + 512)],
                            writes=[r(VH, 0, 128, 1024 * co + hl0, 1024 * co + hl0 + 512)])

                def rwkv_out_block(t0, hl0):
                    for oc in range(NCC):
                        ps = PB[oc % 4]
                        for ic in range(NCC):
                            p.op("t", lambda e, ps=ps, oc=oc, ic=ic, hl0=hl0: e.matmul(
                                ps[:, :],
                                WSLAB[:, 1024 * ic + 128 * oc:1024 * ic + 128 * (oc + 1)],
                                VH[:, 1024 * ic + hl0:1024 * ic + hl0 + 512],
                                start=(ic == 0), stop=(ic == NCC - 1)),
                                reads=[r(WSLAB, 0, 128, 1024 * ic + 128 * oc,
                                         1024 * ic + 128 * (oc + 1)),
                                       r(VH, 0, 128, 1024 * ic + hl0, 1024 * ic + hl0 + 512)],
                                writes=[rfull(ps)])
                        p.op("v", lambda e, ps=ps, oc=oc, t0=t0: e.tensor_add(
                            X1[:, tcol(oc, t0):tcol(oc, t0) + 512],
                            Z[:, tcol(oc, t0):tcol(oc, t0) + 512], ps[:, :]),
                            reads=[r(Z, 0, 128, tcol(oc, t0), tcol(oc, t0) + 512), rfull(ps)],
                            writes=[r(X1, 0, 128, tcol(oc, t0), tcol(oc, t0) + 512)])
                    ln_channel_major(X1, X1, t0)

                p.op("v", lambda e: e.memset(SA[:, :], 0.0), writes=[rfull(SA)])
                p.op("v", lambda e: e.memset(SB2[:, :], 0.0), writes=[rfull(SB2)])
                for half in range(2):
                    th0 = half * 1024
                    for nm, dstbuf, mixoff in (("wkT", KH, 0), ("wvT", VH, 8)):
                        load_wslab(nm)
                        for tb in range(2):
                            mix_tiles(th0 + tb * 512, mixoff)
                            proj_block(dstbuf, tb * 512)
                    if rwkv_sub == "kv":
                        continue
                    for cc in range(NCC):
                        for q in range(2):
                            wkv_quarter(cc, q * 512)
                    if rwkv_sub == "wkv":
                        if half == 1 and b in dump_batches:
                            p.op("v", lambda e: e.tensor_copy(out=SUMT[:, :], in_=PROW[:, :]),
                                 reads=[rfull(PROW)], writes=[rfull(SUMT)])
                            for nm, bb in (("SCN", SCN), ("EKS", EKS), ("SBB", SBB),
                                           ("SUMT", SUMT), ("XKB", XKB)):
                                if nm in dump_stages:
                                    dump(bb)
                        continue
                    load_wslab("wrT")
                    for tb in range(2):
                        mix_tiles(th0 + tb * 512, 16)
                        rwkv_r_block(tb * 512)
                    if rwkv_sub == "r":
                        continue
                    load_wslab("woT")
                    for tb in range(2):
                        rwkv_out_block(th0 + tb * 512, tb * 512)
                if b in dump_batches:
                    for nm in ("KH", "VH"):
                        if nm in dump_stages:
                            dump(p.bufs[nm])
                p.phase_barrier()


        def ffn_att_batch(b, l, R):
            with ExitStack() as ph:
                sb = lambda name, shape, dt: Buf(
                    ph.enter_context(nc.sbuf_tensor(p.uname(name), list(shape), dt)),
                    name, tuple(shape), dt)
                WQKVB = sb("WQKVB", (128, NCC * 512), BF16)
                WOFr = [sb("WOF", (128, 2 * 512), BF16) for _ in range(2)]
                SHr = [sb("SH", (128, NCC * 129), BF16) for _ in range(2)]
                QT32r = [sb("QT32", (128, 512), F32) for _ in range(2)]
                QKB2r = [sb("QKB2", (128, 512), BF16) for _ in range(2)]
                T3Br = [sb("T3B", (128, 512), BF16)] * 2
                QHATr = [sb("QHAT", (128, 512), F32)] * 2
                QKTE = sb("QTE", (64, 8 * 512), BF16)
                KTE0 = sb("KTE0", (64, 8 * 512), BF16)
                KTE1 = sb("KTE1", (64, 8 * 512), BF16)
                QTEv = (QKTE, 0)
                KTEv = [(KTE0, 0), (KTE1, 0)]
                VE = [sb("VE0", (128, 8 * 4 * 65), BF16), sb("VE1", (128, 8 * 4 * 65), BF16)]
                ATTTr = [sb("ATTT", (128, 512), BF16) for _ in range(4)]
                OT = sb("OT", (128, 4 * 512), BF16)
                RC2B = sb("RC2B", (1, 512), BF16)
                RCBS = sb("RCBS", (128, 512), BF16)
                RC2A = RCBS
                RSQ = sb("RSQ", (128, 20), F32)
                SVCOL = sb("SVCOL", (128, 8), F32)
                CSQr = [sb("CSQ", (128, 256), BF16)] * 2
                SSQr = [sb("SSQ", (128, 256), BF16)] * 2
                CSKr = [sb("CSK", (128, 256), BF16)] * 2
                SSKr = [sb("SSK", (128, 256), BF16)] * 2
                TR1 = QT32r[0]
                TR2 = QT32r[1]
                QSB = sb("QSB", (128, 256), BF16)

                def ldw(nm, dst, nicc, ncols):
                    def fn(e, nm=nm, dst=dst, nicc=nicc, ncols=ncols):
                        src = bass.AP(dt_in[nm], 0,
                                      [[ncols, 128], [128 * ncols, nicc], [1, ncols]])
                        dap = bass.AP(dst.h, 0,
                                      [[dst.shape[1], 128], [ncols, nicc], [1, ncols]])
                        return e.dma_start(out=dap, in_=src)
                    return p.op("g", fn, writes=[rfull(dst)], dma=True)


                def load_wof(qh):
                    WOF = WOFr[qh % 2]

                    def fn(e, qh=qh, WOF=WOF):
                        src = bass.AP(dt_in[f"woffn{l}"], qh * 256,
                                      [[1024, 128], [131072, 4], [1, 256]])
                        dap = bass.AP(WOF.h, 0, [[1024, 128], [256, 4], [1, 256]])
                        return e.dma_start(out=dap, in_=src)
                    return p.op("g", fn, writes=[rfull(WOF)], dma=True)

                def load_qkv_slab(cob):
                    def fn(e, cob=cob):
                        src = bass.AP(dt_in[f"wqkv{l}"], cob * 512,
                                      [[1536, 128], [128 * 1536, 8], [1, 512]])
                        dap = bass.AP(WQKVB.h, 0, [[NCC * 512, 128], [512, NCC], [1, 512]])
                        return e.dma_start(out=dap, in_=src)
                    return p.op("g", fn, writes=[rfull(WQKVB)], dma=True)
                for i, nm in enumerate((f"qsc{l}", f"ksc{l}")):
                    p.op("y", lambda e, nm=nm: e.dma_start(
                        out=TR2[0:1, 0:64],
                        in_=bass.AP(dt_in[nm], 0, [[64, 1], [1, 64]])),
                        writes=[r(TR2, 0, 1, 0, 64)], dma=True)
                    p.op("t", lambda e: e.matmul(PROW[:, 0:64], ONE1F[:, :], TR2[0:1, 0:64],
                                                 start=True, stop=True),
                         reads=[rfull(ONE1F), r(TR2, 0, 1, 0, 64)],
                         writes=[r(PROW, 0, 128, 0, 64)])
                    p.op("v", lambda e, i=i: e.tensor_scalar_mul(
                        QSB[:, 128 * i:128 * i + 64], PROW[:, 0:64],
                        8.0 if i == 0 else 1.0),
                        reads=[r(PROW, 0, 128, 0, 64)],
                        writes=[r(QSB, 0, 128, 128 * i, 128 * i + 64)])
                    p.op("v", lambda e, i=i: e.tensor_scalar_mul(
                        QSB[:, 128 * i + 64:128 * i + 96], QSB[:, 128 * i + 32:128 * i + 64], -1.0),
                        reads=[r(QSB, 0, 128, 128 * i, 128 * i + 64)],
                        writes=[r(QSB, 0, 128, 128 * i + 64, 128 * i + 96)])
                    p.op("v", lambda e, i=i: e.tensor_copy(
                        out=QSB[:, 128 * i + 96:128 * i + 128], in_=QSB[:, 128 * i:128 * i + 32]),
                        reads=[r(QSB, 0, 128, 128 * i, 128 * i + 32)],
                        writes=[r(QSB, 0, 128, 128 * i + 96, 128 * i + 128)])
                for vb in VE:
                    def vones(e, vb=vb):
                        ap = bass.AP(vb.h, 64, [[2080, 128], [65, 32], [1, 1]])
                        return e.memset(ap, 1.0)
                    p.op("v", vones, writes=[r(vb, 0, 128, 0, vb.shape[1])])

                def emit_tables(w):
                    for tt in range(4):
                        c0t = tt * 64
                        for dst, mast, qs_off in (
                                (CSQr[0], PQC, 0), (SSQr[0], PQS, 64),
                                (CSKr[0], PKC, 128), (SSKr[0], PKS, 192)):
                            def f1(e, dst=dst, mast=mast, qs_off=qs_off, c0t=c0t, tt=tt):
                                i0 = bass.AP(mast.h, 32 * tt, [[128, 128], [0, 2], [1, 32]])
                                i1 = QSB[:, qs_off:qs_off + 64].rearrange(
                                    "p (a f) -> p a f", a=2)
                                o0 = dst[:, c0t:c0t + 64].rearrange("p (a f) -> p a f", a=2)
                                return e.tensor_tensor(out=o0, in0=i0, in1=i1, op=ALU.mult)
                            p.op("g", f1,
                                 reads=[rfull(mast), r(QSB, 0, 128, qs_off, qs_off + 64)],
                                 writes=[r(dst, 0, 128, c0t, c0t + 64)])

                def advance(back):
                    co = 128 if back else 0
                    r3 = lambda buf: bass.AP(buf.h, 0,
                                             [[buf.shape[1], 128], [32, 4], [1, 32]])
                    r128 = lambda buf: r(buf, 0, 128, 0, 128)
                    for pc, ps, cc0, sc0 in ((PQC, PQS, co + 0, co + 32),
                                             (PKC, PKS, co + 64, co + 96)):
                        CA = bass.AP(RSTC.h, cc0, [[256, 128], [0, 4], [1, 32]])
                        SA_ = bass.AP(RSTC.h, sc0, [[256, 128], [0, 4], [1, 32]])
                        steps = (
                            (TR1, pc, CA), (TR2, pc, SA_), (pc, ps, SA_))
                        for dstb, in0b, capb in steps:
                            p.op("g", lambda e, dstb=dstb, in0b=in0b, capb=capb:
                                 e.tensor_tensor(
                                out=r3(dstb), in0=r3(in0b), in1=capb, op=ALU.mult),
                                reads=[r128(in0b), rfull(RSTC)], writes=[r128(dstb)])
                        p.op("g", lambda e, pc=pc: e.tensor_sub(
                            pc[:, 0:128], TR1[:, 0:128], pc[:, 0:128]),
                            reads=[r128(TR1), r128(pc)], writes=[r128(pc)])
                        p.op("g", lambda e, ps=ps, CA=CA: e.tensor_tensor(
                            out=r3(TR1), in0=r3(ps), in1=CA, op=ALU.mult),
                            reads=[r128(ps), rfull(RSTC)], writes=[r128(TR1)])
                        p.op("g", lambda e, ps=ps: e.tensor_add(
                            ps[:, 0:128], TR1[:, 0:128], TR2[:, 0:128]),
                            reads=[r128(TR1), r128(TR2)], writes=[r128(ps)])

                def sh_build(w, tt):
                    SH = SHr[tt % 2]
                    t0 = w * 512 + tt * 128
                    for cc in range(NCC):
                        p.op("s", lambda e, SH=SH, cc=cc, t0=t0: e.copy(
                            out=SH[:, 129 * cc:129 * (cc + 1)],
                            in_=R[:, tcol(cc, t0) - 1:tcol(cc, t0) + 128]),
                            reads=[r(R, 0, 128, tcol(cc, t0) - 1, tcol(cc, t0) + 128)],
                            writes=[r(SH, 0, 128, 129 * cc, 129 * (cc + 1))])
                    if l == 1 and w > 0 and tt == 0:
                        def fix(e, SH=SH):
                            dst = bass.AP(SH.h, 0, [[NCC * 129, 128], [129, 8], [1, 1]])
                            return e.tensor_copy(out=dst, in_=SVCOL[:, :])
                        p.op("v", fix, reads=[rfull(SVCOL)],
                             writes=[r(SH, 0, 128, 0, SH.shape[1])])

                def lhs_sh(cc, tt):
                    SH = SHr[tt % 2]
                    off = 129 * cc
                    if cc < 4:
                        return SH[:, off + 1:off + 129], r(SH, 0, 128, off + 1, off + 129)
                    return SH[:, off:off + 128], r(SH, 0, 128, off, off + 128)

                def qkv_cob(w, cob, tt):
                    ps = PB[tt % 4]
                    for cc in range(NCC):
                        ap, rd = lhs_sh(cc, tt)
                        p.op("t", lambda e, ps=ps, ap=ap, cc=cc: e.matmul(
                            ps[:, :], ap, WQKVB[:, 512 * cc:512 * (cc + 1)],
                            start=(cc == 0), stop=(cc == NCC - 1)),
                            reads=[rd, r(WQKVB, 0, 128, 512 * cc, 512 * (cc + 1))],
                            writes=[rfull(ps)])
                    return ps

                def v_evac(w, tt, ps):
                    wb = w % 2

                    def vev(e, wb=wb, tt=tt, ps=ps):
                        dst = bass.AP(VE[wb].h, tt * 65, [[2080, 128], [260, 8], [1, 64]])
                        return e.copy(out=dst, in_=ps[:, :])
                    p.op("s", vev, reads=[rfull(ps)],
                         writes=[r(VE[wb], 0, 128, 0, VE[wb].shape[1])])

                def qk_process(w, iq, tt, ps):
                    wb = w % 2
                    QT32 = QT32r[tt % 2]
                    QKB2 = QKB2r[tt % 2]
                    T3B = T3Br[tt % 2]
                    QHAT = QHATr[tt % 2]
                    wr2 = w % 2
                    CST, SST = (CSQr[wr2], SSQr[wr2]) if iq == 0 else (CSKr[wr2], SSKr[wr2])
                    p.op("s", lambda e, QHAT=QHAT, ps=ps: e.activation(out=QHAT[:, :], in_=ps[:, :],
                                                     func=AF.Square),
                         reads=[rfull(ps)], writes=[rfull(QHAT)])
                    p.op("v", lambda e, QHAT=QHAT: e.tensor_reduce(
                        RSQ[:, 0:8], QHAT.ap().rearrange("p (h d) -> p h d", h=8),
                        axis=AX.X, op=ALU.add),
                        reads=[rfull(QHAT)], writes=[r(RSQ, 0, 128, 0, 8)])
                    p.op("v", lambda e: e.tensor_scalar_max(RSQ[:, 8:16], RSQ[:, 0:8], 1e-24),
                         reads=[r(RSQ, 0, 128, 0, 8)], writes=[r(RSQ, 0, 128, 8, 16)])
                    p.op("s", lambda e: e.activation(out=RSQ[:, 8:16], in_=RSQ[:, 8:16],
                                                     func=AF.Ln),
                         reads=[r(RSQ, 0, 128, 8, 16)], writes=[r(RSQ, 0, 128, 8, 16)])
                    p.op("s", lambda e: e.activation(out=RSQ[:, 8:16], in_=RSQ[:, 8:16],
                                                     func=AF.Exp, scale=-0.5),
                         reads=[r(RSQ, 0, 128, 8, 16)], writes=[r(RSQ, 0, 128, 8, 16)])

                    def qhat(e, QHAT=QHAT, ps=ps):
                        i1 = bass.AP(RSQ.h, 8, [[20, 128], [1, 8], [0, 64]])
                        return e.tensor_tensor(out=QHAT[:, :], in0=ps[:, :], in1=i1,
                                               op=ALU.mult)
                    p.op("v", qhat, reads=[rfull(ps), r(RSQ, 0, 128, 8, 16)],
                         writes=[rfull(QHAT)])

                    def t2f(e, CST=CST, tt=tt, QT32=QT32, QHAT=QHAT):
                        i1 = bass.AP(CST.h, tt * 64, [[256, 128], [0, 8], [1, 64]])
                        return e.tensor_tensor(out=QT32[:, :], in0=QHAT[:, :], in1=i1,
                                               op=ALU.mult)
                    p.op("v", t2f, reads=[rfull(QHAT), rfull(CST)], writes=[rfull(QT32)])

                    def t3a(e, SST=SST, tt=tt, QHAT=QHAT, T3B=T3B):
                        i0 = bass.AP(QHAT.h, 32, [[512, 128], [64, 8], [1, 32]])
                        i1 = bass.AP(SST.h, tt * 64, [[256, 128], [0, 8], [1, 32]])
                        o0 = bass.AP(T3B.h, 0, [[512, 128], [64, 8], [1, 32]])
                        return e.tensor_tensor(out=o0, in0=i0, in1=i1, op=ALU.mult)
                    p.op("g", t3a, reads=[rfull(QHAT), rfull(SST)],
                         writes=[r(T3B, 0, 128, 0, 512)])

                    def t3b(e, SST=SST, tt=tt, QHAT=QHAT, T3B=T3B):
                        i0 = bass.AP(QHAT.h, 0, [[512, 128], [64, 8], [1, 32]])
                        i1 = bass.AP(SST.h, tt * 64 + 32, [[256, 128], [0, 8], [1, 32]])
                        o0 = bass.AP(T3B.h, 32, [[512, 128], [64, 8], [1, 32]])
                        return e.tensor_tensor(out=o0, in0=i0, in1=i1, op=ALU.mult)
                    p.op("g", t3b, reads=[rfull(QHAT), rfull(SST)],
                         writes=[r(T3B, 0, 128, 0, 512)])
                    p.op("g", lambda e, QKB2=QKB2, QT32=QT32, T3B=T3B: e.tensor_add(QKB2[:, :], QT32[:, :], T3B[:, 0:512]),
                         reads=[rfull(QT32), r(T3B, 0, 128, 0, 512)], writes=[rfull(QKB2)])
                    for hp in range(4):
                        p.op("t", lambda e, hp=hp, QKB2=QKB2: e.transpose(
                            PTRB[:, 128 * hp:128 * (hp + 1)], QKB2[:, 128 * hp:128 * (hp + 1)],
                            IDNB[:, :]),
                            reads=[r(QKB2, 0, 128, 128 * hp, 128 * (hp + 1)), rfull(IDNB)],
                            writes=[r(PTRB, 0, 128, 128 * hp, 128 * (hp + 1))])
                    dbuf, dp0 = QTEv if iq == 0 else KTEv[wb]
                    for par in range(2):
                        def tev(e, dbuf=dbuf, dp0=dp0, par=par, tt=tt):
                            dst = bass.AP(dbuf.h, dp0 * 4096 + 512 * par + tt * 128,
                                          [[8 * 512, 64], [1024, 4], [1, 128]])
                            src = bass.AP(PTRB.h, 64 * par * 512,
                                          [[512, 64], [128, 4], [1, 128]])
                            return e.tensor_copy(out=dst, in_=src)
                        p.op("v", tev, reads=[rfull(PTRB)],
                             writes=[r(dbuf, dp0, dp0 + 64, 0, 4096)])

                def head_attention(w, h, pav):
                    wb, pb = w % 2, (w - 1) % 2
                    jts = ([("p", jl) for jl in range(4)] if w > 0 else []) +                           [("c", jl) for jl in range(4)]
                    njts = len(jts)
                    units = []
                    for jidx, (kind, jl) in enumerate(jts):
                        ATTT = ATTTr[jidx % 4]
                        PP2 = (PTRN, PB[1])[jidx % 2]
                        if kind == "p":
                            kbuf, kp0 = KTEv[pb]
                            i0, i1 = 0, 128 * (jl + 1)
                            bnd0, bmask = 128 * jl, 128
                            vb = VE[pb]
                        else:
                            kbuf, kp0 = KTEv[wb]
                            i0, i1 = 128 * jl, 512
                            bnd0, bmask = 0, 0
                            vb = VE[wb]
                        units.append((jidx, jl, ATTT, PP2, kbuf, kp0, i0, i1,
                                      bnd0, bmask, vb))

                    def emit_sim(u):
                        jidx, jl, ATTT, PP2, kbuf, kp0, i0, i1, bnd0, bmask, vb = u
                        span = i1 - i0
                        p.op("t", lambda e, kbuf=kbuf, kp0=kp0, jl=jl, h=h, i0=i0, i1=i1,
                             PP2=PP2: e.matmul(
                            PP2[:, i0:i1],
                            kbuf[kp0:kp0 + 64, 512 * h + 128 * jl:512 * h + 128 * (jl + 1)],
                            QKTE[0:64, 512 * h + i0:512 * h + i1], start=True, stop=True),
                            reads=[r(kbuf, kp0, kp0 + 64, 512 * h + 128 * jl, 512 * h + 128 * (jl + 1)),
                                   r(QKTE, 0, 64, 512 * h + i0, 512 * h + i1)],
                            writes=[rfull(PP2)])
                        p.op("s", lambda e, i0=i0, span=span, PP2=PP2, ATTT=ATTT: e.activation(
                            out=ATTT[:, 0:span], in_=PP2[:, i0:i0 + span], func=AF.Exp),
                            reads=[rfull(PP2)], writes=[r(ATTT, 0, 128, 0, span)])
                        p.op("v", lambda e, bnd0=bnd0, bmask=bmask, ATTT=ATTT: e.tensor_mul(
                            ATTT[:, bnd0:bnd0 + 128], ATTT[:, bnd0:bnd0 + 128],
                            MCAT[:, bmask:bmask + 128]),
                            reads=[r(ATTT, 0, 128, bnd0, bnd0 + 128),
                                   r(MCAT, 0, 128, bmask, bmask + 128)],
                            writes=[r(ATTT, 0, 128, bnd0, bnd0 + 128)])

                    def emit_av(u):
                        jidx, jl, ATTT, PP2, kbuf, kp0, i0, i1, bnd0, bmask, vb = u
                        span = i1 - i0
                        p.op("t", lambda e, vb=vb, h=h, jl=jl, i0=i0, span=span,
                             first=(jidx == 0), last=(jidx == njts - 1), pav=pav,
                             ATTT=ATTT: e.matmul(
                            pav[0:65, i0:i0 + span], vb[:, 260 * h + 65 * jl:260 * h + 65 * (jl + 1)],
                            ATTT[:, 0:span], start=first, stop=last),
                            reads=[r(vb, 0, 128, 260 * h + 65 * jl, 260 * h + 65 * (jl + 1)),
                                   r(ATTT, 0, 128, 0, span)],
                            writes=[rfull(pav)])

                    for k in range(njts + 2):
                        if k < njts:
                            emit_sim(units[k])
                        if k >= 2:
                            emit_av(units[k - 2])

                def pair_out(w, hp, pava, pavb):
                    for pav, rcdst in ((pava, RC2A), (pavb, RC2B)):
                        p.op("s", lambda e, pav=pav, rcdst=rcdst: e.activation(
                            out=rcdst[0:1, :], in_=pav[64:65, :], func=AF.Ln),
                            reads=[rfull(pav)], writes=[rfull(rcdst)])
                        p.op("s", lambda e, rcdst=rcdst: e.activation(
                            out=rcdst[0:1, :], in_=rcdst[0:1, :], func=AF.Exp, scale=-1.0),
                            reads=[rfull(rcdst)], writes=[rfull(rcdst)])
                    p.op("t", lambda e: e.matmul(PROW[:, :], SEL2A[:, :], RC2A[0:1, :],
                                                 start=True, stop=False),
                         reads=[rfull(SEL2A), rfull(RC2A)], writes=[rfull(PROW)])
                    p.op("t", lambda e: e.matmul(PROW[:, :], SEL2B[:, :], RC2B[0:1, :],
                                                 start=False, stop=True),
                         reads=[rfull(SEL2B), rfull(RC2B)], writes=[rfull(PROW)])
                    p.op("v", lambda e: e.tensor_copy(out=RCBS[:, :], in_=PROW[:, :]),
                         reads=[rfull(PROW)], writes=[rfull(RCBS)])
                    for par, pav in ((0, pava), (1, pavb)):
                        p.op("v", lambda e, par=par, pav=pav, hp=hp: e.tensor_mul(
                            OT[64 * par:64 * (par + 1), 512 * hp:512 * (hp + 1)],
                            pav[0:64, :], RCBS[64 * par:64 * (par + 1), :]),
                            reads=[rfull(pav), r(RCBS, 64 * par, 64 * (par + 1), 0, 512)],
                            writes=[r(OT, 64 * par, 64 * (par + 1), 512 * hp, 512 * (hp + 1))])

                def wo_out(w):
                    t0 = w * 512
                    if l == 1:
                        def sv(e, t0=t0):
                            src = bass.AP(Z.h, tcol(0, t0 + 511), [[NCC * TC, 128], [TC, 8], [1, 1]])
                            return e.tensor_copy(out=SVCOL[:, :], in_=src)
                        p.op("v", sv, reads=[r(Z, 0, 128, 0, Z.shape[1])], writes=[rfull(SVCOL)])
                    for qh in range(4):
                        WOF = WOFr[qh % 2]
                        for j2 in range(2):
                            oc = 2 * qh + j2
                            ps = PB[oc % 4]
                            for icc in range(4):
                                p.op("t", lambda e, ps=ps, WOF=WOF, j2=j2, icc=icc: e.matmul(
                                    ps[:, :],
                                    WOF[:, 256 * icc + 128 * j2:256 * icc + 128 * (j2 + 1)],
                                    OT[:, 512 * icc:512 * (icc + 1)],
                                    start=(icc == 0), stop=(icc == 3)),
                                    reads=[r(WOF, 0, 128, 256 * icc + 128 * j2,
                                             256 * icc + 128 * (j2 + 1)),
                                           r(OT, 0, 128, 512 * icc, 512 * (icc + 1))],
                                    writes=[rfull(ps)])
                            p.op("v", lambda e, ps=ps, oc=oc, t0=t0: e.tensor_add(
                                Z[:, tcol(oc, t0):tcol(oc, t0) + 512],
                                R[:, tcol(oc, t0):tcol(oc, t0) + 512], ps[:, :]),
                                reads=[r(R, 0, 128, tcol(oc, t0), tcol(oc, t0) + 512), rfull(ps)],
                                writes=[r(Z, 0, 128, tcol(oc, t0), tcol(oc, t0) + 512)])
                        if qh + 2 < 4:
                            load_wof(qh + 2)

                for w in range(NW):
                    emit_tables(w)
                    advance(back=(w == NW - 1))
                    for cob in range(3):
                        if not (cob == 0 and w > 0):
                            load_qkv_slab(cob)
                        for tt in range(4):
                            sh_build(w, tt)
                            ps = qkv_cob(w, cob, tt)
                            if cob == 2:
                                v_evac(w, tt, ps)
                            else:
                                qk_process(w, cob, tt, ps)
                    load_wof(0)
                    load_wof(1)
                    for hp in range(4):
                        pava, pavb = ((PAV, PB[0]), (PB[2], PB[3]))[hp % 2]
                        for par in range(2):
                            h = 2 * hp + par
                            head_attention(w, h, pava if par == 0 else pavb)
                        pair_out(w, hp, pava, pavb)
                    if w + 1 < NW:
                        load_qkv_slab(0)
                    wo_out(w)
                if b in dump_batches:
                    dd = {"QTE": QKTE, "KTE0": KTE0, "VE0": VE[0], "OT": OT,
                          "CSQ": CSQr[0], "SSQ": SSQr[0], "CSK": CSKr[0], "SSK": SSKr[0]}
                    for nm in dd:
                        if nm in dump_stages:
                            dump(dd[nm])
                p.phase_barrier()


        HGROUPS = [5, 5, 4, 4, 4]

        def ffn_mlp_batch(b, l):
            with ExitStack() as ph:
                sb = lambda name, shape, dt: Buf(
                    ph.enter_context(nc.sbuf_tensor(p.uname(name), list(shape), dt)),
                    name, tuple(shape), dt)
                LNZ = sb("LNZ", (128, NCC * 2048), BF16)
                MGELr = [sb("MGEL", (128, 512), F32)] * 2
                p.bufs["LNROWV"] = ((SUMT, 0, 0), (MGELr[0], 0, 0))
                WA = sb("WA", (128, 5 * 1024), BF16)
                WG = sb("WG", (128, 5 * 1024), BF16)
                W2P = sb("W2P", (128, 5 * 1024), BF16)
                HTG = sb("HTG", (128, 5 * 512), BF16)
                LNGC = sb("LNGC", (128, 16), F32)
                B1AC = sb("B1AC", (128, NHC), F32)
                B1GC = sb("B1GC", (128, NHC), F32)
                B2C = sb("B2C", (128, 8), F32)

                def cl(dstbuf, c0, nm, ncols):
                    def fn(e, dstbuf=dstbuf, c0=c0, nm=nm, ncols=ncols):
                        src = bass.AP(dt_in[nm], 0, [[1, 128], [128, ncols]])
                        return e.dma_start(out=dstbuf[:, c0:c0 + ncols], in_=src,
                                           allow_slow_non_contiguous=True)
                    p.op_slow_dma("g", fn, writes=[r(dstbuf, 0, 128, c0, c0 + ncols)])
                cl(LNGC, 0, f"lng{l}", 8)
                cl(LNGC, 8, f"lnb{l}", 8)
                cl(B1AC, 0, f"b1a{l}", NHC)
                cl(B1GC, 0, f"b1g{l}", NHC)
                cl(B2C, 0, f"b2_{l}", 8)

                def zsh(cc, t0, n=512):
                    base = tcol(cc, t0) - (1 if cc >= 4 else 0)
                    return Z[:, base:base + n], r(Z, 0, 128, base, base + n)

                for tb in range(4):
                    t0 = tb * 512
                    for cc in range(NCC):
                        ap, rd = zsh(cc, t0)
                        cb, cb0 = (STATB, 0) if cc % 2 == 0 else (HTG, 512)
                        qb, qb0 = (HTG, 0) if cc % 2 == 0 else (HTG, 1024)
                        p.op("s", lambda e, ap=ap, cb=cb, cb0=cb0: e.copy(
                            out=cb[:, cb0:cb0 + 512], in_=ap),
                            reads=[rd], writes=[r(cb, 0, 128, cb0, cb0 + 512)])
                        p.op("t", lambda e, cc=cc, cb=cb, cb0=cb0: e.matmul(
                            PROW[0:1, :], ONE1B[:, :], cb[:, cb0:cb0 + 512],
                            start=(cc == 0), stop=(cc == NCC - 1)),
                            reads=[rfull(ONE1B), r(cb, 0, 128, cb0, cb0 + 512)],
                            writes=[r(PROW, 0, 1, 0, 512)])
                        p.op("s", lambda e, ap=ap, qb=qb, qb0=qb0: e.activation(
                            out=qb[:, qb0:qb0 + 512], in_=ap, func=AF.Square),
                            reads=[rd], writes=[r(qb, 0, 128, qb0, qb0 + 512)])
                        p.op("t", lambda e, cc=cc, qb=qb, qb0=qb0: e.matmul(
                            PAV[0:1, :], ONE1B[:, :], qb[:, qb0:qb0 + 512],
                            start=(cc == 0), stop=(cc == NCC - 1)),
                            reads=[rfull(ONE1B), r(qb, 0, 128, qb0, qb0 + 512)],
                            writes=[r(PAV, 0, 1, 0, 512)])
                    ln_rows_mid()
                    for cc in range(NCC):
                        ap, rd = zsh(cc, t0)
                        TB1 = SUMT
                        p.op("v", lambda e, ap=ap, TB1=TB1: e.tensor_mul(TB1[:, :], ap, PROW[:, :]),
                             reads=[rd, rfull(PROW)], writes=[rfull(TB1)])
                        def b2f(e, cc=cc):
                            i1 = LNGC[:, 8 + cc:9 + cc].broadcast_to([128, 512])
                            return e.scalar_tensor_tensor(
                                out=STATB[:, 0:512], in0=PTRN[:, :],
                                scalar=LNGC[:, cc:cc + 1], in1=i1,
                                op0=ALU.mult, op1=ALU.add)
                        p.op("v", b2f, reads=[rfull(PTRN), r(LNGC, 0, 128, 0, 16)],
                             writes=[r(STATB, 0, 128, 0, 512)])
                        p.op("v", lambda e, cc=cc, t0=t0, TB1=TB1: e.scalar_tensor_tensor(
                            out=LNZ[:, 2048 * cc + t0:2048 * cc + t0 + 512], in0=TB1[:, :],
                            scalar=LNGC[:, cc:cc + 1], in1=STATB[:, 0:512],
                            op0=ALU.mult, op1=ALU.add),
                            reads=[rfull(TB1), r(LNGC, 0, 128, 0, 8),
                                   r(STATB, 0, 128, 0, 512)],
                            writes=[r(LNZ, 0, 128, 2048 * cc + t0, 2048 * cc + t0 + 512)])

                hid0 = 0
                for gi, gsz in enumerate(HGROUPS):
                    for wsl, nm in ((WA, f"wa{l}"), (WG, f"wg{l}")):
                        for cc in range(NCC):
                            def ldp(e, wsl=wsl, nm=nm, hid0=hid0, gsz=gsz, cc=cc):
                                src = bass.AP(dt_in[nm], hid0 * 128 + cc * 128 * HIDP,
                                              [[HIDP, 128], [128, gsz], [1, 128]])
                                dst = bass.AP(wsl.h, cc * 128,
                                              [[wsl.shape[1], 128], [1024, gsz], [1, 128]])
                                return e.dma_start(out=dst, in_=src)
                            p.op("g", ldp, writes=[rfull(wsl)], dma=True)
                    def ldw2(e, hid0=hid0, gsz=gsz):
                        src = bass.AP(dt_in[f"w2_{l}"], hid0 * 128 * 1024,
                                      [[1024, 128], [128 * 1024, gsz], [1, 1024]])
                        dst = bass.AP(W2P.h, 0,
                                      [[W2P.shape[1], 128], [1024, gsz], [1, 1024]])
                        return e.dma_start(out=dst, in_=src)
                    p.op("g", ldw2, writes=[rfull(W2P)], dma=True)
                    for tb in range(4):
                        t0 = tb * 512
                        for j in range(gsz):
                            hidc = hid0 + j
                            MGEL = MGELr[j % 2]
                            psa = (PB[0], PB[2])[j % 2]
                            psg = (PB[1], PB[3])[j % 2]
                            for ps, wsl in ((psa, WA), (psg, WG)):
                                for cc in range(NCC):
                                    p.op("t", lambda e, ps=ps, wsl=wsl, j=j, cc=cc, t0=t0: e.matmul(
                                        ps[:, :], wsl[:, 1024 * j + 128 * cc:1024 * j + 128 * (cc + 1)],
                                        LNZ[:, 2048 * cc + t0:2048 * cc + t0 + 512],
                                        start=(cc == 0), stop=(cc == NCC - 1)),
                                        reads=[r(wsl, 0, 128, 1024 * j + 128 * cc,
                                                 1024 * j + 128 * (cc + 1)),
                                               r(LNZ, 0, 128, 2048 * cc + t0, 2048 * cc + t0 + 512)],
                                        writes=[rfull(ps)])
                            p.op("s", lambda e, hidc=hidc, MGEL=MGEL, psg=psg: e.activation(
                                out=MGEL[:, :], in_=psg[:, :], func=AF.Gelu,
                                bias=B1GC[:, hidc:hidc + 1]),
                                reads=[rfull(psg), r(B1GC, 0, 128, hidc, hidc + 1)],
                                writes=[rfull(MGEL)])
                            p.op("v", lambda e, hidc=hidc, j=j, MGEL=MGEL, psa=psa:
                                 e.scalar_tensor_tensor(
                                out=HTG[:, 512 * j:512 * (j + 1)], in0=psa[:, :],
                                scalar=B1AC[:, hidc:hidc + 1], in1=MGEL[:, :],
                                op0=ALU.add, op1=ALU.mult),
                                reads=[rfull(psa), r(B1AC, 0, 128, hidc, hidc + 1),
                                       rfull(MGEL)],
                                writes=[r(HTG, 0, 128, 512 * j, 512 * (j + 1))])
                        for och in range(2):
                            for oci in range(4):
                                oc = och * 4 + oci
                                ps = (PB[0], PB[1], PTRN, PB[2])[oci]
                                for j in range(gsz):
                                    p.op("t", lambda e, ps=ps, j=j, oc=oc: e.matmul(
                                        ps[:, :], W2P[:, 1024 * j + 128 * oc:1024 * j + 128 * (oc + 1)],
                                        HTG[:, 512 * j:512 * (j + 1)],
                                        start=(j == 0), stop=(j == gsz - 1)),
                                        reads=[r(W2P, 0, 128, 1024 * j + 128 * oc,
                                                 1024 * j + 128 * (oc + 1)),
                                               r(HTG, 0, 128, 512 * j, 512 * (j + 1))],
                                        writes=[rfull(ps)])
                                if gi == 0:
                                    p.op("v", lambda e, ps=ps, oc=oc, t0=t0: e.scalar_tensor_tensor(
                                        out=Z[:, tcol(oc, t0):tcol(oc, t0) + 512], in0=ps[:, :],
                                        scalar=B2C[:, oc:oc + 1],
                                        in1=Z[:, tcol(oc, t0):tcol(oc, t0) + 512],
                                        op0=ALU.add, op1=ALU.add),
                                        reads=[rfull(ps), r(B2C, 0, 128, oc, oc + 1),
                                               r(Z, 0, 128, tcol(oc, t0), tcol(oc, t0) + 512)],
                                        writes=[r(Z, 0, 128, tcol(oc, t0), tcol(oc, t0) + 512)])
                                else:
                                    p.op("v", lambda e, ps=ps, oc=oc, t0=t0: e.tensor_add(
                                        Z[:, tcol(oc, t0):tcol(oc, t0) + 512],
                                        Z[:, tcol(oc, t0):tcol(oc, t0) + 512], ps[:, :]),
                                        reads=[rfull(ps), r(Z, 0, 128, tcol(oc, t0),
                                                            tcol(oc, t0) + 512)],
                                        writes=[r(Z, 0, 128, tcol(oc, t0), tcol(oc, t0) + 512)])
                    hid0 += gsz
                if b in dump_batches and "LNZ" in dump_stages:
                    dump(LNZ)
                p.phase_barrier()

        def final_ln_store(b):
          with ExitStack() as ph:
            XTM = Buf(ph.enter_context(nc.sbuf_tensor(p.uname("XTM"), [128, 1024], F32)),
                      "XTM", (128, 1024), F32)
            for tt in range(T // 128):
                t0 = tt * 128
                for half in range(2):
                    for k in range(4):
                        cc = half * 4 + k
                        p.op("v", lambda e, cc=cc, k=k, t0=t0: e.tensor_add(
                            SUMT[:, 128 * k:128 * (k + 1)],
                            X1[:, tcol(cc, t0):tcol(cc, t0) + 128],
                            Z[:, tcol(cc, t0):tcol(cc, t0) + 128]),
                            reads=[r(X1, 0, 128, tcol(cc, t0), tcol(cc, t0) + 128),
                                   r(Z, 0, 128, tcol(cc, t0), tcol(cc, t0) + 128)],
                            writes=[r(SUMT, 0, 128, 128 * k, 128 * (k + 1))])
                    for k in range(4):
                        p.op("t", lambda e, k=k: e.transpose(
                            PTRN[:, 128 * k:128 * (k + 1)], SUMT[:, 128 * k:128 * (k + 1)],
                            IDNF[:, :]),
                            reads=[r(SUMT, 0, 128, 128 * k, 128 * (k + 1)), rfull(IDNF)],
                            writes=[r(PTRN, 0, 128, 128 * k, 128 * (k + 1))])
                    p.op("s", lambda e, half=half: e.copy(
                        out=XTM[:, 512 * half:512 * (half + 1)], in_=PTRN[:, :]),
                        reads=[rfull(PTRN)],
                        writes=[r(XTM, 0, 128, 512 * half, 512 * (half + 1))])
                p.op("v", lambda e: e.tensor_reduce(MUC[:, 0:1], XTM[:, :], axis=AX.X,
                                                    op=ALU.add),
                     reads=[rfull(XTM)], writes=[r(MUC, 0, 128, 0, 1)])
                p.op("s", lambda e, XTM=XTM: e.activation(
                    out=STATB[:, :], in_=XTM[:, 0:512], func=AF.Square,
                    accum_out=MUC[:, 1:2]),
                    reads=[rfull(XTM)], writes=[rfull(STATB), r(MUC, 0, 128, 1, 2)])
                p.op("s", lambda e, XTM=XTM: e.activation(
                    out=STATB[:, :], in_=XTM[:, 512:1024], func=AF.Square,
                    accum_out=MUC[:, 3:4]),
                    reads=[rfull(XTM)], writes=[rfull(STATB), r(MUC, 0, 128, 3, 4)])
                p.op("v", lambda e: e.tensor_add(MUC[:, 1:2], MUC[:, 1:2], MUC[:, 3:4]),
                     reads=[r(MUC, 0, 128, 1, 4)], writes=[r(MUC, 0, 128, 1, 2)])
                p.op("v", lambda e: e.tensor_scalar_mul(MUC[:, 0:1], MUC[:, 0:1], 1.0 / C),
                     reads=[r(MUC, 0, 128, 0, 1)], writes=[r(MUC, 0, 128, 0, 1)])
                p.op("v", lambda e: e.tensor_scalar_mul(MUC[:, 1:2], MUC[:, 1:2], 1.0 / C),
                     reads=[r(MUC, 0, 128, 1, 2)], writes=[r(MUC, 0, 128, 1, 2)])
                p.op("v", lambda e: e.tensor_mul(MUC[:, 2:3], MUC[:, 0:1], MUC[:, 0:1]),
                     reads=[r(MUC, 0, 128, 0, 1)], writes=[r(MUC, 0, 128, 2, 3)])
                p.op("v", lambda e: e.tensor_sub(MUC[:, 1:2], MUC[:, 1:2], MUC[:, 2:3]),
                     reads=[r(MUC, 0, 128, 1, 3)], writes=[r(MUC, 0, 128, 1, 2)])
                p.op("s", lambda e: e.activation(out=MUC[:, 1:2], in_=MUC[:, 1:2], func=AF.Ln,
                                                 bias=KEPS[:, 0:1]),
                     reads=[r(MUC, 0, 128, 1, 2), rfull(KEPS)], writes=[r(MUC, 0, 128, 1, 2)])
                p.op("s", lambda e: e.activation(out=MUC[:, 1:2], in_=MUC[:, 1:2], func=AF.Exp,
                                                 scale=-0.5),
                     reads=[r(MUC, 0, 128, 1, 2)], writes=[r(MUC, 0, 128, 1, 2)])
                p.op("v", lambda e: e.tensor_scalar(
                    out=XTM[:, :], in0=XTM[:, :], scalar1=MUC[:, 0:1], scalar2=MUC[:, 1:2],
                    op0=ALU.subtract, op1=ALU.mult),
                    reads=[rfull(XTM), r(MUC, 0, 128, 0, 2)], writes=[rfull(XTM)])
                tok = p.op("y", lambda e, b=b, t0=t0: e.dma_start(
                    out=out_t.ap()[b * T + t0:b * T + t0 + 128, :], in_=XTM[:, :]),
                    reads=[rfull(XTM)], dma=True)
                p.out_dma_tokens.append(tok)
            p.phase_barrier()

        dump_batches = {nbatch - 1}
        dump_stages = set(dumps)
        init_consts()
        init_rotary()
        for b in range(nbatch):
            load_input_batch(b)
            if "rwkv" in stages:
                rwkv_batch(b)
            else:
                for cc in range(NCC):
                    p.op("v", lambda e, cc=cc: e.tensor_copy(
                        out=X1[:, cc * TC:(cc + 1) * TC], in_=Z[:, cc * TC:(cc + 1) * TC]),
                        reads=[r(Z, 0, 128, cc * TC, (cc + 1) * TC)],
                        writes=[r(X1, 0, 128, cc * TC, (cc + 1) * TC)])
            for l in range(2):
                if f"att{l}" in stages or "att" in stages:
                    ffn_att_batch(b, l, X1 if l == 0 else Z)
                if f"mlp{l}" in stages or "mlp" in stages:
                    ffn_mlp_batch(b, l)
            final_ln_store(b)
        for nm in dump_stages:
            if nm in ("KH", "VH", "WSLAB", "XKB", "SUMT", "SCN", "EKS", "SBB"):
                continue
            if nm in p.bufs:
                dump(p.bufs[nm])

        p.wait_tokens("y", p.out_dma_tokens + dump_toks)

        block = ctx.enter_context(nc.Block())
        p.finalize_into_block(block)

    return nc


def _prep_weights(inputs):
    f = np.ascontiguousarray
    w = {}
    w["wkT"] = f(inputs["L0_attn_Wk"].T)
    w["wvT"] = f(inputs["L0_attn_Wv"].T)
    w["wrT"] = f(inputs["L0_attn_Wr"].T)
    w["woT"] = f(inputs["L0_attn_Wo"].T)
    w["mixk"] = f(inputs["L0_attn_mix_k"].reshape(-1))
    w["mixv"] = f(inputs["L0_attn_mix_v"].reshape(-1))
    w["mixr"] = f(inputs["L0_attn_mix_r"].reshape(-1))
    w["tdec"] = f(inputs["L0_attn_time_decay"])
    w["tfirst"] = f(inputs["L0_attn_time_first"])
    pad = HIDP - HID
    for l in range(2):
        w[f"wqkv{l}"] = f(inputs["L0_ffn_wqkv"][l])
        w[f"qsc{l}"] = f(inputs["L0_ffn_q_scale"][l])
        w[f"ksc{l}"] = f(inputs["L0_ffn_k_scale"][l])
        w[f"woffn{l}"] = f(inputs["L0_ffn_wo"][l])
        w[f"lng{l}"] = f(inputs["L0_ffn_ln_g"][l])
        w[f"lnb{l}"] = f(inputs["L0_ffn_ln_b"][l])
        w1 = inputs["L0_ffn_w1"][l]
        b1 = inputs["L0_ffn_b1"][l]
        w[f"wa{l}"] = np.pad(w1[:, :HID], ((0, 0), (0, pad)))
        w[f"wg{l}"] = np.pad(w1[:, HID:], ((0, 0), (0, pad)))
        w[f"b1a{l}"] = np.pad(b1[:HID], (0, pad))
        w[f"b1g{l}"] = np.pad(b1[HID:], (0, pad))
        w[f"w2_{l}"] = np.pad(inputs["L0_ffn_w2"][l], ((0, pad), (0, 0)))
        w[f"b2_{l}"] = f(inputs["L0_ffn_b2"][l])
    return {k: np.ascontiguousarray(np.asarray(v, dtype=np.float32)) for k, v in w.items()}


def kernel(**inputs):
    x = np.asarray(inputs["L0_attn_x"], dtype=np.float32)
    w = _prep_weights({k: np.asarray(v, dtype=np.float32) for k, v in inputs.items()})
    nc = build_program()
    in_maps = []
    for c in range(NCORES):
        m = dict(w)
        m["x"] = np.ascontiguousarray(x[c * B_LOC:(c + 1) * B_LOC].reshape(B_LOC * T, C))
        in_maps.append(m)
    res = run_bass_kernel_spmd(nc, in_maps, core_ids=list(range(NCORES)))
    out = np.concatenate([res.results[c]["out"].reshape(B_LOC, T, C) for c in range(NCORES)],
                         axis=0)
    return out


# version 37
# speedup vs baseline: 1.0909x; 1.0408x over previous
import numpy as np
import ml_dtypes
from contextlib import ExitStack

import concourse.bass as bass
import concourse.mybir as mybir
from concourse.bass_utils import run_bass_kernel_spmd

F32 = mybir.dt.float32
BF16 = mybir.dt.bfloat16
AF = mybir.ActivationFunctionType
ALU = mybir.AluOpType
AX = mybir.AxisListType

NCORES = 8
B_LOC = 4
T = 2048
C = 1024
NCC = 8
TC = T + 1
H_ATT = 8
D_ATT = 64
INNER = 512
WINDOW = 512
NW = T // WINDOW
HID = 2730
HIDP = 2816
NHC = HIDP // 128
QK_SCALE = 8.0
LN_EPS = 1e-5

ENG = ("t", "v", "s", "g", "y")
ENG_FULL = {"t": "tensor", "v": "vector", "s": "scalar", "g": "gpsimd", "y": "sync"}
SELF_WAIT = {"t": False, "v": True, "s": True, "g": True, "y": False}


class _Op:
    __slots__ = ("fn", "waits", "inc", "need_inc", "cumval")

    def __init__(self, fn):
        self.fn = fn
        self.waits = []
        self.inc = None
        self.need_inc = False
        self.cumval = None


class Buf:
    __slots__ = ("h", "name", "shape", "dtype")

    def __init__(self, h, name, shape, dtype):
        self.h = h
        self.name = name
        self.shape = shape
        self.dtype = dtype

    def __getitem__(self, key):
        return self.h.ap()[key]

    def ap(self):
        return self.h.ap()


class Prog:
    def __init__(self, nc, ctx):
        self.nc = nc
        self.ctx = ctx
        self.ops = {e: [] for e in ENG}
        self.sem = {e: ctx.enter_context(nc.semaphore(f"c_{e}")) for e in ENG}
        self.dsem = {e: ctx.enter_context(nc.semaphore(f"d_{e}")) for e in ("y", "g", "s")}
        self.sem_n = {e: 0 for e in ENG}
        self.dsem_n = {e: 0 for e in ENG}
        self.waited = {}
        self.dma_barrier = {}
        self.regions = {}
        self.out_dma_tokens = []
        self.bufs = {}
        self._names = {}

    def uname(self, base):
        n = self._names.get(base, 0)
        self._names[base] = n + 1
        return base if n == 0 else f"{base}_{n}"

    def sbuf(self, name, shape, dtype):
        h = self.ctx.enter_context(self.nc.sbuf_tensor(self.uname(name), list(shape), dtype))
        b = Buf(h, name, tuple(shape), dtype)
        self.bufs[name] = b
        return b

    def psum(self, name, shape, dtype):
        h = self.ctx.enter_context(self.nc.psum_tensor(self.uname(name), list(shape), dtype))
        b = Buf(h, name, tuple(shape), dtype)
        self.bufs[name] = b
        return b

    def _ensure_inc(self, e, pos, kind):
        lst = self.ops[e]
        if kind == "d":
            op = lst[pos]
            assert op.inc is not None
            self.dma_barrier[e] = self.dsem_n[e]
            return (op.inc[0], 16 * self.dsem_n[e])
        lst[pos].need_inc = True
        return ("c", e, pos)

    def op_slow_dma(self, e, fn, writes=()):
        self.op(e, fn, reads=(), writes=writes, dma=True)

    def op(self, e, fn, reads=(), writes=(), dma=False):
        o = _Op(fn)
        lst = self.ops[e]
        pos = len(lst)
        need = {}
        for (buf, p0, p1, f0, f1) in reads:
            for a in self.regions.get(buf.name, ()):
                if a[0] == "w" and a[3] < p1 and p0 < a[4] and a[5] < f1 and f0 < a[6]:
                    need[(a[1], a[2], a[7])] = True
        for (buf, p0, p1, f0, f1) in writes:
            for a in self.regions.get(buf.name, ()):
                if a[3] < p1 and p0 < a[4] and a[5] < f1 and f0 < a[6]:
                    need[(a[1], a[2], a[7])] = True
        for (ae, apos, akind) in sorted(need):
            if ae == e and akind != "d" and not SELF_WAIT[e]:
                continue
            w = self._ensure_inc(ae, apos, akind)
            if w[0] == "c":
                key = (e, "c", ae)
                if self.waited.get(key, -1) < w[2]:
                    o.waits.append(w)
                    self.waited[key] = w[2]
            else:
                sem, val = w
                key = (e, sem.name)
                if self.waited.get(key, -1) < val:
                    o.waits.append(w)
                    self.waited[key] = val
        if dma:
            if lst and lst[-1].inc is not None and lst[-1].inc[0] is self.dsem[e]:
                lst.append(_Op(lambda eng: eng.nop()))
            self.dsem_n[e] += 1
            o.inc = (self.dsem[e], 16, 16 * self.dsem_n[e])
        kind = "d" if dma else "c"
        pos = len(lst)
        lst.append(o)
        for (buf, p0, p1, f0, f1) in writes:
            rl = self.regions.setdefault(buf.name, [])
            rl[:] = [a for a in rl if not (p0 <= a[3] and a[4] <= p1 and f0 <= a[5] and a[6] <= f1)]
            rl.append(("w", e, pos, p0, p1, f0, f1, kind))
        for (buf, p0, p1, f0, f1) in reads:
            rl = self.regions.setdefault(buf.name, [])
            rl[:] = [a for a in rl
                     if not (a[0] == "r" and a[1] == e and a[3] == p0 and a[4] == p1
                             and a[5] == f0 and a[6] == f1)]
            rl.append(("r", e, pos, p0, p1, f0, f1, kind))
        return (e, pos, kind)

    def phase_barrier(self):
        pairs = {}
        for oe in ENG:
            lst = self.ops[oe]
            if not lst:
                continue
            last = lst[-1]
            if last.inc is not None:
                pairs[oe] = (last.inc[0], 16 * self.dsem_n[oe])
            else:
                last.need_inc = True
                pairs[oe] = ("c", oe, len(lst) - 1)
        for e in ENG:
            o = _Op(lambda eng: eng.nop())
            for k, v in pairs.items():
                if k == e:
                    continue
                o.waits.append(v)
                if v[0] == "c":
                    key = (e, "c", v[1])
                    if self.waited.get(key, -1) < v[2]:
                        self.waited[key] = v[2]
                else:
                    key = (e, v[0].name)
                    if self.waited.get(key, -1) < v[1]:
                        self.waited[key] = v[1]
            self.ops[e].append(o)
        self.regions.clear()

    def wait_tokens(self, e, toks):
        o = _Op(lambda eng: eng.nop())
        o.waits = [self._ensure_inc(*t) for t in toks]
        self.ops[e].append(o)

    def finalize_into_block(self, block):
        for e in ENG:
            cum = 0
            for o in self.ops[e]:
                if o.need_inc and o.inc is None:
                    o.inc = (self.sem[e], 1, None)
                if o.inc is not None and o.inc[0] is self.sem[e]:
                    cum += 1
                    o.cumval = cum
        for e in ENG:
            for o in self.ops[e]:
                rw = []
                for w in o.waits:
                    if w[0] == "c":
                        _, ae, pos = w
                        rw.append((self.sem[ae], self.ops[ae][pos].cumval))
                    else:
                        rw.append(w)
                o.waits = rw
        for e in ENG:
            section = getattr(block, ENG_FULL[e])
            ops = self.ops[e]

            def make(ops):
                def body(eng):
                    for o in ops:
                        for sem, val in o.waits:
                            eng.wait_ge(sem, val)
                        inst = o.fn(eng)
                        if o.inc is not None:
                            inst.then_inc(o.inc[0], o.inc[1])

                return body
            section(make(ops))


def r(buf, p0, p1, f0, f1):
    return (buf, p0, p1, f0, f1)


def rfull(buf):
    return (buf, 0, buf.shape[0], 0, buf.shape[1])





def build_program(nbatch=B_LOC, stages=("rwkv", "att", "mlp"), dumps=(), rwkv_sub="full"):
    nc = bass.Bass("TRN2", target_bir_lowering=False)
    dt_in = {}

    def din(name, shape):
        t = nc.dram_tensor(name, list(shape), F32, kind="ExternalInput")
        dt_in[name] = t
        return t

    x_in = din("x", (B_LOC * T, C))
    out_t = nc.dram_tensor("out", [B_LOC * T, C], F32, kind="ExternalOutput")

    din("wkT", (C, C)); din("wvT", (C, C)); din("wrT", (C, C)); din("woT", (C, C))
    din("mixk", (C,)); din("mixv", (C,)); din("mixr", (C,))
    din("tdec", (16,)); din("tfirst", (16,))
    for l in range(2):
        din(f"wqkv{l}", (C, 3 * INNER))
        din(f"qsc{l}", (D_ATT,)); din(f"ksc{l}", (D_ATT,))
        din(f"woffn{l}", (INNER, C))
        din(f"lng{l}", (C,)); din(f"lnb{l}", (C,))
        din(f"wa{l}", (C, HIDP)); din(f"wg{l}", (C, HIDP))
        din(f"b1a{l}", (HIDP,)); din(f"b1g{l}", (HIDP,))
        din(f"w2_{l}", (HIDP, C)); din(f"b2_{l}", (C,))

    WCAST = {}
    for nm in ("wkT", "wvT", "wrT", "woT", "wqkv0", "wqkv1", "woffn0", "woffn1",
               "wa0", "wa1", "wg0", "wg1", "w2_0", "w2_1"):
        rows, cols = dt_in[nm].shape
        tot = (rows * cols) // 128
        WCAST[nm] = (nc.dram_tensor(nm + "_b", [rows * cols], BF16,
                                    kind="Internal"), tot)

    ctx = ExitStack()
    p = Prog(nc, ctx)
    dump_toks = []

    def dump(buf):
        dt = nc.dram_tensor(f"dump_{buf.name}", list(buf.shape), buf.dtype,
                            kind="ExternalOutput")
        tok = p.op("y", lambda e, buf=buf, dt=dt: e.dma_start(
            out=dt.ap()[:, :], in_=buf[:, :]), reads=[rfull(buf)], dma=True)
        dump_toks.append(tok)

    with ctx:
        X1 = p.sbuf("X1", (128, NCC * TC), F32)
        Z = p.sbuf("Z", (128, NCC * TC), F32)
        IDNF = p.sbuf("IDNF", (128, 128), F32)
        IDNB = p.sbuf("IDNB", (128, 128), BF16)
        ONE1B = p.sbuf("ONE1B", (128, 1), BF16)
        ONE1F = p.sbuf("ONE1F", (1, 128), F32)
        ONE1BR = p.sbuf("ONE1BR", (1, 128), BF16)
        ONES2 = p.sbuf("ONES2", (128, 2), BF16)
        SEL2 = p.sbuf("SEL2", (2, 128), BF16)
        SEL2A = p.sbuf("SEL2A", (1, 128), BF16)
        SEL2B = p.sbuf("SEL2B", (1, 128), BF16)
        MCAT = p.sbuf("MCAT", (128, 256), BF16)
        MIXC = p.sbuf("MIXC", (128, 24), F32)
        EWC = p.sbuf("EWC", (128, 8), F32)
        EUC = p.sbuf("EUC", (128, 8), F32)
        EW2 = p.sbuf("EW2", (2, 8), F32)
        EU2 = p.sbuf("EU2", (2, 8), F32)
        SA = p.sbuf("SA", (128, 8), F32)
        SB2 = p.sbuf("SB2", (2, 8), F32)
        STATB = p.sbuf("STATB", (128, 512), BF16)
        SUMT = p.sbuf("SUMT", (128, 512), F32)
        MUC = p.sbuf("MUC", (128, 4), F32)
        KEPS = p.sbuf("KEPS", (128, 1), F32)
        PQC = p.sbuf("PQC", (128, 128), F32)
        PQS = p.sbuf("PQS", (128, 128), F32)
        PKC = p.sbuf("PKC", (128, 128), F32)
        PKS = p.sbuf("PKS", (128, 128), F32)
        RSTC = p.sbuf("RSTC", (128, 256), F32)

        PB = [p.psum(f"PB{i}", (128, 512), F32) for i in range(4)]
        PTRN = p.psum("PTRN", (128, 512), F32)
        PTRB = p.psum("PTRB", (128, 512), BF16)
        PROW = p.psum("PROW", (128, 512), F32)
        PAV = p.psum("PAV", (65, 512), F32)

        def tcol(cc, t):
            return cc * TC + 1 + t

        def aff(buf, sl, cmp, fill, base, pattern, cm):
            p.op("g", lambda eng: eng.affine_select(out=sl, in_=sl, compare_op=cmp,
                                                    fill=fill, base=base, pattern=pattern,
                                                    channel_multiplier=cm),
                 reads=[rfull(buf)], writes=[rfull(buf)])

        def init_consts():
            for buf in (IDNF, IDNB):
                p.op("g", lambda e, b=buf: e.memset(b[:, :], 0.0), writes=[rfull(buf)])
                aff(buf, buf[:, :], ALU.not_equal, 1.0, 0, [[-1, 128]], 1)
            p.op("g", lambda e: e.memset(ONE1B[:, :], 1.0), writes=[rfull(ONE1B)])
            p.op("g", lambda e: e.memset(ONE1F[:, :], 1.0), writes=[rfull(ONE1F)])
            p.op("g", lambda e: e.memset(ONE1BR[:, :], 1.0), writes=[rfull(ONE1BR)])
            p.op("g", lambda e: e.memset(ONES2[:, :], 0.0), writes=[rfull(ONES2)])
            aff(ONES2, ONES2[:, 0:1], ALU.is_ge, 1.0, -64, [[0, 1]], 1)
            aff(ONES2, ONES2[:, 1:2], ALU.is_gt, 1.0, 64, [[0, 1]], -1)
            p.op("g", lambda e: e.memset(SEL2[:, :], 0.0), writes=[rfull(SEL2)])
            p.op("g", lambda e: e.memset(SEL2A[:, :], 0.0), writes=[rfull(SEL2A)])
            aff(SEL2A, SEL2A[:, :], ALU.is_ge, 1.0, -64, [[1, 128]], 0)
            p.op("g", lambda e: e.memset(SEL2B[:, :], 0.0), writes=[rfull(SEL2B)])
            aff(SEL2B, SEL2B[:, :], ALU.is_gt, 1.0, 64, [[-1, 128]], 0)
            aff(SEL2, SEL2[:, :], ALU.is_ge, 1.0, -64, [[1, 128]], 999)
            aff(SEL2, SEL2[:, :], ALU.is_gt, 1.0, 1063, [[-1, 128]], -999)
            p.op("g", lambda e: e.memset(MCAT[:, :], 0.0), writes=[rfull(MCAT)])
            aff(MCAT, MCAT[:, 0:128], ALU.is_gt, 1.0, 0, [[-1, 128]], 1)
            aff(MCAT, MCAT[:, 128:256], ALU.is_gt, 1.0, 0, [[1, 128]], -1)
            for i, nm in enumerate(("mixk", "mixv", "mixr")):
                def fn(e, nm=nm, i=i):
                    src = bass.AP(dt_in[nm], 0, [[1, 128], [128, 8]])
                    return e.dma_start(out=MIXC[:, 8 * i:8 * (i + 1)], in_=src,
                                       allow_slow_non_contiguous=True)
                p.op_slow_dma("g", fn, writes=[r(MIXC, 0, 128, 8 * i, 8 * (i + 1))])

            def rep(dst, nm, rows):
                if rows == 128:
                    for hh in range(2):
                        def fn(e, dst=dst, nm=nm, hh=hh):
                            ap = bass.AP(dt_in[nm], hh, [[0, 64], [2, 8]])
                            return e.dma_start(out=dst[64 * hh:64 * (hh + 1), :], in_=ap,
                                               allow_slow_non_contiguous=True)
                        p.op_slow_dma("g", fn, writes=[r(dst, 64 * hh, 64 * (hh + 1), 0, 8)])
                else:
                    def fn(e, dst=dst, nm=nm):
                        ap = bass.AP(dt_in[nm], 0, [[1, 2], [2, 8]])
                        return e.dma_start(out=dst[:, :], in_=ap,
                                           allow_slow_non_contiguous=True)
                    p.op_slow_dma("g", fn, writes=[rfull(dst)])
            rep(EWC, "tdec", 128); rep(EUC, "tfirst", 128)
            rep(EW2, "tdec", 2); rep(EU2, "tfirst", 2)
            for buf in (EWC, EW2):
                p.op("s", lambda e, b=buf: e.activation(out=b[:, :], in_=b[:, :], func=AF.Exp),
                     reads=[rfull(buf)], writes=[rfull(buf)])
                p.op("s", lambda e, b=buf: e.activation(out=b[:, :], in_=b[:, :], func=AF.Exp,
                                                        scale=-1.0),
                     reads=[rfull(buf)], writes=[rfull(buf)])
            for buf in (EUC, EU2):
                p.op("s", lambda e, b=buf: e.activation(out=b[:, :], in_=b[:, :], func=AF.Exp),
                     reads=[rfull(buf)], writes=[rfull(buf)])
            p.op("v", lambda e: e.memset(KEPS[:, :], LN_EPS), writes=[rfull(KEPS)])
            for buf in (X1, Z):
                def zc(e, b=buf):
                    ap = bass.AP(b.h, 0, [[NCC * TC, 128], [TC, 8], [1, 1]])
                    return e.memset(ap, 0.0)
                p.op("v", zc, writes=[r(buf, 0, 128, 0, buf.shape[1])])

        def init_rotary():
          with ExitStack() as ph:
            TWO_PI = 2.0 * np.pi
            mk = lambda name, shape, dt: Buf(
                ph.enter_context(nc.sbuf_tensor(p.uname(name), list(shape), dt)),
                name, tuple(shape), dt)
            IVFC = mk("IVFC", (128, 32), F32)
            LNBC = mk("LNBC", (128, 32), F32)
            BKIC = mk("BKIC", (128, 32), mybir.dt.int32)
            for d2 in range(32):
                ivf = 1.0 / 10000.0 ** (2.0 * d2 / 64.0)
                lnb = float(np.log((2.0 * d2 + 0.4 * 64.0) / (1.4 * 64.0)))
                bsq = float(np.exp(2.0 * lnb))
                c5, s5 = float(np.cos(512.0 * ivf)), float(np.sin(512.0 * ivf))
                cb, sb = float(np.cos(1536.0 * ivf)), float(-np.sin(1536.0 * ivf))
                b3 = float(np.exp(-6.0 * lnb))
                for k4, v in enumerate((c5 * bsq, s5 * bsq, c5 / bsq, s5 / bsq,
                                        cb * b3, sb * b3, cb / b3, sb / b3)):
                    col = 32 * k4 + d2
                    p.op("g", lambda e, c=float(np.float32(v)), col=col: e.memset(
                        RSTC[:, col:col + 1], c),
                        writes=[r(RSTC, 0, 128, col, col + 1)])
                p.op("g", lambda e, c=float(np.float32(ivf)), d2=d2: e.memset(
                    IVFC[:, d2:d2 + 1], c), writes=[r(IVFC, 0, 128, d2, d2 + 1)])
                p.op("g", lambda e, c=float(np.float32(lnb)), d2=d2: e.memset(
                    LNBC[:, d2:d2 + 1], c), writes=[r(LNBC, 0, 128, d2, d2 + 1)])
            S = SUMT
            for tt in range(4):
                c0 = 32 * tt
                p.op("g", lambda e: e.iota(S[:, 288:320], pattern=[[0, 32]], base=0,
                                           channel_multiplier=1,
                                           allow_small_or_imprecise_dtypes=True),
                     writes=[r(S, 0, 128, 288, 320)])
                p.op("v", lambda e, tt=tt: e.scalar_tensor_tensor(
                    out=S[:, 0:32], in0=S[:, 288:320], scalar=float(128.0 * tt),
                    in1=IVFC[:, :], op0=ALU.add, op1=ALU.mult),
                    reads=[r(S, 0, 128, 288, 320), rfull(IVFC)],
                    writes=[r(S, 0, 128, 0, 32)])
                for shift, dstc in ((0.0, 192), (float(np.pi / 2), 160)):
                    p.op("v", lambda e, shift=shift: e.tensor_scalar(
                        out=S[:, 32:64], in0=S[:, 0:32], scalar1=float(1.0 / TWO_PI),
                        scalar2=float(shift / TWO_PI), op0=ALU.mult, op1=ALU.add),
                        reads=[r(S, 0, 128, 0, 32)], writes=[r(S, 0, 128, 32, 64)])
                    p.op("v", lambda e: e.tensor_copy(out=BKIC[:, :], in_=S[:, 32:64]),
                         reads=[r(S, 0, 128, 32, 64)], writes=[rfull(BKIC)])
                    p.op("v", lambda e: e.tensor_copy(out=S[:, 64:96], in_=BKIC[:, :]),
                         reads=[rfull(BKIC)], writes=[r(S, 0, 128, 64, 96)])
                    p.op("v", lambda e: e.tensor_sub(S[:, 96:128], S[:, 32:64], S[:, 64:96]),
                         reads=[r(S, 0, 128, 32, 96)], writes=[r(S, 0, 128, 96, 128)])
                    p.op("v", lambda e: e.tensor_scalar(
                        out=S[:, 96:128], in0=S[:, 96:128], scalar1=0.5, scalar2=None,
                        op0=ALU.is_ge),
                        reads=[r(S, 0, 128, 96, 128)], writes=[r(S, 0, 128, 96, 128)])
                    p.op("v", lambda e: e.tensor_add(S[:, 64:96], S[:, 64:96], S[:, 96:128]),
                         reads=[r(S, 0, 128, 64, 128)], writes=[r(S, 0, 128, 64, 96)])
                    p.op("v", lambda e, shift=shift: e.tensor_scalar(
                        out=S[:, 128:160], in0=S[:, 64:96], scalar1=float(-TWO_PI),
                        scalar2=float(shift), op0=ALU.mult, op1=ALU.add),
                        reads=[r(S, 0, 128, 64, 96)], writes=[r(S, 0, 128, 128, 160)])
                    p.op("v", lambda e: e.tensor_add(S[:, 128:160], S[:, 128:160], S[:, 0:32]),
                         reads=[r(S, 0, 128, 0, 32), r(S, 0, 128, 128, 160)],
                         writes=[r(S, 0, 128, 128, 160)])
                    p.op("s", lambda e, dstc=dstc: e.activation(
                        out=S[:, dstc:dstc + 32], in_=S[:, 128:160], func=AF.Sin),
                        reads=[r(S, 0, 128, 128, 160)],
                        writes=[r(S, 0, 128, dstc, dstc + 32)])
                p.op("v", lambda e, tt=tt: e.tensor_scalar(
                    out=S[:, 320:321], in0=S[:, 288:289], scalar1=float(1.0 / 256.0),
                    scalar2=float((128.0 * tt - 1024.0) / 256.0), op0=ALU.mult,
                    op1=ALU.add),
                    reads=[r(S, 0, 128, 288, 289)], writes=[r(S, 0, 128, 320, 321)])
                p.op("v", lambda e: e.tensor_scalar(
                    out=S[:, 32:64], in0=LNBC[:, :], scalar1=S[:, 320:321], scalar2=None,
                    op0=ALU.mult),
                    reads=[rfull(LNBC), r(S, 0, 128, 320, 321)],
                    writes=[r(S, 0, 128, 32, 64)])
                p.op("s", lambda e: e.activation(out=S[:, 224:256], in_=S[:, 32:64],
                                                 func=AF.Exp),
                     reads=[r(S, 0, 128, 32, 64)], writes=[r(S, 0, 128, 224, 256)])
                p.op("s", lambda e: e.activation(out=S[:, 256:288], in_=S[:, 32:64],
                                                 func=AF.Exp, scale=-1.0),
                     reads=[r(S, 0, 128, 32, 64)], writes=[r(S, 0, 128, 256, 288)])
                for dst, trig, sc in ((PQC, 160, 224), (PQS, 192, 224),
                                      (PKC, 160, 256), (PKS, 192, 256)):
                    p.op("v", lambda e, dst=dst, trig=trig, sc=sc, c0=c0: e.tensor_mul(
                        dst[:, c0:c0 + 32], S[:, trig:trig + 32], S[:, sc:sc + 32]),
                        reads=[r(S, 0, 128, trig, trig + 32), r(S, 0, 128, sc, sc + 32)],
                        writes=[r(dst, 0, 128, c0, c0 + 32)])
            p.phase_barrier()

        def init_wcast():
          with ExitStack() as ph:
            mk = lambda name, shape, dt: Buf(
                ph.enter_context(nc.sbuf_tensor(p.uname(name), list(shape), dt)),
                name, tuple(shape), dt)
            SFr = [mk("WCF", (128, 2048), F32) for _ in range(2)]
            SBr = [mk("WCB", (128, 2048), BF16) for _ in range(2)]
            ci = 0
            for nm, (mir, tot) in WCAST.items():
                for c0 in range(0, tot, 2048):
                    w = min(2048, tot - c0)
                    SF, SB = SFr[ci % 2], SBr[ci % 2]
                    ci += 1
                    p.op("y", lambda e, nm=nm, c0=c0, w=w, SF=SF, tot=tot:
                         e.dma_start(out=SF[:, 0:w], in_=bass.AP(
                             dt_in[nm], c0, [[tot, 128], [1, w]])),
                         writes=[r(SF, 0, 128, 0, w)], dma=True)
                    p.op("v", lambda e, w=w, SF=SF, SB=SB: e.tensor_copy(
                        out=SB[:, 0:w], in_=SF[:, 0:w]),
                        reads=[r(SF, 0, 128, 0, w)], writes=[r(SB, 0, 128, 0, w)])
                    p.op("y", lambda e, mir=mir, c0=c0, w=w, SB=SB, tot=tot:
                         e.dma_start(out=bass.AP(mir, c0, [[tot, 128], [1, w]]),
                                     in_=SB[:, 0:w]),
                         reads=[r(SB, 0, 128, 0, w)], dma=True)
            p.phase_barrier()

        def load_input_batch(b):
          with ExitStack() as ph:
            XTMr = [Buf(ph.enter_context(nc.sbuf_tensor(p.uname("XTM"), [128, 1024], F32)),
                        "XTM", (128, 1024), F32) for _ in range(2)]
            for tt in range(T // 128):
                t0 = tt * 128
                XTM = XTMr[tt % 2]
                PTX = (PTRN, PB[0])[tt % 2]
                p.op("y", lambda e, b=b, t0=t0, XTM=XTM: e.dma_start(
                    out=XTM[:, :], in_=x_in.ap()[b * T + t0: b * T + t0 + 128, :]),
                    writes=[rfull(XTM)], dma=True)
                for half in range(2):
                    for k in range(4):
                        cc = half * 4 + k
                        p.op("t", lambda e, cc=cc, k=k, XTM=XTM, PTX=PTX: e.transpose(
                            PTX[:, 128 * k:128 * (k + 1)],
                            XTM[:, 128 * cc:128 * (cc + 1)], IDNF[:, :]),
                            reads=[r(XTM, 0, 128, 128 * cc, 128 * (cc + 1)), rfull(IDNF)],
                            writes=[r(PTX, 0, 128, 128 * k, 128 * (k + 1))])
                    cc0 = half * 4

                    def ev(e, cc0=cc0, t0=t0, PTX=PTX):
                        dst = bass.AP(Z.h, tcol(cc0, t0), [[NCC * TC, 128], [TC, 4], [1, 128]])
                        return e.tensor_copy(out=dst, in_=PTX[:, :])
                    p.op("v", ev, reads=[rfull(PTX)],
                         writes=[r(Z, 0, 128, tcol(cc0, t0), tcol(cc0 + 3, t0) + 128)])
            p.phase_barrier()

        def ln_rows_mid():
            (MB, mr, mc), (RB, rr, rc) = p.bufs["LNROWV"]
            MU = MB[mr:mr + 1, mc:mc + 512]
            RS = RB[rr:rr + 1, rc:rc + 512]
            mreg = r(MB, mr, mr + 1, mc, mc + 512)
            rreg = r(RB, rr, rr + 1, rc, rc + 512)
            p.op("v", lambda e: e.tensor_scalar_mul(MU, PROW[0:1, :], 1.0 / C),
                 reads=[r(PROW, 0, 1, 0, 512)], writes=[mreg])
            p.op("v", lambda e: e.tensor_mul(PROW[0:1, :], MU, MU),
                 reads=[mreg, r(PROW, 0, 1, 0, 512)],
                 writes=[r(PROW, 0, 1, 0, 512)])
            p.op("v", lambda e: e.tensor_scalar_mul(RS, PAV[0:1, :], 1.0 / C),
                 reads=[r(PAV, 0, 1, 0, 512)], writes=[rreg])
            p.op("v", lambda e: e.tensor_sub(RS, RS, PROW[0:1, :]),
                 reads=[rreg, r(PROW, 0, 1, 0, 512)], writes=[rreg])
            p.op("s", lambda e: e.activation(out=RS, in_=RS,
                                             func=AF.Ln, bias=KEPS[0:1, 0:1]),
                 reads=[rreg, rfull(KEPS)], writes=[rreg])
            p.op("s", lambda e: e.activation(out=RS, in_=RS, func=AF.Exp, scale=-0.5),
                 reads=[rreg], writes=[rreg])
            p.op("t", lambda e: e.matmul(PROW[:, :], ONE1F[:, :], RS,
                                         start=True, stop=True),
                 reads=[rfull(ONE1F), rreg], writes=[rfull(PROW)])
            p.op("v", lambda e: e.scalar_tensor_tensor(
                out=MU, in0=MU, scalar=-1.0, in1=RS, op0=ALU.mult, op1=ALU.mult),
                reads=[mreg, rreg], writes=[mreg])
            p.op("t", lambda e: e.matmul(PTRN[:, :], ONE1F[:, :], MU,
                                         start=True, stop=True),
                 reads=[rfull(ONE1F), mreg], writes=[rfull(PTRN)])

        def ln_channel_major(src, dst, t0):
            for cc in range(NCC):
                sl = r(src, 0, 128, tcol(cc, t0), tcol(cc, t0) + 512)
                p.op("s", lambda e, cc=cc, t0=t0, src=src: e.copy(
                    out=STATB[:, 0:512], in_=src[:, tcol(cc, t0):tcol(cc, t0) + 512]),
                    reads=[sl], writes=[r(STATB, 0, 128, 0, 512)])
                p.op("t", lambda e, cc=cc: e.matmul(
                    PROW[0:1, :], ONE1B[:, :], STATB[:, 0:512],
                    start=(cc == 0), stop=(cc == NCC - 1)),
                    reads=[rfull(ONE1B), r(STATB, 0, 128, 0, 512)],
                    writes=[r(PROW, 0, 1, 0, 512)])
                p.op("s", lambda e, cc=cc, t0=t0, src=src: e.activation(
                    out=STATB[:, 0:512], in_=src[:, tcol(cc, t0):tcol(cc, t0) + 512],
                    func=AF.Square),
                    reads=[sl], writes=[r(STATB, 0, 128, 0, 512)])
                p.op("t", lambda e, cc=cc: e.matmul(
                    PAV[0:1, :], ONE1B[:, :], STATB[:, 0:512],
                    start=(cc == 0), stop=(cc == NCC - 1)),
                    reads=[rfull(ONE1B), r(STATB, 0, 128, 0, 512)],
                    writes=[r(PAV, 0, 1, 0, 512)])
            ln_rows_mid()
            for cc in range(NCC):
                sl = r(src, 0, 128, tcol(cc, t0), tcol(cc, t0) + 512)
                dl = r(dst, 0, 128, tcol(cc, t0), tcol(cc, t0) + 512)
                p.op("v", lambda e, cc=cc, t0=t0, src=src, dst=dst: e.tensor_mul(
                    dst[:, tcol(cc, t0):tcol(cc, t0) + 512],
                    src[:, tcol(cc, t0):tcol(cc, t0) + 512], PROW[:, :]),
                    reads=[sl, rfull(PROW)], writes=[dl])
                p.op("v", lambda e, cc=cc, t0=t0, dst=dst: e.tensor_add(
                    dst[:, tcol(cc, t0):tcol(cc, t0) + 512],
                    dst[:, tcol(cc, t0):tcol(cc, t0) + 512], PTRN[:, :]),
                    reads=[dl, rfull(PTRN)], writes=[dl])

        def rwkv_batch(b):
            with ExitStack() as ph:
                sb = lambda name, shape, dt: Buf(
                    ph.enter_context(nc.sbuf_tensor(p.uname(name), list(shape), dt)),
                    name, tuple(shape), dt)
                WSLAB = sb("WSLAB", (128, NCC * 1024), BF16)
                LNROW = sb("LNROW", (1, 1024), F32)
                p.bufs["LNROWV"] = ((LNROW, 0, 0), (LNROW, 0, 512))
                KH = sb("KH", (128, NCC * 1024), BF16)
                VH = sb("VH", (128, NCC * 1024), BF16)
                XKB = sb("XKB", (128, NCC * 512), BF16)
                DDr = [sb("DDr", (128, 512), F32)] * 2
                EKVQ = SUMT
                SCN = sb("SCN", (128, 513), F32)
                EKS = sb("EKS", (2, 512), F32)
                SBB = sb("SBB", (2, 513), F32)
                RSIGr = [sb("RSIGr", (128, 512), F32)] * 2
                for nm, bb in (("WSLAB", WSLAB), ("KH", KH), ("VH", VH), ("XKB", XKB)):
                    p.bufs[nm] = bb

                def load_wslab(nm):
                    def fn(e, nm=nm):
                        src = bass.AP(WCAST[nm][0], 0,
                                      [[1024, 128], [131072, 8], [1, 1024]])
                        return e.dma_start(out=WSLAB[:, :].rearrange("p (cc co) -> p cc co", cc=8),
                                           in_=src)
                    return p.op("y", fn, writes=[rfull(WSLAB)], dma=True)

                def mix_tiles(t0, mixoff):
                    for cc in range(NCC):
                        DD = DDr[cc % 2]
                        xc = r(Z, 0, 128, tcol(cc, t0) - 1, tcol(cc, t0) + 512)
                        p.op("v", lambda e, DD=DD, cc=cc, t0=t0: e.tensor_sub(
                            DD[:, :], Z[:, tcol(cc, t0):tcol(cc, t0) + 512],
                            Z[:, tcol(cc, t0) - 1:tcol(cc, t0) + 511]),
                            reads=[xc], writes=[rfull(DD)])
                        p.op("v", lambda e, DD=DD, cc=cc, t0=t0, mo=mixoff: e.scalar_tensor_tensor(
                            out=XKB[:, 512 * cc:512 * (cc + 1)], in0=DD[:, :],
                            scalar=MIXC[:, mo + cc:mo + cc + 1],
                            in1=Z[:, tcol(cc, t0) - 1:tcol(cc, t0) + 511],
                            op0=ALU.mult, op1=ALU.add),
                            reads=[rfull(DD), xc, r(MIXC, 0, 128, mixoff + cc, mixoff + cc + 1)],
                            writes=[r(XKB, 0, 128, 512 * cc, 512 * (cc + 1))])

                def proj_block(dstbuf, hl0, act=None):
                    for co in range(NCC):
                        ps = PB[co % 4]
                        for cc in range(NCC):
                            p.op("t", lambda e, ps=ps, co=co, cc=cc: e.matmul(
                                ps[:, :],
                                WSLAB[:, 1024 * cc + 128 * co:1024 * cc + 128 * (co + 1)],
                                XKB[:, 512 * cc:512 * (cc + 1)],
                                start=(cc == 0), stop=(cc == NCC - 1)),
                                reads=[r(WSLAB, 0, 128, 1024 * cc + 128 * co,
                                         1024 * cc + 128 * (co + 1)),
                                       r(XKB, 0, 128, 512 * cc, 512 * (cc + 1))],
                                writes=[rfull(ps)])
                        p.op("s", lambda e, ps=ps, co=co, hl0=hl0, dstbuf=dstbuf: e.copy(
                            out=dstbuf[:, 1024 * co + hl0:1024 * co + hl0 + 512], in_=ps[:, :]),
                            reads=[rfull(ps)],
                            writes=[r(dstbuf, 0, 128, 1024 * co + hl0, 1024 * co + hl0 + 512)])

                def wkv_quarter(cc, hl0):
                    par = (cc + hl0 // 512) % 2
                    EKB0 = 1536 * par
                    RCP0 = 512 + 1536 * par
                    ksl = r(KH, 0, 128, 1024 * cc + hl0, 1024 * cc + hl0 + 512)
                    vsl = r(VH, 0, 128, 1024 * cc + hl0, 1024 * cc + hl0 + 512)
                    p.op("s", lambda e: e.activation(
                        out=XKB[:, EKB0:EKB0 + 512], in_=KH[:, 1024 * cc + hl0:1024 * cc + hl0 + 512],
                        func=AF.Exp),
                        reads=[ksl], writes=[r(XKB, 0, 128, EKB0, EKB0 + 512)])
                    p.op("v", lambda e: e.tensor_mul(
                        EKVQ[:, :], XKB[:, EKB0:EKB0 + 512], VH[:, 1024 * cc + hl0:1024 * cc + hl0 + 512]),
                        reads=[r(XKB, 0, 128, EKB0, EKB0 + 512), vsl], writes=[rfull(EKVQ)])
                    p.op("t", lambda e: e.matmul(PROW[0:2, :], ONES2[:, :], XKB[:, EKB0:EKB0 + 512],
                                                 start=True, stop=True),
                         reads=[rfull(ONES2), r(XKB, 0, 128, EKB0, EKB0 + 512)], writes=[rfull(PROW)])
                    p.op("v", lambda e: e.tensor_copy(out=EKS[:, :], in_=PROW[0:2, :]),
                         reads=[rfull(PROW)], writes=[rfull(EKS)])
                    p.op("v", lambda e: e.tensor_copy(out=SCN[:, 0:1], in_=SA[:, cc:cc + 1]),
                         reads=[r(SA, 0, 128, cc, cc + 1)], writes=[r(SCN, 0, 128, 0, 1)])
                    p.op("v", lambda e: e.tensor_tensor_scan(
                        out=SCN[:, 1:513], data0=EWC[:, cc:cc + 1].broadcast_to([128, 512]),
                        data1=EKVQ[:, :], initial=SA[:, cc:cc + 1], op0=ALU.mult, op1=ALU.add),
                        reads=[rfull(EKVQ), r(EWC, 0, 128, cc, cc + 1), r(SA, 0, 128, cc, cc + 1)],
                        writes=[r(SCN, 0, 128, 1, 513)])
                    p.op("v", lambda e: e.tensor_copy(out=SA[:, cc:cc + 1], in_=SCN[:, 512:513]),
                         reads=[r(SCN, 0, 128, 512, 513)], writes=[r(SA, 0, 128, cc, cc + 1)])
                    p.op("v", lambda e: e.scalar_tensor_tensor(
                        out=SCN[:, 0:512], in0=EKVQ[:, :], scalar=EUC[:, cc:cc + 1],
                        in1=SCN[:, 0:512], op0=ALU.mult, op1=ALU.add),
                        reads=[rfull(EKVQ), r(EUC, 0, 128, cc, cc + 1), r(SCN, 0, 128, 0, 512)],
                        writes=[r(SCN, 0, 128, 0, 512)])
                    p.op("v", lambda e: e.tensor_copy(out=SBB[:, 0:1], in_=SB2[:, cc:cc + 1]),
                         reads=[r(SB2, 0, 2, cc, cc + 1)], writes=[r(SBB, 0, 2, 0, 1)])
                    p.op("v", lambda e: e.tensor_tensor_scan(
                        out=SBB[:, 1:513], data0=EW2[:, cc:cc + 1].broadcast_to([2, 512]),
                        data1=EKS[:, :], initial=SB2[:, cc:cc + 1], op0=ALU.mult, op1=ALU.add),
                        reads=[rfull(EKS), r(EW2, 0, 2, cc, cc + 1), r(SB2, 0, 2, cc, cc + 1)],
                        writes=[r(SBB, 0, 2, 1, 513)])
                    p.op("v", lambda e: e.tensor_copy(out=SB2[:, cc:cc + 1], in_=SBB[:, 512:513]),
                         reads=[r(SBB, 0, 2, 512, 513)], writes=[r(SB2, 0, 2, cc, cc + 1)])
                    p.op("v", lambda e: e.tensor_scalar(
                        out=EKS[:, :], in0=EKS[:, :], scalar1=EU2[:, cc:cc + 1], scalar2=1e-6,
                        op0=ALU.mult, op1=ALU.max),
                        reads=[rfull(EKS), r(EU2, 0, 2, cc, cc + 1)], writes=[rfull(EKS)])
                    p.op("v", lambda e: e.tensor_add(EKS[:, :], EKS[:, :], SBB[:, 0:512]),
                         reads=[rfull(EKS), r(SBB, 0, 2, 0, 512)], writes=[rfull(EKS)])
                    p.op("s", lambda e: e.activation(out=EKS[:, :], in_=EKS[:, :], func=AF.Ln),
                         reads=[rfull(EKS)], writes=[rfull(EKS)])
                    p.op("s", lambda e: e.activation(out=XKB[0:2, RCP0:RCP0 + 512],
                                                     in_=EKS[:, :], func=AF.Exp, scale=-1.0),
                         reads=[rfull(EKS)], writes=[r(XKB, 0, 2, RCP0, RCP0 + 512)])
                    p.op("t", lambda e: e.matmul(PROW[:, :], SEL2[:, :], XKB[0:2, RCP0:RCP0 + 512],
                                                 start=True, stop=True),
                         reads=[rfull(SEL2), r(XKB, 0, 2, RCP0, RCP0 + 512)], writes=[rfull(PROW)])
                    p.op("v", lambda e: e.tensor_mul(
                        KH[:, 1024 * cc + hl0:1024 * cc + hl0 + 512], SCN[:, 0:512], PROW[:, :]),
                        reads=[r(SCN, 0, 128, 0, 512), rfull(PROW)],
                        writes=[ksl])

                def rwkv_r_block(hl0):
                    for co in range(NCC):
                        ps = PB[co % 4]
                        for cc in range(NCC):
                            p.op("t", lambda e, ps=ps, co=co, cc=cc: e.matmul(
                                ps[:, :],
                                WSLAB[:, 1024 * cc + 128 * co:1024 * cc + 128 * (co + 1)],
                                XKB[:, 512 * cc:512 * (cc + 1)],
                                start=(cc == 0), stop=(cc == NCC - 1)),
                                reads=[r(WSLAB, 0, 128, 1024 * cc + 128 * co,
                                         1024 * cc + 128 * (co + 1)),
                                       r(XKB, 0, 128, 512 * cc, 512 * (cc + 1))],
                                writes=[rfull(ps)])
                        RSIG = RSIGr[co % 2]
                        p.op("s", lambda e, ps=ps, RSIG=RSIG: e.activation(
                            out=RSIG[:, :], in_=ps[:, :], func=AF.Sigmoid),
                            reads=[rfull(ps)], writes=[rfull(RSIG)])
                        p.op("v", lambda e, co=co, hl0=hl0, RSIG=RSIG: e.tensor_mul(
                            VH[:, 1024 * co + hl0:1024 * co + hl0 + 512], RSIG[:, :],
                            KH[:, 1024 * co + hl0:1024 * co + hl0 + 512]),
                            reads=[rfull(RSIG),
                                   r(KH, 0, 128, 1024 * co + hl0, 1024 * co + hl0 + 512)],
                            writes=[r(VH, 0, 128, 1024 * co + hl0, 1024 * co + hl0 + 512)])

                def rwkv_out_block(t0, hl0):
                    for oc in range(NCC):
                        ps = PB[oc % 4]
                        for ic in range(NCC):
                            p.op("t", lambda e, ps=ps, oc=oc, ic=ic, hl0=hl0: e.matmul(
                                ps[:, :],
                                WSLAB[:, 1024 * ic + 128 * oc:1024 * ic + 128 * (oc + 1)],
                                VH[:, 1024 * ic + hl0:1024 * ic + hl0 + 512],
                                start=(ic == 0), stop=(ic == NCC - 1)),
                                reads=[r(WSLAB, 0, 128, 1024 * ic + 128 * oc,
                                         1024 * ic + 128 * (oc + 1)),
                                       r(VH, 0, 128, 1024 * ic + hl0, 1024 * ic + hl0 + 512)],
                                writes=[rfull(ps)])
                        p.op("v", lambda e, ps=ps, oc=oc, t0=t0: e.tensor_add(
                            X1[:, tcol(oc, t0):tcol(oc, t0) + 512],
                            Z[:, tcol(oc, t0):tcol(oc, t0) + 512], ps[:, :]),
                            reads=[r(Z, 0, 128, tcol(oc, t0), tcol(oc, t0) + 512), rfull(ps)],
                            writes=[r(X1, 0, 128, tcol(oc, t0), tcol(oc, t0) + 512)])
                    ln_channel_major(X1, X1, t0)

                p.op("v", lambda e: e.memset(SA[:, :], 0.0), writes=[rfull(SA)])
                p.op("v", lambda e: e.memset(SB2[:, :], 0.0), writes=[rfull(SB2)])
                for half in range(2):
                    th0 = half * 1024
                    for nm, dstbuf, mixoff in (("wkT", KH, 0), ("wvT", VH, 8)):
                        load_wslab(nm)
                        for tb in range(2):
                            mix_tiles(th0 + tb * 512, mixoff)
                            proj_block(dstbuf, tb * 512)
                    if rwkv_sub == "kv":
                        continue
                    for cc in range(NCC):
                        for q in range(2):
                            wkv_quarter(cc, q * 512)
                    if rwkv_sub == "wkv":
                        if half == 1 and b in dump_batches:
                            p.op("v", lambda e: e.tensor_copy(out=SUMT[:, :], in_=PROW[:, :]),
                                 reads=[rfull(PROW)], writes=[rfull(SUMT)])
                            for nm, bb in (("SCN", SCN), ("EKS", EKS), ("SBB", SBB),
                                           ("SUMT", SUMT), ("XKB", XKB)):
                                if nm in dump_stages:
                                    dump(bb)
                        continue
                    load_wslab("wrT")
                    for tb in range(2):
                        mix_tiles(th0 + tb * 512, 16)
                        rwkv_r_block(tb * 512)
                    if rwkv_sub == "r":
                        continue
                    load_wslab("woT")
                    for tb in range(2):
                        rwkv_out_block(th0 + tb * 512, tb * 512)
                if b in dump_batches:
                    for nm in ("KH", "VH"):
                        if nm in dump_stages:
                            dump(p.bufs[nm])
                p.phase_barrier()


        def ffn_att_batch(b, l, R):
            with ExitStack() as ph:
                sb = lambda name, shape, dt: Buf(
                    ph.enter_context(nc.sbuf_tensor(p.uname(name), list(shape), dt)),
                    name, tuple(shape), dt)
                WQKVB = sb("WQKVB", (128, NCC * 512), BF16)
                WOFr = [sb("WOF", (128, 2 * 512), BF16) for _ in range(2)]
                SHr = [sb("SH", (128, NCC * 129), BF16) for _ in range(2)]
                QT32r = [sb("QT32", (128, 512), F32) for _ in range(2)]
                QKB2r = [sb("QKB2", (128, 512), BF16) for _ in range(2)]
                T3Br = [sb("T3B", (128, 512), BF16)] * 2
                QHATr = [sb("QHAT", (128, 512), F32)] * 2
                QKTE = sb("QTE", (64, 8 * 512), BF16)
                KTE0 = sb("KTE0", (64, 8 * 512), BF16)
                KTE1 = sb("KTE1", (64, 8 * 512), BF16)
                QTEv = (QKTE, 0)
                KTEv = [(KTE0, 0), (KTE1, 0)]
                VE = [sb("VE0", (128, 8 * 4 * 65), BF16), sb("VE1", (128, 8 * 4 * 65), BF16)]
                ATTTr = [sb("ATTT", (128, 512), BF16) for _ in range(4)]
                OT = sb("OT", (128, 4 * 512), BF16)
                RC2B = sb("RC2B", (1, 512), BF16)
                RCBS = sb("RCBS", (128, 512), BF16)
                RC2A = RCBS
                RSQ = sb("RSQ", (128, 20), F32)
                SVCOL = sb("SVCOL", (128, 8), F32)
                CSQr = [sb("CSQ", (128, 256), BF16)] * 2
                SSQr = [sb("SSQ", (128, 256), BF16)] * 2
                CSKr = [sb("CSK", (128, 256), BF16)] * 2
                SSKr = [sb("SSK", (128, 256), BF16)] * 2
                TR1 = QT32r[0]
                TR2 = QT32r[1]
                QSB = sb("QSB", (128, 256), BF16)

                def ldw(nm, dst, nicc, ncols):
                    def fn(e, nm=nm, dst=dst, nicc=nicc, ncols=ncols):
                        src = bass.AP(dt_in[nm], 0,
                                      [[ncols, 128], [128 * ncols, nicc], [1, ncols]])
                        dap = bass.AP(dst.h, 0,
                                      [[dst.shape[1], 128], [ncols, nicc], [1, ncols]])
                        return e.dma_start(out=dap, in_=src)
                    return p.op("g", fn, writes=[rfull(dst)], dma=True)


                def load_wof(qh):
                    WOF = WOFr[qh % 2]

                    def fn(e, qh=qh, WOF=WOF):
                        src = bass.AP(WCAST[f"woffn{l}"][0], qh * 256,
                                      [[1024, 128], [131072, 4], [1, 256]])
                        dap = bass.AP(WOF.h, 0, [[1024, 128], [256, 4], [1, 256]])
                        return e.dma_start(out=dap, in_=src)
                    return p.op("y", fn, writes=[rfull(WOF)], dma=True)

                def load_qkv_slab(cob):
                    def fn(e, cob=cob):
                        src = bass.AP(WCAST[f"wqkv{l}"][0], cob * 512,
                                      [[1536, 128], [128 * 1536, 8], [1, 512]])
                        dap = bass.AP(WQKVB.h, 0, [[NCC * 512, 128], [512, NCC], [1, 512]])
                        return e.dma_start(out=dap, in_=src)
                    return p.op("y", fn, writes=[rfull(WQKVB)], dma=True)
                for i, nm in enumerate((f"qsc{l}", f"ksc{l}")):
                    p.op("y", lambda e, nm=nm: e.dma_start(
                        out=TR2[0:1, 0:64],
                        in_=bass.AP(dt_in[nm], 0, [[64, 1], [1, 64]])),
                        writes=[r(TR2, 0, 1, 0, 64)], dma=True)
                    p.op("t", lambda e: e.matmul(PROW[:, 0:64], ONE1F[:, :], TR2[0:1, 0:64],
                                                 start=True, stop=True),
                         reads=[rfull(ONE1F), r(TR2, 0, 1, 0, 64)],
                         writes=[r(PROW, 0, 128, 0, 64)])
                    p.op("v", lambda e, i=i: e.tensor_scalar_mul(
                        QSB[:, 128 * i:128 * i + 64], PROW[:, 0:64],
                        8.0 if i == 0 else 1.0),
                        reads=[r(PROW, 0, 128, 0, 64)],
                        writes=[r(QSB, 0, 128, 128 * i, 128 * i + 64)])
                    p.op("v", lambda e, i=i: e.tensor_scalar_mul(
                        QSB[:, 128 * i + 64:128 * i + 96], QSB[:, 128 * i + 32:128 * i + 64], -1.0),
                        reads=[r(QSB, 0, 128, 128 * i, 128 * i + 64)],
                        writes=[r(QSB, 0, 128, 128 * i + 64, 128 * i + 96)])
                    p.op("v", lambda e, i=i: e.tensor_copy(
                        out=QSB[:, 128 * i + 96:128 * i + 128], in_=QSB[:, 128 * i:128 * i + 32]),
                        reads=[r(QSB, 0, 128, 128 * i, 128 * i + 32)],
                        writes=[r(QSB, 0, 128, 128 * i + 96, 128 * i + 128)])
                for vb in VE:
                    def vones(e, vb=vb):
                        ap = bass.AP(vb.h, 64, [[2080, 128], [65, 32], [1, 1]])
                        return e.memset(ap, 1.0)
                    p.op("v", vones, writes=[r(vb, 0, 128, 0, vb.shape[1])])

                def emit_tables(w):
                    for tt in range(4):
                        c0t = tt * 64
                        for dst, mast, qs_off in (
                                (CSQr[0], PQC, 0), (SSQr[0], PQS, 64),
                                (CSKr[0], PKC, 128), (SSKr[0], PKS, 192)):
                            def f1(e, dst=dst, mast=mast, qs_off=qs_off, c0t=c0t, tt=tt):
                                i0 = bass.AP(mast.h, 32 * tt, [[128, 128], [0, 2], [1, 32]])
                                i1 = QSB[:, qs_off:qs_off + 64].rearrange(
                                    "p (a f) -> p a f", a=2)
                                o0 = dst[:, c0t:c0t + 64].rearrange("p (a f) -> p a f", a=2)
                                return e.tensor_tensor(out=o0, in0=i0, in1=i1, op=ALU.mult)
                            p.op("g", f1,
                                 reads=[rfull(mast), r(QSB, 0, 128, qs_off, qs_off + 64)],
                                 writes=[r(dst, 0, 128, c0t, c0t + 64)])

                def advance(back):
                    co = 128 if back else 0
                    r3 = lambda buf: bass.AP(buf.h, 0,
                                             [[buf.shape[1], 128], [32, 4], [1, 32]])
                    r128 = lambda buf: r(buf, 0, 128, 0, 128)
                    for pc, ps, cc0, sc0 in ((PQC, PQS, co + 0, co + 32),
                                             (PKC, PKS, co + 64, co + 96)):
                        CA = bass.AP(RSTC.h, cc0, [[256, 128], [0, 4], [1, 32]])
                        SA_ = bass.AP(RSTC.h, sc0, [[256, 128], [0, 4], [1, 32]])
                        steps = (
                            (TR1, pc, CA), (TR2, pc, SA_), (pc, ps, SA_))
                        for dstb, in0b, capb in steps:
                            p.op("g", lambda e, dstb=dstb, in0b=in0b, capb=capb:
                                 e.tensor_tensor(
                                out=r3(dstb), in0=r3(in0b), in1=capb, op=ALU.mult),
                                reads=[r128(in0b), rfull(RSTC)], writes=[r128(dstb)])
                        p.op("g", lambda e, pc=pc: e.tensor_sub(
                            pc[:, 0:128], TR1[:, 0:128], pc[:, 0:128]),
                            reads=[r128(TR1), r128(pc)], writes=[r128(pc)])
                        p.op("g", lambda e, ps=ps, CA=CA: e.tensor_tensor(
                            out=r3(TR1), in0=r3(ps), in1=CA, op=ALU.mult),
                            reads=[r128(ps), rfull(RSTC)], writes=[r128(TR1)])
                        p.op("g", lambda e, ps=ps: e.tensor_add(
                            ps[:, 0:128], TR1[:, 0:128], TR2[:, 0:128]),
                            reads=[r128(TR1), r128(TR2)], writes=[r128(ps)])

                def sh_build(w, tt):
                    SH = SHr[tt % 2]
                    t0 = w * 512 + tt * 128
                    for cc in range(NCC):
                        p.op("s", lambda e, SH=SH, cc=cc, t0=t0: e.copy(
                            out=SH[:, 129 * cc:129 * (cc + 1)],
                            in_=R[:, tcol(cc, t0) - 1:tcol(cc, t0) + 128]),
                            reads=[r(R, 0, 128, tcol(cc, t0) - 1, tcol(cc, t0) + 128)],
                            writes=[r(SH, 0, 128, 129 * cc, 129 * (cc + 1))])
                    if l == 1 and w > 0 and tt == 0:
                        def fix(e, SH=SH):
                            dst = bass.AP(SH.h, 0, [[NCC * 129, 128], [129, 8], [1, 1]])
                            return e.tensor_copy(out=dst, in_=SVCOL[:, :])
                        p.op("v", fix, reads=[rfull(SVCOL)],
                             writes=[r(SH, 0, 128, 0, SH.shape[1])])

                def lhs_sh(cc, tt):
                    SH = SHr[tt % 2]
                    off = 129 * cc
                    if cc < 4:
                        return SH[:, off + 1:off + 129], r(SH, 0, 128, off + 1, off + 129)
                    return SH[:, off:off + 128], r(SH, 0, 128, off, off + 128)

                def qkv_cob(w, cob, tt):
                    ps = PB[tt % 4]
                    for cc in range(NCC):
                        ap, rd = lhs_sh(cc, tt)
                        p.op("t", lambda e, ps=ps, ap=ap, cc=cc: e.matmul(
                            ps[:, :], ap, WQKVB[:, 512 * cc:512 * (cc + 1)],
                            start=(cc == 0), stop=(cc == NCC - 1)),
                            reads=[rd, r(WQKVB, 0, 128, 512 * cc, 512 * (cc + 1))],
                            writes=[rfull(ps)])
                    return ps

                def v_evac(w, tt, ps):
                    wb = w % 2

                    def vev(e, wb=wb, tt=tt, ps=ps):
                        dst = bass.AP(VE[wb].h, tt * 65, [[2080, 128], [260, 8], [1, 64]])
                        return e.copy(out=dst, in_=ps[:, :])
                    p.op("s", vev, reads=[rfull(ps)],
                         writes=[r(VE[wb], 0, 128, 0, VE[wb].shape[1])])

                def qk_process(w, iq, tt, ps):
                    wb = w % 2
                    QT32 = QT32r[tt % 2]
                    QKB2 = QKB2r[tt % 2]
                    T3B = T3Br[tt % 2]
                    QHAT = QHATr[tt % 2]
                    wr2 = w % 2
                    CST, SST = (CSQr[wr2], SSQr[wr2]) if iq == 0 else (CSKr[wr2], SSKr[wr2])
                    p.op("s", lambda e, QHAT=QHAT, ps=ps: e.activation(out=QHAT[:, :], in_=ps[:, :],
                                                     func=AF.Square),
                         reads=[rfull(ps)], writes=[rfull(QHAT)])
                    p.op("v", lambda e, QHAT=QHAT: e.tensor_reduce(
                        RSQ[:, 0:8], QHAT.ap().rearrange("p (h d) -> p h d", h=8),
                        axis=AX.X, op=ALU.add),
                        reads=[rfull(QHAT)], writes=[r(RSQ, 0, 128, 0, 8)])
                    p.op("v", lambda e: e.tensor_scalar_max(RSQ[:, 8:16], RSQ[:, 0:8], 1e-24),
                         reads=[r(RSQ, 0, 128, 0, 8)], writes=[r(RSQ, 0, 128, 8, 16)])
                    p.op("s", lambda e: e.activation(out=RSQ[:, 8:16], in_=RSQ[:, 8:16],
                                                     func=AF.Ln),
                         reads=[r(RSQ, 0, 128, 8, 16)], writes=[r(RSQ, 0, 128, 8, 16)])
                    p.op("s", lambda e: e.activation(out=RSQ[:, 8:16], in_=RSQ[:, 8:16],
                                                     func=AF.Exp, scale=-0.5),
                         reads=[r(RSQ, 0, 128, 8, 16)], writes=[r(RSQ, 0, 128, 8, 16)])

                    def qhat(e, QHAT=QHAT, ps=ps):
                        i1 = bass.AP(RSQ.h, 8, [[20, 128], [1, 8], [0, 64]])
                        return e.tensor_tensor(out=QHAT[:, :], in0=ps[:, :], in1=i1,
                                               op=ALU.mult)
                    p.op("v", qhat, reads=[rfull(ps), r(RSQ, 0, 128, 8, 16)],
                         writes=[rfull(QHAT)])

                    def t2f(e, CST=CST, tt=tt, QT32=QT32, QHAT=QHAT):
                        i1 = bass.AP(CST.h, tt * 64, [[256, 128], [0, 8], [1, 64]])
                        return e.tensor_tensor(out=QT32[:, :], in0=QHAT[:, :], in1=i1,
                                               op=ALU.mult)
                    p.op("v", t2f, reads=[rfull(QHAT), rfull(CST)], writes=[rfull(QT32)])

                    def t3a(e, SST=SST, tt=tt, QHAT=QHAT, T3B=T3B):
                        i0 = bass.AP(QHAT.h, 32, [[512, 128], [64, 8], [1, 32]])
                        i1 = bass.AP(SST.h, tt * 64, [[256, 128], [0, 8], [1, 32]])
                        o0 = bass.AP(T3B.h, 0, [[512, 128], [64, 8], [1, 32]])
                        return e.tensor_tensor(out=o0, in0=i0, in1=i1, op=ALU.mult)
                    p.op("g", t3a, reads=[rfull(QHAT), rfull(SST)],
                         writes=[r(T3B, 0, 128, 0, 512)])

                    def t3b(e, SST=SST, tt=tt, QHAT=QHAT, T3B=T3B):
                        i0 = bass.AP(QHAT.h, 0, [[512, 128], [64, 8], [1, 32]])
                        i1 = bass.AP(SST.h, tt * 64 + 32, [[256, 128], [0, 8], [1, 32]])
                        o0 = bass.AP(T3B.h, 32, [[512, 128], [64, 8], [1, 32]])
                        return e.tensor_tensor(out=o0, in0=i0, in1=i1, op=ALU.mult)
                    p.op("g", t3b, reads=[rfull(QHAT), rfull(SST)],
                         writes=[r(T3B, 0, 128, 0, 512)])
                    p.op("g", lambda e, QKB2=QKB2, QT32=QT32, T3B=T3B: e.tensor_add(QKB2[:, :], QT32[:, :], T3B[:, 0:512]),
                         reads=[rfull(QT32), r(T3B, 0, 128, 0, 512)], writes=[rfull(QKB2)])
                    for hp in range(4):
                        p.op("t", lambda e, hp=hp, QKB2=QKB2: e.transpose(
                            PTRB[:, 128 * hp:128 * (hp + 1)], QKB2[:, 128 * hp:128 * (hp + 1)],
                            IDNB[:, :]),
                            reads=[r(QKB2, 0, 128, 128 * hp, 128 * (hp + 1)), rfull(IDNB)],
                            writes=[r(PTRB, 0, 128, 128 * hp, 128 * (hp + 1))])
                    dbuf, dp0 = QTEv if iq == 0 else KTEv[wb]
                    for par in range(2):
                        def tev(e, dbuf=dbuf, dp0=dp0, par=par, tt=tt):
                            dst = bass.AP(dbuf.h, dp0 * 4096 + 512 * par + tt * 128,
                                          [[8 * 512, 64], [1024, 4], [1, 128]])
                            src = bass.AP(PTRB.h, 64 * par * 512,
                                          [[512, 64], [128, 4], [1, 128]])
                            return e.tensor_copy(out=dst, in_=src)
                        p.op("v", tev, reads=[rfull(PTRB)],
                             writes=[r(dbuf, dp0, dp0 + 64, 0, 4096)])

                def head_attention(w, h, pav):
                    wb, pb = w % 2, (w - 1) % 2
                    jts = ([("p", jl) for jl in range(4)] if w > 0 else []) +                           [("c", jl) for jl in range(4)]
                    njts = len(jts)
                    units = []
                    for jidx, (kind, jl) in enumerate(jts):
                        ATTT = ATTTr[jidx % 4]
                        PP2 = (PTRN, PB[1])[jidx % 2]
                        if kind == "p":
                            kbuf, kp0 = KTEv[pb]
                            i0, i1 = 0, 128 * (jl + 1)
                            bnd0, bmask = 128 * jl, 128
                            vb = VE[pb]
                        else:
                            kbuf, kp0 = KTEv[wb]
                            i0, i1 = 128 * jl, 512
                            bnd0, bmask = 0, 0
                            vb = VE[wb]
                        units.append((jidx, jl, ATTT, PP2, kbuf, kp0, i0, i1,
                                      bnd0, bmask, vb))

                    def emit_sim(u):
                        jidx, jl, ATTT, PP2, kbuf, kp0, i0, i1, bnd0, bmask, vb = u
                        span = i1 - i0
                        p.op("t", lambda e, kbuf=kbuf, kp0=kp0, jl=jl, h=h, i0=i0, i1=i1,
                             PP2=PP2: e.matmul(
                            PP2[:, i0:i1],
                            kbuf[kp0:kp0 + 64, 512 * h + 128 * jl:512 * h + 128 * (jl + 1)],
                            QKTE[0:64, 512 * h + i0:512 * h + i1], start=True, stop=True),
                            reads=[r(kbuf, kp0, kp0 + 64, 512 * h + 128 * jl, 512 * h + 128 * (jl + 1)),
                                   r(QKTE, 0, 64, 512 * h + i0, 512 * h + i1)],
                            writes=[rfull(PP2)])
                        p.op("s", lambda e, i0=i0, span=span, PP2=PP2, ATTT=ATTT: e.activation(
                            out=ATTT[:, 0:span], in_=PP2[:, i0:i0 + span], func=AF.Exp),
                            reads=[rfull(PP2)], writes=[r(ATTT, 0, 128, 0, span)])
                        p.op("v", lambda e, bnd0=bnd0, bmask=bmask, ATTT=ATTT: e.tensor_mul(
                            ATTT[:, bnd0:bnd0 + 128], ATTT[:, bnd0:bnd0 + 128],
                            MCAT[:, bmask:bmask + 128]),
                            reads=[r(ATTT, 0, 128, bnd0, bnd0 + 128),
                                   r(MCAT, 0, 128, bmask, bmask + 128)],
                            writes=[r(ATTT, 0, 128, bnd0, bnd0 + 128)])

                    def emit_av(u):
                        jidx, jl, ATTT, PP2, kbuf, kp0, i0, i1, bnd0, bmask, vb = u
                        span = i1 - i0
                        p.op("t", lambda e, vb=vb, h=h, jl=jl, i0=i0, span=span,
                             first=(jidx == 0), last=(jidx == njts - 1), pav=pav,
                             ATTT=ATTT: e.matmul(
                            pav[0:65, i0:i0 + span], vb[:, 260 * h + 65 * jl:260 * h + 65 * (jl + 1)],
                            ATTT[:, 0:span], start=first, stop=last),
                            reads=[r(vb, 0, 128, 260 * h + 65 * jl, 260 * h + 65 * (jl + 1)),
                                   r(ATTT, 0, 128, 0, span)],
                            writes=[rfull(pav)])

                    for k in range(njts + 2):
                        if k < njts:
                            emit_sim(units[k])
                        if k >= 2:
                            emit_av(units[k - 2])

                def pair_out(w, hp, pava, pavb):
                    for pav, rcdst in ((pava, RC2A), (pavb, RC2B)):
                        p.op("s", lambda e, pav=pav, rcdst=rcdst: e.activation(
                            out=rcdst[0:1, :], in_=pav[64:65, :], func=AF.Ln),
                            reads=[rfull(pav)], writes=[rfull(rcdst)])
                        p.op("s", lambda e, rcdst=rcdst: e.activation(
                            out=rcdst[0:1, :], in_=rcdst[0:1, :], func=AF.Exp, scale=-1.0),
                            reads=[rfull(rcdst)], writes=[rfull(rcdst)])
                    p.op("t", lambda e: e.matmul(PROW[:, :], SEL2A[:, :], RC2A[0:1, :],
                                                 start=True, stop=False),
                         reads=[rfull(SEL2A), rfull(RC2A)], writes=[rfull(PROW)])
                    p.op("t", lambda e: e.matmul(PROW[:, :], SEL2B[:, :], RC2B[0:1, :],
                                                 start=False, stop=True),
                         reads=[rfull(SEL2B), rfull(RC2B)], writes=[rfull(PROW)])
                    p.op("v", lambda e: e.tensor_copy(out=RCBS[:, :], in_=PROW[:, :]),
                         reads=[rfull(PROW)], writes=[rfull(RCBS)])
                    for par, pav in ((0, pava), (1, pavb)):
                        p.op("v", lambda e, par=par, pav=pav, hp=hp: e.tensor_mul(
                            OT[64 * par:64 * (par + 1), 512 * hp:512 * (hp + 1)],
                            pav[0:64, :], RCBS[64 * par:64 * (par + 1), :]),
                            reads=[rfull(pav), r(RCBS, 64 * par, 64 * (par + 1), 0, 512)],
                            writes=[r(OT, 64 * par, 64 * (par + 1), 512 * hp, 512 * (hp + 1))])

                def wo_out(w):
                    t0 = w * 512
                    if l == 1:
                        def sv(e, t0=t0):
                            src = bass.AP(Z.h, tcol(0, t0 + 511), [[NCC * TC, 128], [TC, 8], [1, 1]])
                            return e.tensor_copy(out=SVCOL[:, :], in_=src)
                        p.op("v", sv, reads=[r(Z, 0, 128, 0, Z.shape[1])], writes=[rfull(SVCOL)])
                    for qh in range(4):
                        WOF = WOFr[qh % 2]
                        for j2 in range(2):
                            oc = 2 * qh + j2
                            ps = PB[oc % 4]
                            for icc in range(4):
                                p.op("t", lambda e, ps=ps, WOF=WOF, j2=j2, icc=icc: e.matmul(
                                    ps[:, :],
                                    WOF[:, 256 * icc + 128 * j2:256 * icc + 128 * (j2 + 1)],
                                    OT[:, 512 * icc:512 * (icc + 1)],
                                    start=(icc == 0), stop=(icc == 3)),
                                    reads=[r(WOF, 0, 128, 256 * icc + 128 * j2,
                                             256 * icc + 128 * (j2 + 1)),
                                           r(OT, 0, 128, 512 * icc, 512 * (icc + 1))],
                                    writes=[rfull(ps)])
                            p.op("v", lambda e, ps=ps, oc=oc, t0=t0: e.tensor_add(
                                Z[:, tcol(oc, t0):tcol(oc, t0) + 512],
                                R[:, tcol(oc, t0):tcol(oc, t0) + 512], ps[:, :]),
                                reads=[r(R, 0, 128, tcol(oc, t0), tcol(oc, t0) + 512), rfull(ps)],
                                writes=[r(Z, 0, 128, tcol(oc, t0), tcol(oc, t0) + 512)])
                        if qh + 2 < 4:
                            load_wof(qh + 2)

                for w in range(NW):
                    emit_tables(w)
                    advance(back=(w == NW - 1))
                    for cob in range(3):
                        if not (cob == 0 and w > 0):
                            load_qkv_slab(cob)
                        for tt in range(4):
                            sh_build(w, tt)
                            ps = qkv_cob(w, cob, tt)
                            if cob == 2:
                                v_evac(w, tt, ps)
                            else:
                                qk_process(w, cob, tt, ps)
                    load_wof(0)
                    load_wof(1)
                    for hp in range(4):
                        pava, pavb = ((PAV, PB[0]), (PB[2], PB[3]))[hp % 2]
                        for par in range(2):
                            h = 2 * hp + par
                            head_attention(w, h, pava if par == 0 else pavb)
                        pair_out(w, hp, pava, pavb)
                    if w + 1 < NW:
                        load_qkv_slab(0)
                    wo_out(w)
                if b in dump_batches:
                    dd = {"QTE": QKTE, "KTE0": KTE0, "VE0": VE[0], "OT": OT,
                          "CSQ": CSQr[0], "SSQ": SSQr[0], "CSK": CSKr[0], "SSK": SSKr[0]}
                    for nm in dd:
                        if nm in dump_stages:
                            dump(dd[nm])
                p.phase_barrier()


        HGROUPS = [5, 5, 4, 4, 4]

        def ffn_mlp_batch(b, l):
            with ExitStack() as ph:
                sb = lambda name, shape, dt: Buf(
                    ph.enter_context(nc.sbuf_tensor(p.uname(name), list(shape), dt)),
                    name, tuple(shape), dt)
                LNZ = sb("LNZ", (128, NCC * 2048), BF16)
                MGELr = [sb("MGEL", (128, 512), F32)] * 2
                p.bufs["LNROWV"] = ((SUMT, 0, 0), (MGELr[0], 0, 0))
                WA = sb("WA", (128, 5 * 1024), BF16)
                WG = sb("WG", (128, 5 * 1024), BF16)
                W2P = sb("W2P", (128, 5 * 1024), BF16)
                HTG = sb("HTG", (128, 5 * 512), BF16)
                LNGC = sb("LNGC", (128, 16), F32)
                B1AC = sb("B1AC", (128, NHC), F32)
                B1GC = sb("B1GC", (128, NHC), F32)
                B2C = sb("B2C", (128, 8), F32)

                def cl(dstbuf, c0, nm, ncols):
                    def fn(e, dstbuf=dstbuf, c0=c0, nm=nm, ncols=ncols):
                        src = bass.AP(dt_in[nm], 0, [[1, 128], [128, ncols]])
                        return e.dma_start(out=dstbuf[:, c0:c0 + ncols], in_=src,
                                           allow_slow_non_contiguous=True)
                    p.op_slow_dma("g", fn, writes=[r(dstbuf, 0, 128, c0, c0 + ncols)])
                cl(LNGC, 0, f"lng{l}", 8)
                cl(LNGC, 8, f"lnb{l}", 8)
                cl(B1AC, 0, f"b1a{l}", NHC)
                cl(B1GC, 0, f"b1g{l}", NHC)
                cl(B2C, 0, f"b2_{l}", 8)

                def zsh(cc, t0, n=512):
                    base = tcol(cc, t0) - (1 if cc >= 4 else 0)
                    return Z[:, base:base + n], r(Z, 0, 128, base, base + n)

                for tb in range(4):
                    t0 = tb * 512
                    for cc in range(NCC):
                        ap, rd = zsh(cc, t0)
                        cb, cb0 = (STATB, 0) if cc % 2 == 0 else (HTG, 512)
                        qb, qb0 = (HTG, 0) if cc % 2 == 0 else (HTG, 1024)
                        p.op("s", lambda e, ap=ap, cb=cb, cb0=cb0: e.copy(
                            out=cb[:, cb0:cb0 + 512], in_=ap),
                            reads=[rd], writes=[r(cb, 0, 128, cb0, cb0 + 512)])
                        p.op("t", lambda e, cc=cc, cb=cb, cb0=cb0: e.matmul(
                            PROW[0:1, :], ONE1B[:, :], cb[:, cb0:cb0 + 512],
                            start=(cc == 0), stop=(cc == NCC - 1)),
                            reads=[rfull(ONE1B), r(cb, 0, 128, cb0, cb0 + 512)],
                            writes=[r(PROW, 0, 1, 0, 512)])
                        p.op("s", lambda e, ap=ap, qb=qb, qb0=qb0: e.activation(
                            out=qb[:, qb0:qb0 + 512], in_=ap, func=AF.Square),
                            reads=[rd], writes=[r(qb, 0, 128, qb0, qb0 + 512)])
                        p.op("t", lambda e, cc=cc, qb=qb, qb0=qb0: e.matmul(
                            PAV[0:1, :], ONE1B[:, :], qb[:, qb0:qb0 + 512],
                            start=(cc == 0), stop=(cc == NCC - 1)),
                            reads=[rfull(ONE1B), r(qb, 0, 128, qb0, qb0 + 512)],
                            writes=[r(PAV, 0, 1, 0, 512)])
                    ln_rows_mid()
                    for cc in range(NCC):
                        ap, rd = zsh(cc, t0)
                        TB1 = SUMT
                        p.op("v", lambda e, ap=ap, TB1=TB1: e.tensor_mul(TB1[:, :], ap, PROW[:, :]),
                             reads=[rd, rfull(PROW)], writes=[rfull(TB1)])
                        def b2f(e, cc=cc):
                            i1 = LNGC[:, 8 + cc:9 + cc].broadcast_to([128, 512])
                            return e.scalar_tensor_tensor(
                                out=STATB[:, 0:512], in0=PTRN[:, :],
                                scalar=LNGC[:, cc:cc + 1], in1=i1,
                                op0=ALU.mult, op1=ALU.add)
                        p.op("v", b2f, reads=[rfull(PTRN), r(LNGC, 0, 128, 0, 16)],
                             writes=[r(STATB, 0, 128, 0, 512)])
                        p.op("v", lambda e, cc=cc, t0=t0, TB1=TB1: e.scalar_tensor_tensor(
                            out=LNZ[:, 2048 * cc + t0:2048 * cc + t0 + 512], in0=TB1[:, :],
                            scalar=LNGC[:, cc:cc + 1], in1=STATB[:, 0:512],
                            op0=ALU.mult, op1=ALU.add),
                            reads=[rfull(TB1), r(LNGC, 0, 128, 0, 8),
                                   r(STATB, 0, 128, 0, 512)],
                            writes=[r(LNZ, 0, 128, 2048 * cc + t0, 2048 * cc + t0 + 512)])

                hid0 = 0
                for gi, gsz in enumerate(HGROUPS):
                    for wsl, nm in ((WA, f"wa{l}"), (WG, f"wg{l}")):
                        for cc in range(NCC):
                            def ldp(e, wsl=wsl, nm=nm, hid0=hid0, gsz=gsz, cc=cc):
                                src = bass.AP(WCAST[nm][0], hid0 * 128 + cc * 128 * HIDP,
                                              [[HIDP, 128], [128, gsz], [1, 128]])
                                dst = bass.AP(wsl.h, cc * 128,
                                              [[wsl.shape[1], 128], [1024, gsz], [1, 128]])
                                return e.dma_start(out=dst, in_=src)
                            p.op("y", ldp, writes=[rfull(wsl)], dma=True)
                    def ldw2(e, hid0=hid0, gsz=gsz):
                        src = bass.AP(WCAST[f"w2_{l}"][0], hid0 * 128 * 1024,
                                      [[1024, 128], [128 * 1024, gsz], [1, 1024]])
                        dst = bass.AP(W2P.h, 0,
                                      [[W2P.shape[1], 128], [1024, gsz], [1, 1024]])
                        return e.dma_start(out=dst, in_=src)
                    p.op("y", ldw2, writes=[rfull(W2P)], dma=True)
                    for tb in range(4):
                        t0 = tb * 512
                        for j in range(gsz):
                            hidc = hid0 + j
                            MGEL = MGELr[j % 2]
                            psa = (PB[0], PB[2])[j % 2]
                            psg = (PB[1], PB[3])[j % 2]
                            for ps, wsl in ((psa, WA), (psg, WG)):
                                for cc in range(NCC):
                                    p.op("t", lambda e, ps=ps, wsl=wsl, j=j, cc=cc, t0=t0: e.matmul(
                                        ps[:, :], wsl[:, 1024 * j + 128 * cc:1024 * j + 128 * (cc + 1)],
                                        LNZ[:, 2048 * cc + t0:2048 * cc + t0 + 512],
                                        start=(cc == 0), stop=(cc == NCC - 1)),
                                        reads=[r(wsl, 0, 128, 1024 * j + 128 * cc,
                                                 1024 * j + 128 * (cc + 1)),
                                               r(LNZ, 0, 128, 2048 * cc + t0, 2048 * cc + t0 + 512)],
                                        writes=[rfull(ps)])
                            p.op("s", lambda e, hidc=hidc, MGEL=MGEL, psg=psg: e.activation(
                                out=MGEL[:, :], in_=psg[:, :], func=AF.Gelu,
                                bias=B1GC[:, hidc:hidc + 1]),
                                reads=[rfull(psg), r(B1GC, 0, 128, hidc, hidc + 1)],
                                writes=[rfull(MGEL)])
                            p.op("v", lambda e, hidc=hidc, j=j, MGEL=MGEL, psa=psa:
                                 e.scalar_tensor_tensor(
                                out=HTG[:, 512 * j:512 * (j + 1)], in0=psa[:, :],
                                scalar=B1AC[:, hidc:hidc + 1], in1=MGEL[:, :],
                                op0=ALU.add, op1=ALU.mult),
                                reads=[rfull(psa), r(B1AC, 0, 128, hidc, hidc + 1),
                                       rfull(MGEL)],
                                writes=[r(HTG, 0, 128, 512 * j, 512 * (j + 1))])
                        for och in range(2):
                            for oci in range(4):
                                oc = och * 4 + oci
                                ps = (PB[0], PB[1], PTRN, PB[2])[oci]
                                for j in range(gsz):
                                    p.op("t", lambda e, ps=ps, j=j, oc=oc: e.matmul(
                                        ps[:, :], W2P[:, 1024 * j + 128 * oc:1024 * j + 128 * (oc + 1)],
                                        HTG[:, 512 * j:512 * (j + 1)],
                                        start=(j == 0), stop=(j == gsz - 1)),
                                        reads=[r(W2P, 0, 128, 1024 * j + 128 * oc,
                                                 1024 * j + 128 * (oc + 1)),
                                               r(HTG, 0, 128, 512 * j, 512 * (j + 1))],
                                        writes=[rfull(ps)])
                                if gi == 0:
                                    p.op("v", lambda e, ps=ps, oc=oc, t0=t0: e.scalar_tensor_tensor(
                                        out=Z[:, tcol(oc, t0):tcol(oc, t0) + 512], in0=ps[:, :],
                                        scalar=B2C[:, oc:oc + 1],
                                        in1=Z[:, tcol(oc, t0):tcol(oc, t0) + 512],
                                        op0=ALU.add, op1=ALU.add),
                                        reads=[rfull(ps), r(B2C, 0, 128, oc, oc + 1),
                                               r(Z, 0, 128, tcol(oc, t0), tcol(oc, t0) + 512)],
                                        writes=[r(Z, 0, 128, tcol(oc, t0), tcol(oc, t0) + 512)])
                                else:
                                    p.op("v", lambda e, ps=ps, oc=oc, t0=t0: e.tensor_add(
                                        Z[:, tcol(oc, t0):tcol(oc, t0) + 512],
                                        Z[:, tcol(oc, t0):tcol(oc, t0) + 512], ps[:, :]),
                                        reads=[rfull(ps), r(Z, 0, 128, tcol(oc, t0),
                                                            tcol(oc, t0) + 512)],
                                        writes=[r(Z, 0, 128, tcol(oc, t0), tcol(oc, t0) + 512)])
                    hid0 += gsz
                if b in dump_batches and "LNZ" in dump_stages:
                    dump(LNZ)
                p.phase_barrier()

        def final_ln_store(b):
          with ExitStack() as ph:
            XTM = Buf(ph.enter_context(nc.sbuf_tensor(p.uname("XTM"), [128, 1024], F32)),
                      "XTM", (128, 1024), F32)
            for tt in range(T // 128):
                t0 = tt * 128
                for half in range(2):
                    for k in range(4):
                        cc = half * 4 + k
                        p.op("v", lambda e, cc=cc, k=k, t0=t0: e.tensor_add(
                            SUMT[:, 128 * k:128 * (k + 1)],
                            X1[:, tcol(cc, t0):tcol(cc, t0) + 128],
                            Z[:, tcol(cc, t0):tcol(cc, t0) + 128]),
                            reads=[r(X1, 0, 128, tcol(cc, t0), tcol(cc, t0) + 128),
                                   r(Z, 0, 128, tcol(cc, t0), tcol(cc, t0) + 128)],
                            writes=[r(SUMT, 0, 128, 128 * k, 128 * (k + 1))])
                    for k in range(4):
                        p.op("t", lambda e, k=k: e.transpose(
                            PTRN[:, 128 * k:128 * (k + 1)], SUMT[:, 128 * k:128 * (k + 1)],
                            IDNF[:, :]),
                            reads=[r(SUMT, 0, 128, 128 * k, 128 * (k + 1)), rfull(IDNF)],
                            writes=[r(PTRN, 0, 128, 128 * k, 128 * (k + 1))])
                    p.op("s", lambda e, half=half: e.copy(
                        out=XTM[:, 512 * half:512 * (half + 1)], in_=PTRN[:, :]),
                        reads=[rfull(PTRN)],
                        writes=[r(XTM, 0, 128, 512 * half, 512 * (half + 1))])
                p.op("v", lambda e: e.tensor_reduce(MUC[:, 0:1], XTM[:, :], axis=AX.X,
                                                    op=ALU.add),
                     reads=[rfull(XTM)], writes=[r(MUC, 0, 128, 0, 1)])
                p.op("s", lambda e, XTM=XTM: e.activation(
                    out=STATB[:, :], in_=XTM[:, 0:512], func=AF.Square,
                    accum_out=MUC[:, 1:2]),
                    reads=[rfull(XTM)], writes=[rfull(STATB), r(MUC, 0, 128, 1, 2)])
                p.op("s", lambda e, XTM=XTM: e.activation(
                    out=STATB[:, :], in_=XTM[:, 512:1024], func=AF.Square,
                    accum_out=MUC[:, 3:4]),
                    reads=[rfull(XTM)], writes=[rfull(STATB), r(MUC, 0, 128, 3, 4)])
                p.op("v", lambda e: e.tensor_add(MUC[:, 1:2], MUC[:, 1:2], MUC[:, 3:4]),
                     reads=[r(MUC, 0, 128, 1, 4)], writes=[r(MUC, 0, 128, 1, 2)])
                p.op("v", lambda e: e.tensor_scalar_mul(MUC[:, 0:1], MUC[:, 0:1], 1.0 / C),
                     reads=[r(MUC, 0, 128, 0, 1)], writes=[r(MUC, 0, 128, 0, 1)])
                p.op("v", lambda e: e.tensor_scalar_mul(MUC[:, 1:2], MUC[:, 1:2], 1.0 / C),
                     reads=[r(MUC, 0, 128, 1, 2)], writes=[r(MUC, 0, 128, 1, 2)])
                p.op("v", lambda e: e.tensor_mul(MUC[:, 2:3], MUC[:, 0:1], MUC[:, 0:1]),
                     reads=[r(MUC, 0, 128, 0, 1)], writes=[r(MUC, 0, 128, 2, 3)])
                p.op("v", lambda e: e.tensor_sub(MUC[:, 1:2], MUC[:, 1:2], MUC[:, 2:3]),
                     reads=[r(MUC, 0, 128, 1, 3)], writes=[r(MUC, 0, 128, 1, 2)])
                p.op("s", lambda e: e.activation(out=MUC[:, 1:2], in_=MUC[:, 1:2], func=AF.Ln,
                                                 bias=KEPS[:, 0:1]),
                     reads=[r(MUC, 0, 128, 1, 2), rfull(KEPS)], writes=[r(MUC, 0, 128, 1, 2)])
                p.op("s", lambda e: e.activation(out=MUC[:, 1:2], in_=MUC[:, 1:2], func=AF.Exp,
                                                 scale=-0.5),
                     reads=[r(MUC, 0, 128, 1, 2)], writes=[r(MUC, 0, 128, 1, 2)])
                p.op("v", lambda e: e.tensor_scalar(
                    out=XTM[:, :], in0=XTM[:, :], scalar1=MUC[:, 0:1], scalar2=MUC[:, 1:2],
                    op0=ALU.subtract, op1=ALU.mult),
                    reads=[rfull(XTM), r(MUC, 0, 128, 0, 2)], writes=[rfull(XTM)])
                tok = p.op("y", lambda e, b=b, t0=t0: e.dma_start(
                    out=out_t.ap()[b * T + t0:b * T + t0 + 128, :], in_=XTM[:, :]),
                    reads=[rfull(XTM)], dma=True)
                p.out_dma_tokens.append(tok)
            p.phase_barrier()

        dump_batches = {nbatch - 1}
        dump_stages = set(dumps)
        init_consts()
        init_rotary()
        init_wcast()
        for b in range(nbatch):
            load_input_batch(b)
            if "rwkv" in stages:
                rwkv_batch(b)
            else:
                for cc in range(NCC):
                    p.op("v", lambda e, cc=cc: e.tensor_copy(
                        out=X1[:, cc * TC:(cc + 1) * TC], in_=Z[:, cc * TC:(cc + 1) * TC]),
                        reads=[r(Z, 0, 128, cc * TC, (cc + 1) * TC)],
                        writes=[r(X1, 0, 128, cc * TC, (cc + 1) * TC)])
            for l in range(2):
                if f"att{l}" in stages or "att" in stages:
                    ffn_att_batch(b, l, X1 if l == 0 else Z)
                if f"mlp{l}" in stages or "mlp" in stages:
                    ffn_mlp_batch(b, l)
            final_ln_store(b)
        for nm in dump_stages:
            if nm in ("KH", "VH", "WSLAB", "XKB", "SUMT", "SCN", "EKS", "SBB"):
                continue
            if nm in p.bufs:
                dump(p.bufs[nm])

        p.wait_tokens("y", p.out_dma_tokens + dump_toks)

        block = ctx.enter_context(nc.Block())
        p.finalize_into_block(block)

    return nc


def _prep_weights(inputs):
    f = np.ascontiguousarray
    w = {}
    w["wkT"] = f(inputs["L0_attn_Wk"].T)
    w["wvT"] = f(inputs["L0_attn_Wv"].T)
    w["wrT"] = f(inputs["L0_attn_Wr"].T)
    w["woT"] = f(inputs["L0_attn_Wo"].T)
    w["mixk"] = f(inputs["L0_attn_mix_k"].reshape(-1))
    w["mixv"] = f(inputs["L0_attn_mix_v"].reshape(-1))
    w["mixr"] = f(inputs["L0_attn_mix_r"].reshape(-1))
    w["tdec"] = f(inputs["L0_attn_time_decay"])
    w["tfirst"] = f(inputs["L0_attn_time_first"])
    pad = HIDP - HID
    for l in range(2):
        w[f"wqkv{l}"] = f(inputs["L0_ffn_wqkv"][l])
        w[f"qsc{l}"] = f(inputs["L0_ffn_q_scale"][l])
        w[f"ksc{l}"] = f(inputs["L0_ffn_k_scale"][l])
        w[f"woffn{l}"] = f(inputs["L0_ffn_wo"][l])
        w[f"lng{l}"] = f(inputs["L0_ffn_ln_g"][l])
        w[f"lnb{l}"] = f(inputs["L0_ffn_ln_b"][l])
        w1 = inputs["L0_ffn_w1"][l]
        b1 = inputs["L0_ffn_b1"][l]
        w[f"wa{l}"] = np.pad(w1[:, :HID], ((0, 0), (0, pad)))
        w[f"wg{l}"] = np.pad(w1[:, HID:], ((0, 0), (0, pad)))
        w[f"b1a{l}"] = np.pad(b1[:HID], (0, pad))
        w[f"b1g{l}"] = np.pad(b1[HID:], (0, pad))
        w[f"w2_{l}"] = np.pad(inputs["L0_ffn_w2"][l], ((0, pad), (0, 0)))
        w[f"b2_{l}"] = f(inputs["L0_ffn_b2"][l])
    return {k: np.ascontiguousarray(np.asarray(v, dtype=np.float32)) for k, v in w.items()}


def kernel(**inputs):
    x = np.asarray(inputs["L0_attn_x"], dtype=np.float32)
    w = _prep_weights({k: np.asarray(v, dtype=np.float32) for k, v in inputs.items()})
    nc = build_program()
    in_maps = []
    for c in range(NCORES):
        m = dict(w)
        m["x"] = np.ascontiguousarray(x[c * B_LOC:(c + 1) * B_LOC].reshape(B_LOC * T, C))
        in_maps.append(m)
    res = run_bass_kernel_spmd(nc, in_maps, core_ids=list(range(NCORES)))
    out = np.concatenate([res.results[c]["out"].reshape(B_LOC, T, C) for c in range(NCORES)],
                         axis=0)
    return out


# version 38
# speedup vs baseline: 1.0935x; 1.0024x over previous
import numpy as np
import ml_dtypes
from contextlib import ExitStack

import concourse.bass as bass
import concourse.mybir as mybir
from concourse.bass_utils import run_bass_kernel_spmd

F32 = mybir.dt.float32
BF16 = mybir.dt.bfloat16
AF = mybir.ActivationFunctionType
ALU = mybir.AluOpType
AX = mybir.AxisListType

NCORES = 8
B_LOC = 4
T = 2048
C = 1024
NCC = 8
TC = T + 1
H_ATT = 8
D_ATT = 64
INNER = 512
WINDOW = 512
NW = T // WINDOW
HID = 2730
HIDP = 2816
NHC = HIDP // 128
QK_SCALE = 8.0
LN_EPS = 1e-5

ENG = ("t", "v", "s", "g", "y")
ENG_FULL = {"t": "tensor", "v": "vector", "s": "scalar", "g": "gpsimd", "y": "sync"}
SELF_WAIT = {"t": False, "v": True, "s": True, "g": True, "y": False}


class _Op:
    __slots__ = ("fn", "waits", "inc", "need_inc", "cumval")

    def __init__(self, fn):
        self.fn = fn
        self.waits = []
        self.inc = None
        self.need_inc = False
        self.cumval = None


class Buf:
    __slots__ = ("h", "name", "shape", "dtype")

    def __init__(self, h, name, shape, dtype):
        self.h = h
        self.name = name
        self.shape = shape
        self.dtype = dtype

    def __getitem__(self, key):
        return self.h.ap()[key]

    def ap(self):
        return self.h.ap()


class Prog:
    def __init__(self, nc, ctx):
        self.nc = nc
        self.ctx = ctx
        self.ops = {e: [] for e in ENG}
        self.sem = {e: ctx.enter_context(nc.semaphore(f"c_{e}")) for e in ENG}
        self.dsem = {e: ctx.enter_context(nc.semaphore(f"d_{e}")) for e in ("y", "g", "s")}
        self.sem_n = {e: 0 for e in ENG}
        self.dsem_n = {e: 0 for e in ENG}
        self.waited = {}
        self.dma_barrier = {}
        self.regions = {}
        self.out_dma_tokens = []
        self.bufs = {}
        self._names = {}

    def uname(self, base):
        n = self._names.get(base, 0)
        self._names[base] = n + 1
        return base if n == 0 else f"{base}_{n}"

    def sbuf(self, name, shape, dtype):
        h = self.ctx.enter_context(self.nc.sbuf_tensor(self.uname(name), list(shape), dtype))
        b = Buf(h, name, tuple(shape), dtype)
        self.bufs[name] = b
        return b

    def psum(self, name, shape, dtype):
        h = self.ctx.enter_context(self.nc.psum_tensor(self.uname(name), list(shape), dtype))
        b = Buf(h, name, tuple(shape), dtype)
        self.bufs[name] = b
        return b

    def _ensure_inc(self, e, pos, kind):
        lst = self.ops[e]
        if kind == "d":
            op = lst[pos]
            assert op.inc is not None
            self.dma_barrier[e] = self.dsem_n[e]
            return (op.inc[0], 16 * self.dsem_n[e])
        lst[pos].need_inc = True
        return ("c", e, pos)

    def op_slow_dma(self, e, fn, writes=()):
        self.op(e, fn, reads=(), writes=writes, dma=True)

    def op(self, e, fn, reads=(), writes=(), dma=False):
        o = _Op(fn)
        lst = self.ops[e]
        pos = len(lst)
        need = {}
        for (buf, p0, p1, f0, f1) in reads:
            for a in self.regions.get(buf.name, ()):
                if a[0] == "w" and a[3] < p1 and p0 < a[4] and a[5] < f1 and f0 < a[6]:
                    need[(a[1], a[2], a[7])] = True
        for (buf, p0, p1, f0, f1) in writes:
            for a in self.regions.get(buf.name, ()):
                if a[3] < p1 and p0 < a[4] and a[5] < f1 and f0 < a[6]:
                    need[(a[1], a[2], a[7])] = True
        for (ae, apos, akind) in sorted(need):
            if ae == e and akind != "d" and not SELF_WAIT[e]:
                continue
            w = self._ensure_inc(ae, apos, akind)
            if w[0] == "c":
                key = (e, "c", ae)
                if self.waited.get(key, -1) < w[2]:
                    o.waits.append(w)
                    self.waited[key] = w[2]
            else:
                sem, val = w
                key = (e, sem.name)
                if self.waited.get(key, -1) < val:
                    o.waits.append(w)
                    self.waited[key] = val
        if dma:
            if lst and lst[-1].inc is not None and lst[-1].inc[0] is self.dsem[e]:
                lst.append(_Op(lambda eng: eng.nop()))
            self.dsem_n[e] += 1
            o.inc = (self.dsem[e], 16, 16 * self.dsem_n[e])
        kind = "d" if dma else "c"
        pos = len(lst)
        lst.append(o)
        for (buf, p0, p1, f0, f1) in writes:
            rl = self.regions.setdefault(buf.name, [])
            rl[:] = [a for a in rl if not (p0 <= a[3] and a[4] <= p1 and f0 <= a[5] and a[6] <= f1)]
            rl.append(("w", e, pos, p0, p1, f0, f1, kind))
        for (buf, p0, p1, f0, f1) in reads:
            rl = self.regions.setdefault(buf.name, [])
            rl[:] = [a for a in rl
                     if not (a[0] == "r" and a[1] == e and a[3] == p0 and a[4] == p1
                             and a[5] == f0 and a[6] == f1)]
            rl.append(("r", e, pos, p0, p1, f0, f1, kind))
        return (e, pos, kind)

    def phase_barrier(self):
        pairs = {}
        for oe in ENG:
            lst = self.ops[oe]
            if not lst:
                continue
            last = lst[-1]
            if last.inc is not None:
                pairs[oe] = (last.inc[0], 16 * self.dsem_n[oe])
            else:
                last.need_inc = True
                pairs[oe] = ("c", oe, len(lst) - 1)
        for e in ENG:
            o = _Op(lambda eng: eng.nop())
            for k, v in pairs.items():
                if k == e:
                    continue
                o.waits.append(v)
                if v[0] == "c":
                    key = (e, "c", v[1])
                    if self.waited.get(key, -1) < v[2]:
                        self.waited[key] = v[2]
                else:
                    key = (e, v[0].name)
                    if self.waited.get(key, -1) < v[1]:
                        self.waited[key] = v[1]
            self.ops[e].append(o)
        self.regions.clear()

    def wait_tokens(self, e, toks):
        o = _Op(lambda eng: eng.nop())
        o.waits = [self._ensure_inc(*t) for t in toks]
        self.ops[e].append(o)

    def finalize_into_block(self, block):
        for e in ENG:
            cum = 0
            for o in self.ops[e]:
                if o.need_inc and o.inc is None:
                    o.inc = (self.sem[e], 1, None)
                if o.inc is not None and o.inc[0] is self.sem[e]:
                    cum += 1
                    o.cumval = cum
        for e in ENG:
            for o in self.ops[e]:
                rw = []
                for w in o.waits:
                    if w[0] == "c":
                        _, ae, pos = w
                        rw.append((self.sem[ae], self.ops[ae][pos].cumval))
                    else:
                        rw.append(w)
                o.waits = rw
        for e in ENG:
            section = getattr(block, ENG_FULL[e])
            ops = self.ops[e]

            def make(ops):
                def body(eng):
                    for o in ops:
                        for sem, val in o.waits:
                            eng.wait_ge(sem, val)
                        inst = o.fn(eng)
                        if o.inc is not None:
                            inst.then_inc(o.inc[0], o.inc[1])

                return body
            section(make(ops))


def r(buf, p0, p1, f0, f1):
    return (buf, p0, p1, f0, f1)


def rfull(buf):
    return (buf, 0, buf.shape[0], 0, buf.shape[1])





def build_program(nbatch=B_LOC, stages=("rwkv", "att", "mlp"), dumps=(), rwkv_sub="full"):
    nc = bass.Bass("TRN2", target_bir_lowering=False)
    dt_in = {}

    def din(name, shape):
        t = nc.dram_tensor(name, list(shape), F32, kind="ExternalInput")
        dt_in[name] = t
        return t

    x_in = din("x", (B_LOC * T, C))
    out_t = nc.dram_tensor("out", [B_LOC * T, C], F32, kind="ExternalOutput")

    din("wkT", (C, C)); din("wvT", (C, C)); din("wrT", (C, C)); din("woT", (C, C))
    din("mixk", (C,)); din("mixv", (C,)); din("mixr", (C,))
    din("tdec", (16,)); din("tfirst", (16,))
    for l in range(2):
        din(f"wqkv{l}", (C, 3 * INNER))
        din(f"qsc{l}", (D_ATT,)); din(f"ksc{l}", (D_ATT,))
        din(f"woffn{l}", (INNER, C))
        din(f"lng{l}", (C,)); din(f"lnb{l}", (C,))
        din(f"wa{l}", (C, HIDP)); din(f"wg{l}", (C, HIDP))
        din(f"b1a{l}", (HIDP,)); din(f"b1g{l}", (HIDP,))
        din(f"w2_{l}", (HIDP, C)); din(f"b2_{l}", (C,))

    WCAST = {}
    for nm in ("wkT", "wvT", "wrT", "woT", "wqkv0", "wqkv1", "woffn0", "woffn1",
               "wa0", "wa1", "wg0", "wg1", "w2_0", "w2_1"):
        rows, cols = dt_in[nm].shape
        tot = (rows * cols) // 128
        WCAST[nm] = (nc.dram_tensor(nm + "_b", [rows * cols], BF16,
                                    kind="Internal"), tot)

    ctx = ExitStack()
    p = Prog(nc, ctx)
    dump_toks = []

    def dump(buf):
        dt = nc.dram_tensor(f"dump_{buf.name}", list(buf.shape), buf.dtype,
                            kind="ExternalOutput")
        tok = p.op("y", lambda e, buf=buf, dt=dt: e.dma_start(
            out=dt.ap()[:, :], in_=buf[:, :]), reads=[rfull(buf)], dma=True)
        dump_toks.append(tok)

    with ctx:
        X1 = p.sbuf("X1", (128, NCC * TC), F32)
        Z = p.sbuf("Z", (128, NCC * TC), F32)
        IDNF = p.sbuf("IDNF", (128, 128), F32)
        IDNB = p.sbuf("IDNB", (128, 128), BF16)
        ONE1B = p.sbuf("ONE1B", (128, 1), BF16)
        ONE1F = p.sbuf("ONE1F", (1, 128), F32)
        ONE1BR = p.sbuf("ONE1BR", (1, 128), BF16)
        ONES2 = p.sbuf("ONES2", (128, 2), BF16)
        SEL2 = p.sbuf("SEL2", (2, 128), BF16)
        SEL2A = p.sbuf("SEL2A", (1, 128), BF16)
        SEL2B = p.sbuf("SEL2B", (1, 128), BF16)
        MCAT = p.sbuf("MCAT", (128, 256), BF16)
        MIXC = p.sbuf("MIXC", (128, 24), F32)
        EWC = p.sbuf("EWC", (128, 8), F32)
        EUC = p.sbuf("EUC", (128, 8), F32)
        EW2 = p.sbuf("EW2", (2, 8), F32)
        EU2 = p.sbuf("EU2", (2, 8), F32)
        SA = p.sbuf("SA", (128, 8), F32)
        SB2 = p.sbuf("SB2", (2, 8), F32)
        STATB = p.sbuf("STATB", (128, 512), BF16)
        SUMT = p.sbuf("SUMT", (128, 512), F32)
        MUC = p.sbuf("MUC", (128, 4), F32)
        KEPS = p.sbuf("KEPS", (128, 1), F32)
        PQC = p.sbuf("PQC", (128, 128), F32)
        PQS = p.sbuf("PQS", (128, 128), F32)
        PKC = p.sbuf("PKC", (128, 128), F32)
        PKS = p.sbuf("PKS", (128, 128), F32)
        RSTC = p.sbuf("RSTC", (128, 256), F32)

        PB = [p.psum(f"PB{i}", (128, 512), F32) for i in range(4)]
        PTRN = p.psum("PTRN", (128, 512), F32)
        PTRB = p.psum("PTRB", (128, 512), BF16)
        PROW = p.psum("PROW", (128, 512), F32)
        PAV = p.psum("PAV", (65, 512), F32)

        def tcol(cc, t):
            return cc * TC + 1 + t

        def aff(buf, sl, cmp, fill, base, pattern, cm):
            p.op("g", lambda eng: eng.affine_select(out=sl, in_=sl, compare_op=cmp,
                                                    fill=fill, base=base, pattern=pattern,
                                                    channel_multiplier=cm),
                 reads=[rfull(buf)], writes=[rfull(buf)])

        def init_consts():
            for buf in (IDNF, IDNB):
                p.op("g", lambda e, b=buf: e.memset(b[:, :], 0.0), writes=[rfull(buf)])
                aff(buf, buf[:, :], ALU.not_equal, 1.0, 0, [[-1, 128]], 1)
            p.op("g", lambda e: e.memset(ONE1B[:, :], 1.0), writes=[rfull(ONE1B)])
            p.op("g", lambda e: e.memset(ONE1F[:, :], 1.0), writes=[rfull(ONE1F)])
            p.op("g", lambda e: e.memset(ONE1BR[:, :], 1.0), writes=[rfull(ONE1BR)])
            p.op("g", lambda e: e.memset(ONES2[:, :], 0.0), writes=[rfull(ONES2)])
            aff(ONES2, ONES2[:, 0:1], ALU.is_ge, 1.0, -64, [[0, 1]], 1)
            aff(ONES2, ONES2[:, 1:2], ALU.is_gt, 1.0, 64, [[0, 1]], -1)
            p.op("g", lambda e: e.memset(SEL2[:, :], 0.0), writes=[rfull(SEL2)])
            p.op("g", lambda e: e.memset(SEL2A[:, :], 0.0), writes=[rfull(SEL2A)])
            aff(SEL2A, SEL2A[:, :], ALU.is_ge, 1.0, -64, [[1, 128]], 0)
            p.op("g", lambda e: e.memset(SEL2B[:, :], 0.0), writes=[rfull(SEL2B)])
            aff(SEL2B, SEL2B[:, :], ALU.is_gt, 1.0, 64, [[-1, 128]], 0)
            aff(SEL2, SEL2[:, :], ALU.is_ge, 1.0, -64, [[1, 128]], 999)
            aff(SEL2, SEL2[:, :], ALU.is_gt, 1.0, 1063, [[-1, 128]], -999)
            p.op("g", lambda e: e.memset(MCAT[:, :], 0.0), writes=[rfull(MCAT)])
            aff(MCAT, MCAT[:, 0:128], ALU.is_gt, 1.0, 0, [[-1, 128]], 1)
            aff(MCAT, MCAT[:, 128:256], ALU.is_gt, 1.0, 0, [[1, 128]], -1)
            for i, nm in enumerate(("mixk", "mixv", "mixr")):
                def fn(e, nm=nm, i=i):
                    src = bass.AP(dt_in[nm], 0, [[1, 128], [128, 8]])
                    return e.dma_start(out=MIXC[:, 8 * i:8 * (i + 1)], in_=src,
                                       allow_slow_non_contiguous=True)
                p.op_slow_dma("g", fn, writes=[r(MIXC, 0, 128, 8 * i, 8 * (i + 1))])

            def rep(dst, nm, rows):
                if rows == 128:
                    for hh in range(2):
                        def fn(e, dst=dst, nm=nm, hh=hh):
                            ap = bass.AP(dt_in[nm], hh, [[0, 64], [2, 8]])
                            return e.dma_start(out=dst[64 * hh:64 * (hh + 1), :], in_=ap,
                                               allow_slow_non_contiguous=True)
                        p.op_slow_dma("g", fn, writes=[r(dst, 64 * hh, 64 * (hh + 1), 0, 8)])
                else:
                    def fn(e, dst=dst, nm=nm):
                        ap = bass.AP(dt_in[nm], 0, [[1, 2], [2, 8]])
                        return e.dma_start(out=dst[:, :], in_=ap,
                                           allow_slow_non_contiguous=True)
                    p.op_slow_dma("g", fn, writes=[rfull(dst)])
            rep(EWC, "tdec", 128); rep(EUC, "tfirst", 128)
            rep(EW2, "tdec", 2); rep(EU2, "tfirst", 2)
            for buf in (EWC, EW2):
                p.op("s", lambda e, b=buf: e.activation(out=b[:, :], in_=b[:, :], func=AF.Exp),
                     reads=[rfull(buf)], writes=[rfull(buf)])
                p.op("s", lambda e, b=buf: e.activation(out=b[:, :], in_=b[:, :], func=AF.Exp,
                                                        scale=-1.0),
                     reads=[rfull(buf)], writes=[rfull(buf)])
            for buf in (EUC, EU2):
                p.op("s", lambda e, b=buf: e.activation(out=b[:, :], in_=b[:, :], func=AF.Exp),
                     reads=[rfull(buf)], writes=[rfull(buf)])
            p.op("v", lambda e: e.memset(KEPS[:, :], LN_EPS), writes=[rfull(KEPS)])
            for buf in (X1, Z):
                def zc(e, b=buf):
                    ap = bass.AP(b.h, 0, [[NCC * TC, 128], [TC, 8], [1, 1]])
                    return e.memset(ap, 0.0)
                p.op("v", zc, writes=[r(buf, 0, 128, 0, buf.shape[1])])

        def init_rotary():
          with ExitStack() as ph:
            TWO_PI = 2.0 * np.pi
            mk = lambda name, shape, dt: Buf(
                ph.enter_context(nc.sbuf_tensor(p.uname(name), list(shape), dt)),
                name, tuple(shape), dt)
            IVFC = mk("IVFC", (128, 32), F32)
            LNBC = mk("LNBC", (128, 32), F32)
            BKIC = mk("BKIC", (128, 32), mybir.dt.int32)
            for d2 in range(32):
                ivf = 1.0 / 10000.0 ** (2.0 * d2 / 64.0)
                lnb = float(np.log((2.0 * d2 + 0.4 * 64.0) / (1.4 * 64.0)))
                bsq = float(np.exp(2.0 * lnb))
                c5, s5 = float(np.cos(512.0 * ivf)), float(np.sin(512.0 * ivf))
                cb, sb = float(np.cos(1536.0 * ivf)), float(-np.sin(1536.0 * ivf))
                b3 = float(np.exp(-6.0 * lnb))
                for k4, v in enumerate((c5 * bsq, s5 * bsq, c5 / bsq, s5 / bsq,
                                        cb * b3, sb * b3, cb / b3, sb / b3)):
                    col = 32 * k4 + d2
                    p.op("g", lambda e, c=float(np.float32(v)), col=col: e.memset(
                        RSTC[:, col:col + 1], c),
                        writes=[r(RSTC, 0, 128, col, col + 1)])
                p.op("g", lambda e, c=float(np.float32(ivf)), d2=d2: e.memset(
                    IVFC[:, d2:d2 + 1], c), writes=[r(IVFC, 0, 128, d2, d2 + 1)])
                p.op("g", lambda e, c=float(np.float32(lnb)), d2=d2: e.memset(
                    LNBC[:, d2:d2 + 1], c), writes=[r(LNBC, 0, 128, d2, d2 + 1)])
            S = SUMT
            for tt in range(4):
                c0 = 32 * tt
                p.op("g", lambda e: e.iota(S[:, 288:320], pattern=[[0, 32]], base=0,
                                           channel_multiplier=1,
                                           allow_small_or_imprecise_dtypes=True),
                     writes=[r(S, 0, 128, 288, 320)])
                p.op("v", lambda e, tt=tt: e.scalar_tensor_tensor(
                    out=S[:, 0:32], in0=S[:, 288:320], scalar=float(128.0 * tt),
                    in1=IVFC[:, :], op0=ALU.add, op1=ALU.mult),
                    reads=[r(S, 0, 128, 288, 320), rfull(IVFC)],
                    writes=[r(S, 0, 128, 0, 32)])
                for shift, dstc in ((0.0, 192), (float(np.pi / 2), 160)):
                    p.op("v", lambda e, shift=shift: e.tensor_scalar(
                        out=S[:, 32:64], in0=S[:, 0:32], scalar1=float(1.0 / TWO_PI),
                        scalar2=float(shift / TWO_PI), op0=ALU.mult, op1=ALU.add),
                        reads=[r(S, 0, 128, 0, 32)], writes=[r(S, 0, 128, 32, 64)])
                    p.op("v", lambda e: e.tensor_copy(out=BKIC[:, :], in_=S[:, 32:64]),
                         reads=[r(S, 0, 128, 32, 64)], writes=[rfull(BKIC)])
                    p.op("v", lambda e: e.tensor_copy(out=S[:, 64:96], in_=BKIC[:, :]),
                         reads=[rfull(BKIC)], writes=[r(S, 0, 128, 64, 96)])
                    p.op("v", lambda e: e.tensor_sub(S[:, 96:128], S[:, 32:64], S[:, 64:96]),
                         reads=[r(S, 0, 128, 32, 96)], writes=[r(S, 0, 128, 96, 128)])
                    p.op("v", lambda e: e.tensor_scalar(
                        out=S[:, 96:128], in0=S[:, 96:128], scalar1=0.5, scalar2=None,
                        op0=ALU.is_ge),
                        reads=[r(S, 0, 128, 96, 128)], writes=[r(S, 0, 128, 96, 128)])
                    p.op("v", lambda e: e.tensor_add(S[:, 64:96], S[:, 64:96], S[:, 96:128]),
                         reads=[r(S, 0, 128, 64, 128)], writes=[r(S, 0, 128, 64, 96)])
                    p.op("v", lambda e, shift=shift: e.tensor_scalar(
                        out=S[:, 128:160], in0=S[:, 64:96], scalar1=float(-TWO_PI),
                        scalar2=float(shift), op0=ALU.mult, op1=ALU.add),
                        reads=[r(S, 0, 128, 64, 96)], writes=[r(S, 0, 128, 128, 160)])
                    p.op("v", lambda e: e.tensor_add(S[:, 128:160], S[:, 128:160], S[:, 0:32]),
                         reads=[r(S, 0, 128, 0, 32), r(S, 0, 128, 128, 160)],
                         writes=[r(S, 0, 128, 128, 160)])
                    p.op("s", lambda e, dstc=dstc: e.activation(
                        out=S[:, dstc:dstc + 32], in_=S[:, 128:160], func=AF.Sin),
                        reads=[r(S, 0, 128, 128, 160)],
                        writes=[r(S, 0, 128, dstc, dstc + 32)])
                p.op("v", lambda e, tt=tt: e.tensor_scalar(
                    out=S[:, 320:321], in0=S[:, 288:289], scalar1=float(1.0 / 256.0),
                    scalar2=float((128.0 * tt - 1024.0) / 256.0), op0=ALU.mult,
                    op1=ALU.add),
                    reads=[r(S, 0, 128, 288, 289)], writes=[r(S, 0, 128, 320, 321)])
                p.op("v", lambda e: e.tensor_scalar(
                    out=S[:, 32:64], in0=LNBC[:, :], scalar1=S[:, 320:321], scalar2=None,
                    op0=ALU.mult),
                    reads=[rfull(LNBC), r(S, 0, 128, 320, 321)],
                    writes=[r(S, 0, 128, 32, 64)])
                p.op("s", lambda e: e.activation(out=S[:, 224:256], in_=S[:, 32:64],
                                                 func=AF.Exp),
                     reads=[r(S, 0, 128, 32, 64)], writes=[r(S, 0, 128, 224, 256)])
                p.op("s", lambda e: e.activation(out=S[:, 256:288], in_=S[:, 32:64],
                                                 func=AF.Exp, scale=-1.0),
                     reads=[r(S, 0, 128, 32, 64)], writes=[r(S, 0, 128, 256, 288)])
                for dst, trig, sc in ((PQC, 160, 224), (PQS, 192, 224),
                                      (PKC, 160, 256), (PKS, 192, 256)):
                    p.op("v", lambda e, dst=dst, trig=trig, sc=sc, c0=c0: e.tensor_mul(
                        dst[:, c0:c0 + 32], S[:, trig:trig + 32], S[:, sc:sc + 32]),
                        reads=[r(S, 0, 128, trig, trig + 32), r(S, 0, 128, sc, sc + 32)],
                        writes=[r(dst, 0, 128, c0, c0 + 32)])
            p.phase_barrier()

        def init_wcast():
          with ExitStack() as ph:
            mk = lambda name, shape, dt: Buf(
                ph.enter_context(nc.sbuf_tensor(p.uname(name), list(shape), dt)),
                name, tuple(shape), dt)
            SFr = [mk("WCF", (128, 2048), F32) for _ in range(2)]
            SBr = [mk("WCB", (128, 2048), BF16) for _ in range(2)]
            ci = 0
            for nm, (mir, tot) in WCAST.items():
                for c0 in range(0, tot, 2048):
                    w = min(2048, tot - c0)
                    SF, SB = SFr[ci % 2], SBr[ci % 2]
                    ci += 1
                    p.op("y", lambda e, nm=nm, c0=c0, w=w, SF=SF, tot=tot:
                         e.dma_start(out=SF[:, 0:w], in_=bass.AP(
                             dt_in[nm], c0, [[tot, 128], [1, w]])),
                         writes=[r(SF, 0, 128, 0, w)], dma=True)
                    p.op("v", lambda e, w=w, SF=SF, SB=SB: e.tensor_copy(
                        out=SB[:, 0:w], in_=SF[:, 0:w]),
                        reads=[r(SF, 0, 128, 0, w)], writes=[r(SB, 0, 128, 0, w)])
                    p.op("y", lambda e, mir=mir, c0=c0, w=w, SB=SB, tot=tot:
                         e.dma_start(out=bass.AP(mir, c0, [[tot, 128], [1, w]]),
                                     in_=SB[:, 0:w]),
                         reads=[r(SB, 0, 128, 0, w)], dma=True)
            p.phase_barrier()

        def load_input_batch(b):
          with ExitStack() as ph:
            XTMr = [Buf(ph.enter_context(nc.sbuf_tensor(p.uname("XTM"), [128, 1024], F32)),
                        "XTM", (128, 1024), F32) for _ in range(2)]
            for tt in range(T // 128):
                t0 = tt * 128
                XTM = XTMr[tt % 2]
                PTX = (PTRN, PB[0])[tt % 2]
                p.op("y", lambda e, b=b, t0=t0, XTM=XTM: e.dma_start(
                    out=XTM[:, :], in_=x_in.ap()[b * T + t0: b * T + t0 + 128, :]),
                    writes=[rfull(XTM)], dma=True)
                for half in range(2):
                    for k in range(4):
                        cc = half * 4 + k
                        p.op("t", lambda e, cc=cc, k=k, XTM=XTM, PTX=PTX: e.transpose(
                            PTX[:, 128 * k:128 * (k + 1)],
                            XTM[:, 128 * cc:128 * (cc + 1)], IDNF[:, :]),
                            reads=[r(XTM, 0, 128, 128 * cc, 128 * (cc + 1)), rfull(IDNF)],
                            writes=[r(PTX, 0, 128, 128 * k, 128 * (k + 1))])
                    cc0 = half * 4

                    def ev(e, cc0=cc0, t0=t0, PTX=PTX):
                        dst = bass.AP(Z.h, tcol(cc0, t0), [[NCC * TC, 128], [TC, 4], [1, 128]])
                        return e.tensor_copy(out=dst, in_=PTX[:, :])
                    p.op("v", ev, reads=[rfull(PTX)],
                         writes=[r(Z, 0, 128, tcol(cc0, t0), tcol(cc0 + 3, t0) + 128)])
            p.phase_barrier()

        def ln_rows_mid():
            (MB, mr, mc), (RB, rr, rc) = p.bufs["LNROWV"]
            MU = MB[mr:mr + 1, mc:mc + 512]
            RS = RB[rr:rr + 1, rc:rc + 512]
            mreg = r(MB, mr, mr + 1, mc, mc + 512)
            rreg = r(RB, rr, rr + 1, rc, rc + 512)
            p.op("v", lambda e: e.tensor_scalar_mul(MU, PROW[0:1, :], 1.0 / C),
                 reads=[r(PROW, 0, 1, 0, 512)], writes=[mreg])
            p.op("v", lambda e: e.tensor_mul(PROW[0:1, :], MU, MU),
                 reads=[mreg, r(PROW, 0, 1, 0, 512)],
                 writes=[r(PROW, 0, 1, 0, 512)])
            p.op("v", lambda e: e.tensor_scalar_mul(RS, PAV[0:1, :], 1.0 / C),
                 reads=[r(PAV, 0, 1, 0, 512)], writes=[rreg])
            p.op("v", lambda e: e.tensor_sub(RS, RS, PROW[0:1, :]),
                 reads=[rreg, r(PROW, 0, 1, 0, 512)], writes=[rreg])
            p.op("s", lambda e: e.activation(out=RS, in_=RS,
                                             func=AF.Ln, bias=KEPS[0:1, 0:1]),
                 reads=[rreg, rfull(KEPS)], writes=[rreg])
            p.op("s", lambda e: e.activation(out=RS, in_=RS, func=AF.Exp, scale=-0.5),
                 reads=[rreg], writes=[rreg])
            p.op("t", lambda e: e.matmul(PROW[:, :], ONE1F[:, :], RS,
                                         start=True, stop=True),
                 reads=[rfull(ONE1F), rreg], writes=[rfull(PROW)])
            p.op("v", lambda e: e.scalar_tensor_tensor(
                out=MU, in0=MU, scalar=-1.0, in1=RS, op0=ALU.mult, op1=ALU.mult),
                reads=[mreg, rreg], writes=[mreg])
            p.op("t", lambda e: e.matmul(PTRN[:, :], ONE1F[:, :], MU,
                                         start=True, stop=True),
                 reads=[rfull(ONE1F), mreg], writes=[rfull(PTRN)])

        def ln_channel_major(src, dst, t0):
            for cc in range(NCC):
                sl = r(src, 0, 128, tcol(cc, t0), tcol(cc, t0) + 512)
                p.op("s", lambda e, cc=cc, t0=t0, src=src: e.copy(
                    out=STATB[:, 0:512], in_=src[:, tcol(cc, t0):tcol(cc, t0) + 512]),
                    reads=[sl], writes=[r(STATB, 0, 128, 0, 512)])
                p.op("t", lambda e, cc=cc: e.matmul(
                    PROW[0:1, :], ONE1B[:, :], STATB[:, 0:512],
                    start=(cc == 0), stop=(cc == NCC - 1)),
                    reads=[rfull(ONE1B), r(STATB, 0, 128, 0, 512)],
                    writes=[r(PROW, 0, 1, 0, 512)])
                p.op("s", lambda e, cc=cc, t0=t0, src=src: e.activation(
                    out=STATB[:, 0:512], in_=src[:, tcol(cc, t0):tcol(cc, t0) + 512],
                    func=AF.Square),
                    reads=[sl], writes=[r(STATB, 0, 128, 0, 512)])
                p.op("t", lambda e, cc=cc: e.matmul(
                    PAV[0:1, :], ONE1B[:, :], STATB[:, 0:512],
                    start=(cc == 0), stop=(cc == NCC - 1)),
                    reads=[rfull(ONE1B), r(STATB, 0, 128, 0, 512)],
                    writes=[r(PAV, 0, 1, 0, 512)])
            ln_rows_mid()
            for cc in range(NCC):
                sl = r(src, 0, 128, tcol(cc, t0), tcol(cc, t0) + 512)
                dl = r(dst, 0, 128, tcol(cc, t0), tcol(cc, t0) + 512)
                p.op("v", lambda e, cc=cc, t0=t0, src=src, dst=dst: e.tensor_mul(
                    dst[:, tcol(cc, t0):tcol(cc, t0) + 512],
                    src[:, tcol(cc, t0):tcol(cc, t0) + 512], PROW[:, :]),
                    reads=[sl, rfull(PROW)], writes=[dl])
                p.op("v", lambda e, cc=cc, t0=t0, dst=dst: e.tensor_add(
                    dst[:, tcol(cc, t0):tcol(cc, t0) + 512],
                    dst[:, tcol(cc, t0):tcol(cc, t0) + 512], PTRN[:, :]),
                    reads=[dl, rfull(PTRN)], writes=[dl])

        def rwkv_batch(b):
            with ExitStack() as ph:
                sb = lambda name, shape, dt: Buf(
                    ph.enter_context(nc.sbuf_tensor(p.uname(name), list(shape), dt)),
                    name, tuple(shape), dt)
                WSLAB = sb("WSLAB", (128, NCC * 1024), BF16)
                LNROW = sb("LNROW", (1, 1024), F32)
                p.bufs["LNROWV"] = ((LNROW, 0, 0), (LNROW, 0, 512))
                KH = sb("KH", (128, NCC * 1024), BF16)
                VH = sb("VH", (128, NCC * 1024), BF16)
                XKB = sb("XKB", (128, NCC * 512), BF16)
                DDr = [sb("DDr", (128, 512), F32)] * 2
                EKVQ = SUMT
                SCN = sb("SCN", (128, 513), F32)
                EKS = sb("EKS", (2, 512), F32)
                SBB = sb("SBB", (2, 513), F32)
                RSIGr = [sb("RSIGr", (128, 512), F32)] * 2
                for nm, bb in (("WSLAB", WSLAB), ("KH", KH), ("VH", VH), ("XKB", XKB)):
                    p.bufs[nm] = bb

                def load_wslab(nm):
                    def fn(e, nm=nm):
                        src = bass.AP(WCAST[nm][0], 0,
                                      [[1024, 128], [131072, 8], [1, 1024]])
                        return e.dma_start(out=WSLAB[:, :].rearrange("p (cc co) -> p cc co", cc=8),
                                           in_=src)
                    return p.op("y", fn, writes=[rfull(WSLAB)], dma=True)

                def mix_tiles(t0, mixoff):
                    for cc in range(NCC):
                        DD = DDr[cc % 2]
                        xc = r(Z, 0, 128, tcol(cc, t0) - 1, tcol(cc, t0) + 512)
                        p.op("v", lambda e, DD=DD, cc=cc, t0=t0: e.tensor_sub(
                            DD[:, :], Z[:, tcol(cc, t0):tcol(cc, t0) + 512],
                            Z[:, tcol(cc, t0) - 1:tcol(cc, t0) + 511]),
                            reads=[xc], writes=[rfull(DD)])
                        p.op("v", lambda e, DD=DD, cc=cc, t0=t0, mo=mixoff: e.scalar_tensor_tensor(
                            out=XKB[:, 512 * cc:512 * (cc + 1)], in0=DD[:, :],
                            scalar=MIXC[:, mo + cc:mo + cc + 1],
                            in1=Z[:, tcol(cc, t0) - 1:tcol(cc, t0) + 511],
                            op0=ALU.mult, op1=ALU.add),
                            reads=[rfull(DD), xc, r(MIXC, 0, 128, mixoff + cc, mixoff + cc + 1)],
                            writes=[r(XKB, 0, 128, 512 * cc, 512 * (cc + 1))])

                def proj_block(dstbuf, hl0, act=None):
                    for co in range(NCC):
                        ps = PB[co % 4]
                        for cc in range(NCC):
                            p.op("t", lambda e, ps=ps, co=co, cc=cc: e.matmul(
                                ps[:, :],
                                WSLAB[:, 1024 * cc + 128 * co:1024 * cc + 128 * (co + 1)],
                                XKB[:, 512 * cc:512 * (cc + 1)],
                                start=(cc == 0), stop=(cc == NCC - 1)),
                                reads=[r(WSLAB, 0, 128, 1024 * cc + 128 * co,
                                         1024 * cc + 128 * (co + 1)),
                                       r(XKB, 0, 128, 512 * cc, 512 * (cc + 1))],
                                writes=[rfull(ps)])
                        p.op("s", lambda e, ps=ps, co=co, hl0=hl0, dstbuf=dstbuf: e.copy(
                            out=dstbuf[:, 1024 * co + hl0:1024 * co + hl0 + 512], in_=ps[:, :]),
                            reads=[rfull(ps)],
                            writes=[r(dstbuf, 0, 128, 1024 * co + hl0, 1024 * co + hl0 + 512)])

                def wkv_quarter(cc, hl0):
                    par = (cc + hl0 // 512) % 2
                    EKB0 = 1536 * par
                    RCP0 = 512 + 1536 * par
                    ksl = r(KH, 0, 128, 1024 * cc + hl0, 1024 * cc + hl0 + 512)
                    vsl = r(VH, 0, 128, 1024 * cc + hl0, 1024 * cc + hl0 + 512)
                    p.op("s", lambda e: e.activation(
                        out=XKB[:, EKB0:EKB0 + 512], in_=KH[:, 1024 * cc + hl0:1024 * cc + hl0 + 512],
                        func=AF.Exp),
                        reads=[ksl], writes=[r(XKB, 0, 128, EKB0, EKB0 + 512)])
                    p.op("v", lambda e: e.tensor_mul(
                        EKVQ[:, :], XKB[:, EKB0:EKB0 + 512], VH[:, 1024 * cc + hl0:1024 * cc + hl0 + 512]),
                        reads=[r(XKB, 0, 128, EKB0, EKB0 + 512), vsl], writes=[rfull(EKVQ)])
                    p.op("t", lambda e: e.matmul(PROW[0:2, :], ONES2[:, :], XKB[:, EKB0:EKB0 + 512],
                                                 start=True, stop=True),
                         reads=[rfull(ONES2), r(XKB, 0, 128, EKB0, EKB0 + 512)], writes=[rfull(PROW)])
                    p.op("v", lambda e: e.tensor_copy(out=EKS[:, :], in_=PROW[0:2, :]),
                         reads=[rfull(PROW)], writes=[rfull(EKS)])
                    p.op("v", lambda e: e.tensor_copy(out=SCN[:, 0:1], in_=SA[:, cc:cc + 1]),
                         reads=[r(SA, 0, 128, cc, cc + 1)], writes=[r(SCN, 0, 128, 0, 1)])
                    p.op("v", lambda e: e.tensor_tensor_scan(
                        out=SCN[:, 1:513], data0=EWC[:, cc:cc + 1].broadcast_to([128, 512]),
                        data1=EKVQ[:, :], initial=SA[:, cc:cc + 1], op0=ALU.mult, op1=ALU.add),
                        reads=[rfull(EKVQ), r(EWC, 0, 128, cc, cc + 1), r(SA, 0, 128, cc, cc + 1)],
                        writes=[r(SCN, 0, 128, 1, 513)])
                    p.op("v", lambda e: e.tensor_copy(out=SA[:, cc:cc + 1], in_=SCN[:, 512:513]),
                         reads=[r(SCN, 0, 128, 512, 513)], writes=[r(SA, 0, 128, cc, cc + 1)])
                    p.op("v", lambda e: e.scalar_tensor_tensor(
                        out=SCN[:, 0:512], in0=EKVQ[:, :], scalar=EUC[:, cc:cc + 1],
                        in1=SCN[:, 0:512], op0=ALU.mult, op1=ALU.add),
                        reads=[rfull(EKVQ), r(EUC, 0, 128, cc, cc + 1), r(SCN, 0, 128, 0, 512)],
                        writes=[r(SCN, 0, 128, 0, 512)])
                    p.op("v", lambda e: e.tensor_copy(out=SBB[:, 0:1], in_=SB2[:, cc:cc + 1]),
                         reads=[r(SB2, 0, 2, cc, cc + 1)], writes=[r(SBB, 0, 2, 0, 1)])
                    p.op("v", lambda e: e.tensor_tensor_scan(
                        out=SBB[:, 1:513], data0=EW2[:, cc:cc + 1].broadcast_to([2, 512]),
                        data1=EKS[:, :], initial=SB2[:, cc:cc + 1], op0=ALU.mult, op1=ALU.add),
                        reads=[rfull(EKS), r(EW2, 0, 2, cc, cc + 1), r(SB2, 0, 2, cc, cc + 1)],
                        writes=[r(SBB, 0, 2, 1, 513)])
                    p.op("v", lambda e: e.tensor_copy(out=SB2[:, cc:cc + 1], in_=SBB[:, 512:513]),
                         reads=[r(SBB, 0, 2, 512, 513)], writes=[r(SB2, 0, 2, cc, cc + 1)])
                    p.op("v", lambda e: e.tensor_scalar(
                        out=EKS[:, :], in0=EKS[:, :], scalar1=EU2[:, cc:cc + 1], scalar2=1e-6,
                        op0=ALU.mult, op1=ALU.max),
                        reads=[rfull(EKS), r(EU2, 0, 2, cc, cc + 1)], writes=[rfull(EKS)])
                    p.op("v", lambda e: e.tensor_add(EKS[:, :], EKS[:, :], SBB[:, 0:512]),
                         reads=[rfull(EKS), r(SBB, 0, 2, 0, 512)], writes=[rfull(EKS)])
                    p.op("s", lambda e: e.activation(out=EKS[:, :], in_=EKS[:, :], func=AF.Ln),
                         reads=[rfull(EKS)], writes=[rfull(EKS)])
                    p.op("s", lambda e: e.activation(out=XKB[0:2, RCP0:RCP0 + 512],
                                                     in_=EKS[:, :], func=AF.Exp, scale=-1.0),
                         reads=[rfull(EKS)], writes=[r(XKB, 0, 2, RCP0, RCP0 + 512)])
                    p.op("t", lambda e: e.matmul(PROW[:, :], SEL2[:, :], XKB[0:2, RCP0:RCP0 + 512],
                                                 start=True, stop=True),
                         reads=[rfull(SEL2), r(XKB, 0, 2, RCP0, RCP0 + 512)], writes=[rfull(PROW)])
                    p.op("v", lambda e: e.tensor_mul(
                        KH[:, 1024 * cc + hl0:1024 * cc + hl0 + 512], SCN[:, 0:512], PROW[:, :]),
                        reads=[r(SCN, 0, 128, 0, 512), rfull(PROW)],
                        writes=[ksl])

                def rwkv_r_block(hl0):
                    for co in range(NCC):
                        ps = PB[co % 4]
                        for cc in range(NCC):
                            p.op("t", lambda e, ps=ps, co=co, cc=cc: e.matmul(
                                ps[:, :],
                                WSLAB[:, 1024 * cc + 128 * co:1024 * cc + 128 * (co + 1)],
                                XKB[:, 512 * cc:512 * (cc + 1)],
                                start=(cc == 0), stop=(cc == NCC - 1)),
                                reads=[r(WSLAB, 0, 128, 1024 * cc + 128 * co,
                                         1024 * cc + 128 * (co + 1)),
                                       r(XKB, 0, 128, 512 * cc, 512 * (cc + 1))],
                                writes=[rfull(ps)])
                        RSIG = RSIGr[co % 2]
                        p.op("s", lambda e, ps=ps, RSIG=RSIG: e.activation(
                            out=RSIG[:, :], in_=ps[:, :], func=AF.Sigmoid),
                            reads=[rfull(ps)], writes=[rfull(RSIG)])
                        p.op("v", lambda e, co=co, hl0=hl0, RSIG=RSIG: e.tensor_mul(
                            VH[:, 1024 * co + hl0:1024 * co + hl0 + 512], RSIG[:, :],
                            KH[:, 1024 * co + hl0:1024 * co + hl0 + 512]),
                            reads=[rfull(RSIG),
                                   r(KH, 0, 128, 1024 * co + hl0, 1024 * co + hl0 + 512)],
                            writes=[r(VH, 0, 128, 1024 * co + hl0, 1024 * co + hl0 + 512)])

                def rwkv_out_block(t0, hl0):
                    for oc in range(NCC):
                        ps = PB[oc % 4]
                        for ic in range(NCC):
                            p.op("t", lambda e, ps=ps, oc=oc, ic=ic, hl0=hl0: e.matmul(
                                ps[:, :],
                                WSLAB[:, 1024 * ic + 128 * oc:1024 * ic + 128 * (oc + 1)],
                                VH[:, 1024 * ic + hl0:1024 * ic + hl0 + 512],
                                start=(ic == 0), stop=(ic == NCC - 1)),
                                reads=[r(WSLAB, 0, 128, 1024 * ic + 128 * oc,
                                         1024 * ic + 128 * (oc + 1)),
                                       r(VH, 0, 128, 1024 * ic + hl0, 1024 * ic + hl0 + 512)],
                                writes=[rfull(ps)])
                        p.op("v", lambda e, ps=ps, oc=oc, t0=t0: e.tensor_add(
                            X1[:, tcol(oc, t0):tcol(oc, t0) + 512],
                            Z[:, tcol(oc, t0):tcol(oc, t0) + 512], ps[:, :]),
                            reads=[r(Z, 0, 128, tcol(oc, t0), tcol(oc, t0) + 512), rfull(ps)],
                            writes=[r(X1, 0, 128, tcol(oc, t0), tcol(oc, t0) + 512)])
                    ln_channel_major(X1, X1, t0)

                p.op("v", lambda e: e.memset(SA[:, :], 0.0), writes=[rfull(SA)])
                p.op("v", lambda e: e.memset(SB2[:, :], 0.0), writes=[rfull(SB2)])
                for half in range(2):
                    th0 = half * 1024
                    for nm, dstbuf, mixoff in (("wkT", KH, 0), ("wvT", VH, 8)):
                        load_wslab(nm)
                        for tb in range(2):
                            mix_tiles(th0 + tb * 512, mixoff)
                            proj_block(dstbuf, tb * 512)
                    if rwkv_sub == "kv":
                        continue
                    for cc in range(NCC):
                        for q in range(2):
                            wkv_quarter(cc, q * 512)
                    if rwkv_sub == "wkv":
                        if half == 1 and b in dump_batches:
                            p.op("v", lambda e: e.tensor_copy(out=SUMT[:, :], in_=PROW[:, :]),
                                 reads=[rfull(PROW)], writes=[rfull(SUMT)])
                            for nm, bb in (("SCN", SCN), ("EKS", EKS), ("SBB", SBB),
                                           ("SUMT", SUMT), ("XKB", XKB)):
                                if nm in dump_stages:
                                    dump(bb)
                        continue
                    load_wslab("wrT")
                    for tb in range(2):
                        mix_tiles(th0 + tb * 512, 16)
                        rwkv_r_block(tb * 512)
                    if rwkv_sub == "r":
                        continue
                    load_wslab("woT")
                    for tb in range(2):
                        rwkv_out_block(th0 + tb * 512, tb * 512)
                if b in dump_batches:
                    for nm in ("KH", "VH"):
                        if nm in dump_stages:
                            dump(p.bufs[nm])
                p.phase_barrier()


        def ffn_att_batch(b, l, R):
            with ExitStack() as ph:
                sb = lambda name, shape, dt: Buf(
                    ph.enter_context(nc.sbuf_tensor(p.uname(name), list(shape), dt)),
                    name, tuple(shape), dt)
                WQKVB = sb("WQKVB", (128, NCC * 512), BF16)
                WOFr = [sb("WOF", (128, 2 * 512), BF16) for _ in range(2)]
                SHr = [sb("SH", (128, NCC * 129), BF16) for _ in range(2)]
                QT32r = [sb("QT32", (128, 512), F32) for _ in range(2)]
                QKB2r = [sb("QKB2", (128, 512), BF16) for _ in range(2)]
                T3Br = [sb("T3B", (128, 512), BF16)] * 2
                QHATr = [sb("QHAT", (128, 512), F32)] * 2
                QKTE = sb("QTE", (64, 8 * 512), BF16)
                KTE0 = sb("KTE0", (64, 8 * 512), BF16)
                KTE1 = sb("KTE1", (64, 8 * 512), BF16)
                QTEv = (QKTE, 0)
                KTEv = [(KTE0, 0), (KTE1, 0)]
                VE = [sb("VE0", (128, 8 * 4 * 65), BF16), sb("VE1", (128, 8 * 4 * 65), BF16)]
                ATTTr = [sb("ATTT", (128, 512), BF16) for _ in range(4)]
                OT = sb("OT", (128, 4 * 512), BF16)
                RC2B = sb("RC2B", (1, 512), BF16)
                RCBS = sb("RCBS", (128, 512), BF16)
                RC2A = RCBS
                RSQ = sb("RSQ", (128, 20), F32)
                SVCOL = sb("SVCOL", (128, 8), F32)
                CSQr = [sb("CSQ", (128, 256), BF16)] * 2
                SSQr = [sb("SSQ", (128, 256), BF16)] * 2
                CSKr = [sb("CSK", (128, 256), BF16)] * 2
                SSKr = [sb("SSK", (128, 256), BF16)] * 2
                TR1 = QT32r[0]
                TR2 = QT32r[1]
                QSB = sb("QSB", (128, 256), BF16)

                def ldw(nm, dst, nicc, ncols):
                    def fn(e, nm=nm, dst=dst, nicc=nicc, ncols=ncols):
                        src = bass.AP(dt_in[nm], 0,
                                      [[ncols, 128], [128 * ncols, nicc], [1, ncols]])
                        dap = bass.AP(dst.h, 0,
                                      [[dst.shape[1], 128], [ncols, nicc], [1, ncols]])
                        return e.dma_start(out=dap, in_=src)
                    return p.op("g", fn, writes=[rfull(dst)], dma=True)


                def load_wof(qh):
                    WOF = WOFr[qh % 2]

                    def fn(e, qh=qh, WOF=WOF):
                        src = bass.AP(WCAST[f"woffn{l}"][0], qh * 256,
                                      [[1024, 128], [131072, 4], [1, 256]])
                        dap = bass.AP(WOF.h, 0, [[1024, 128], [256, 4], [1, 256]])
                        return e.dma_start(out=dap, in_=src)
                    return p.op("y", fn, writes=[rfull(WOF)], dma=True)

                def load_qkv_slab(cob):
                    def fn(e, cob=cob):
                        src = bass.AP(WCAST[f"wqkv{l}"][0], cob * 512,
                                      [[1536, 128], [128 * 1536, 8], [1, 512]])
                        dap = bass.AP(WQKVB.h, 0, [[NCC * 512, 128], [512, NCC], [1, 512]])
                        return e.dma_start(out=dap, in_=src)
                    return p.op("y", fn, writes=[rfull(WQKVB)], dma=True)
                for i, nm in enumerate((f"qsc{l}", f"ksc{l}")):
                    p.op("y", lambda e, nm=nm: e.dma_start(
                        out=TR2[0:1, 0:64],
                        in_=bass.AP(dt_in[nm], 0, [[64, 1], [1, 64]])),
                        writes=[r(TR2, 0, 1, 0, 64)], dma=True)
                    p.op("t", lambda e: e.matmul(PROW[:, 0:64], ONE1F[:, :], TR2[0:1, 0:64],
                                                 start=True, stop=True),
                         reads=[rfull(ONE1F), r(TR2, 0, 1, 0, 64)],
                         writes=[r(PROW, 0, 128, 0, 64)])
                    p.op("v", lambda e, i=i: e.tensor_scalar_mul(
                        QSB[:, 128 * i:128 * i + 64], PROW[:, 0:64],
                        8.0 if i == 0 else 1.0),
                        reads=[r(PROW, 0, 128, 0, 64)],
                        writes=[r(QSB, 0, 128, 128 * i, 128 * i + 64)])
                    p.op("v", lambda e, i=i: e.tensor_scalar_mul(
                        QSB[:, 128 * i + 64:128 * i + 96], QSB[:, 128 * i + 32:128 * i + 64], -1.0),
                        reads=[r(QSB, 0, 128, 128 * i, 128 * i + 64)],
                        writes=[r(QSB, 0, 128, 128 * i + 64, 128 * i + 96)])
                    p.op("v", lambda e, i=i: e.tensor_copy(
                        out=QSB[:, 128 * i + 96:128 * i + 128], in_=QSB[:, 128 * i:128 * i + 32]),
                        reads=[r(QSB, 0, 128, 128 * i, 128 * i + 32)],
                        writes=[r(QSB, 0, 128, 128 * i + 96, 128 * i + 128)])
                for vb in VE:
                    def vones(e, vb=vb):
                        ap = bass.AP(vb.h, 64, [[2080, 128], [65, 32], [1, 1]])
                        return e.memset(ap, 1.0)
                    p.op("v", vones, writes=[r(vb, 0, 128, 0, vb.shape[1])])

                def emit_tables(w):
                    for tt in range(4):
                        c0t = tt * 64
                        for dst, mast, qs_off in (
                                (CSQr[0], PQC, 0), (SSQr[0], PQS, 64),
                                (CSKr[0], PKC, 128), (SSKr[0], PKS, 192)):
                            def f1(e, dst=dst, mast=mast, qs_off=qs_off, c0t=c0t, tt=tt):
                                i0 = bass.AP(mast.h, 32 * tt, [[128, 128], [0, 2], [1, 32]])
                                i1 = QSB[:, qs_off:qs_off + 64].rearrange(
                                    "p (a f) -> p a f", a=2)
                                o0 = dst[:, c0t:c0t + 64].rearrange("p (a f) -> p a f", a=2)
                                return e.tensor_tensor(out=o0, in0=i0, in1=i1, op=ALU.mult)
                            p.op("g", f1,
                                 reads=[rfull(mast), r(QSB, 0, 128, qs_off, qs_off + 64)],
                                 writes=[r(dst, 0, 128, c0t, c0t + 64)])

                def advance(back):
                    co = 128 if back else 0
                    r3 = lambda buf: bass.AP(buf.h, 0,
                                             [[buf.shape[1], 128], [32, 4], [1, 32]])
                    r128 = lambda buf: r(buf, 0, 128, 0, 128)
                    for pc, ps, cc0, sc0 in ((PQC, PQS, co + 0, co + 32),
                                             (PKC, PKS, co + 64, co + 96)):
                        CA = bass.AP(RSTC.h, cc0, [[256, 128], [0, 4], [1, 32]])
                        SA_ = bass.AP(RSTC.h, sc0, [[256, 128], [0, 4], [1, 32]])
                        steps = (
                            (TR1, pc, CA), (TR2, pc, SA_), (pc, ps, SA_))
                        for dstb, in0b, capb in steps:
                            p.op("g", lambda e, dstb=dstb, in0b=in0b, capb=capb:
                                 e.tensor_tensor(
                                out=r3(dstb), in0=r3(in0b), in1=capb, op=ALU.mult),
                                reads=[r128(in0b), rfull(RSTC)], writes=[r128(dstb)])
                        p.op("g", lambda e, pc=pc: e.tensor_sub(
                            pc[:, 0:128], TR1[:, 0:128], pc[:, 0:128]),
                            reads=[r128(TR1), r128(pc)], writes=[r128(pc)])
                        p.op("g", lambda e, ps=ps, CA=CA: e.tensor_tensor(
                            out=r3(TR1), in0=r3(ps), in1=CA, op=ALU.mult),
                            reads=[r128(ps), rfull(RSTC)], writes=[r128(TR1)])
                        p.op("g", lambda e, ps=ps: e.tensor_add(
                            ps[:, 0:128], TR1[:, 0:128], TR2[:, 0:128]),
                            reads=[r128(TR1), r128(TR2)], writes=[r128(ps)])

                def sh_build(w, tt):
                    SH = SHr[tt % 2]
                    t0 = w * 512 + tt * 128
                    for cc in range(NCC):
                        p.op("s", lambda e, SH=SH, cc=cc, t0=t0: e.copy(
                            out=SH[:, 129 * cc:129 * (cc + 1)],
                            in_=R[:, tcol(cc, t0) - 1:tcol(cc, t0) + 128]),
                            reads=[r(R, 0, 128, tcol(cc, t0) - 1, tcol(cc, t0) + 128)],
                            writes=[r(SH, 0, 128, 129 * cc, 129 * (cc + 1))])
                    if l == 1 and w > 0 and tt == 0:
                        def fix(e, SH=SH):
                            dst = bass.AP(SH.h, 0, [[NCC * 129, 128], [129, 8], [1, 1]])
                            return e.tensor_copy(out=dst, in_=SVCOL[:, :])
                        p.op("v", fix, reads=[rfull(SVCOL)],
                             writes=[r(SH, 0, 128, 0, SH.shape[1])])

                def lhs_sh(cc, tt):
                    SH = SHr[tt % 2]
                    off = 129 * cc
                    if cc < 4:
                        return SH[:, off + 1:off + 129], r(SH, 0, 128, off + 1, off + 129)
                    return SH[:, off:off + 128], r(SH, 0, 128, off, off + 128)

                def qkv_cob(w, cob, tt):
                    ps = PB[tt % 4]
                    for cc in range(NCC):
                        ap, rd = lhs_sh(cc, tt)
                        p.op("t", lambda e, ps=ps, ap=ap, cc=cc: e.matmul(
                            ps[:, :], ap, WQKVB[:, 512 * cc:512 * (cc + 1)],
                            start=(cc == 0), stop=(cc == NCC - 1)),
                            reads=[rd, r(WQKVB, 0, 128, 512 * cc, 512 * (cc + 1))],
                            writes=[rfull(ps)])
                    return ps

                def v_evac(w, tt, ps):
                    wb = w % 2

                    def vev(e, wb=wb, tt=tt, ps=ps):
                        dst = bass.AP(VE[wb].h, tt * 65, [[2080, 128], [260, 8], [1, 64]])
                        return e.copy(out=dst, in_=ps[:, :])
                    p.op("s", vev, reads=[rfull(ps)],
                         writes=[r(VE[wb], 0, 128, 0, VE[wb].shape[1])])

                def qk_process(w, iq, tt, ps):
                    wb = w % 2
                    QT32 = QT32r[tt % 2]
                    QKB2 = QKB2r[tt % 2]
                    T3B = T3Br[tt % 2]
                    QHAT = QHATr[tt % 2]
                    wr2 = w % 2
                    CST, SST = (CSQr[wr2], SSQr[wr2]) if iq == 0 else (CSKr[wr2], SSKr[wr2])
                    p.op("s", lambda e, QHAT=QHAT, ps=ps: e.activation(out=QHAT[:, :], in_=ps[:, :],
                                                     func=AF.Square),
                         reads=[rfull(ps)], writes=[rfull(QHAT)])
                    p.op("v", lambda e, QHAT=QHAT: e.tensor_reduce(
                        RSQ[:, 0:8], QHAT.ap().rearrange("p (h d) -> p h d", h=8),
                        axis=AX.X, op=ALU.add),
                        reads=[rfull(QHAT)], writes=[r(RSQ, 0, 128, 0, 8)])
                    p.op("v", lambda e: e.tensor_scalar_max(RSQ[:, 8:16], RSQ[:, 0:8], 1e-24),
                         reads=[r(RSQ, 0, 128, 0, 8)], writes=[r(RSQ, 0, 128, 8, 16)])
                    p.op("s", lambda e: e.activation(out=RSQ[:, 8:16], in_=RSQ[:, 8:16],
                                                     func=AF.Ln),
                         reads=[r(RSQ, 0, 128, 8, 16)], writes=[r(RSQ, 0, 128, 8, 16)])
                    p.op("s", lambda e: e.activation(out=RSQ[:, 8:16], in_=RSQ[:, 8:16],
                                                     func=AF.Exp, scale=-0.5),
                         reads=[r(RSQ, 0, 128, 8, 16)], writes=[r(RSQ, 0, 128, 8, 16)])

                    def qhat(e, QHAT=QHAT, ps=ps):
                        i1 = bass.AP(RSQ.h, 8, [[20, 128], [1, 8], [0, 64]])
                        return e.tensor_tensor(out=QHAT[:, :], in0=ps[:, :], in1=i1,
                                               op=ALU.mult)
                    p.op("v", qhat, reads=[rfull(ps), r(RSQ, 0, 128, 8, 16)],
                         writes=[rfull(QHAT)])

                    def t2f(e, CST=CST, tt=tt, QT32=QT32, QHAT=QHAT):
                        i1 = bass.AP(CST.h, tt * 64, [[256, 128], [0, 8], [1, 64]])
                        return e.tensor_tensor(out=QT32[:, :], in0=QHAT[:, :], in1=i1,
                                               op=ALU.mult)
                    p.op("v", t2f, reads=[rfull(QHAT), rfull(CST)], writes=[rfull(QT32)])

                    def t3a(e, SST=SST, tt=tt, QHAT=QHAT, T3B=T3B):
                        i0 = bass.AP(QHAT.h, 32, [[512, 128], [64, 8], [1, 32]])
                        i1 = bass.AP(SST.h, tt * 64, [[256, 128], [0, 8], [1, 32]])
                        o0 = bass.AP(T3B.h, 0, [[512, 128], [64, 8], [1, 32]])
                        return e.tensor_tensor(out=o0, in0=i0, in1=i1, op=ALU.mult)
                    p.op("g", t3a, reads=[rfull(QHAT), rfull(SST)],
                         writes=[r(T3B, 0, 128, 0, 512)])

                    def t3b(e, SST=SST, tt=tt, QHAT=QHAT, T3B=T3B):
                        i0 = bass.AP(QHAT.h, 0, [[512, 128], [64, 8], [1, 32]])
                        i1 = bass.AP(SST.h, tt * 64 + 32, [[256, 128], [0, 8], [1, 32]])
                        o0 = bass.AP(T3B.h, 32, [[512, 128], [64, 8], [1, 32]])
                        return e.tensor_tensor(out=o0, in0=i0, in1=i1, op=ALU.mult)
                    p.op("g", t3b, reads=[rfull(QHAT), rfull(SST)],
                         writes=[r(T3B, 0, 128, 0, 512)])
                    p.op("g", lambda e, QKB2=QKB2, QT32=QT32, T3B=T3B: e.tensor_add(QKB2[:, :], QT32[:, :], T3B[:, 0:512]),
                         reads=[rfull(QT32), r(T3B, 0, 128, 0, 512)], writes=[rfull(QKB2)])
                    for hp in range(4):
                        p.op("t", lambda e, hp=hp, QKB2=QKB2: e.transpose(
                            PTRB[:, 128 * hp:128 * (hp + 1)], QKB2[:, 128 * hp:128 * (hp + 1)],
                            IDNB[:, :]),
                            reads=[r(QKB2, 0, 128, 128 * hp, 128 * (hp + 1)), rfull(IDNB)],
                            writes=[r(PTRB, 0, 128, 128 * hp, 128 * (hp + 1))])
                    dbuf, dp0 = QTEv if iq == 0 else KTEv[wb]
                    for par in range(2):
                        def tev(e, dbuf=dbuf, dp0=dp0, par=par, tt=tt):
                            dst = bass.AP(dbuf.h, dp0 * 4096 + 512 * par + tt * 128,
                                          [[8 * 512, 64], [1024, 4], [1, 128]])
                            src = bass.AP(PTRB.h, 64 * par * 512,
                                          [[512, 64], [128, 4], [1, 128]])
                            return e.tensor_copy(out=dst, in_=src)
                        p.op("v", tev, reads=[rfull(PTRB)],
                             writes=[r(dbuf, dp0, dp0 + 64, 0, 4096)])

                def head_attention(w, h, pav):
                    wb, pb = w % 2, (w - 1) % 2
                    jts = ([("p", jl) for jl in range(4)] if w > 0 else []) +                           [("c", jl) for jl in range(4)]
                    njts = len(jts)
                    units = []
                    for jidx, (kind, jl) in enumerate(jts):
                        ATTT = ATTTr[jidx % 4]
                        PP2 = (PTRN, PB[1])[jidx % 2]
                        if kind == "p":
                            kbuf, kp0 = KTEv[pb]
                            i0, i1 = 0, 128 * (jl + 1)
                            bnd0, bmask = 128 * jl, 128
                            vb = VE[pb]
                        else:
                            kbuf, kp0 = KTEv[wb]
                            i0, i1 = 128 * jl, 512
                            bnd0, bmask = 0, 0
                            vb = VE[wb]
                        units.append((jidx, jl, ATTT, PP2, kbuf, kp0, i0, i1,
                                      bnd0, bmask, vb))

                    def emit_sim(u):
                        jidx, jl, ATTT, PP2, kbuf, kp0, i0, i1, bnd0, bmask, vb = u
                        span = i1 - i0
                        p.op("t", lambda e, kbuf=kbuf, kp0=kp0, jl=jl, h=h, i0=i0, i1=i1,
                             PP2=PP2: e.matmul(
                            PP2[:, i0:i1],
                            kbuf[kp0:kp0 + 64, 512 * h + 128 * jl:512 * h + 128 * (jl + 1)],
                            QKTE[0:64, 512 * h + i0:512 * h + i1], start=True, stop=True),
                            reads=[r(kbuf, kp0, kp0 + 64, 512 * h + 128 * jl, 512 * h + 128 * (jl + 1)),
                                   r(QKTE, 0, 64, 512 * h + i0, 512 * h + i1)],
                            writes=[rfull(PP2)])
                        p.op("s", lambda e, i0=i0, span=span, PP2=PP2, ATTT=ATTT: e.activation(
                            out=ATTT[:, 0:span], in_=PP2[:, i0:i0 + span], func=AF.Exp),
                            reads=[rfull(PP2)], writes=[r(ATTT, 0, 128, 0, span)])
                        p.op("v", lambda e, bnd0=bnd0, bmask=bmask, ATTT=ATTT: e.tensor_mul(
                            ATTT[:, bnd0:bnd0 + 128], ATTT[:, bnd0:bnd0 + 128],
                            MCAT[:, bmask:bmask + 128]),
                            reads=[r(ATTT, 0, 128, bnd0, bnd0 + 128),
                                   r(MCAT, 0, 128, bmask, bmask + 128)],
                            writes=[r(ATTT, 0, 128, bnd0, bnd0 + 128)])

                    def emit_av(u):
                        jidx, jl, ATTT, PP2, kbuf, kp0, i0, i1, bnd0, bmask, vb = u
                        span = i1 - i0
                        p.op("t", lambda e, vb=vb, h=h, jl=jl, i0=i0, span=span,
                             first=(jidx == 0), last=(jidx == njts - 1), pav=pav,
                             ATTT=ATTT: e.matmul(
                            pav[0:65, i0:i0 + span], vb[:, 260 * h + 65 * jl:260 * h + 65 * (jl + 1)],
                            ATTT[:, 0:span], start=first, stop=last),
                            reads=[r(vb, 0, 128, 260 * h + 65 * jl, 260 * h + 65 * (jl + 1)),
                                   r(ATTT, 0, 128, 0, span)],
                            writes=[rfull(pav)])

                    for k in range(njts + 2):
                        if k < njts:
                            emit_sim(units[k])
                        if k >= 2:
                            emit_av(units[k - 2])

                def pair_out(w, hp, pava, pavb):
                    for pav, rcdst in ((pava, RC2A), (pavb, RC2B)):
                        p.op("s", lambda e, pav=pav, rcdst=rcdst: e.activation(
                            out=rcdst[0:1, :], in_=pav[64:65, :], func=AF.Ln),
                            reads=[rfull(pav)], writes=[rfull(rcdst)])
                        p.op("s", lambda e, rcdst=rcdst: e.activation(
                            out=rcdst[0:1, :], in_=rcdst[0:1, :], func=AF.Exp, scale=-1.0),
                            reads=[rfull(rcdst)], writes=[rfull(rcdst)])
                    p.op("t", lambda e: e.matmul(PROW[:, :], SEL2A[:, :], RC2A[0:1, :],
                                                 start=True, stop=False),
                         reads=[rfull(SEL2A), rfull(RC2A)], writes=[rfull(PROW)])
                    p.op("t", lambda e: e.matmul(PROW[:, :], SEL2B[:, :], RC2B[0:1, :],
                                                 start=False, stop=True),
                         reads=[rfull(SEL2B), rfull(RC2B)], writes=[rfull(PROW)])
                    p.op("v", lambda e: e.tensor_copy(out=RCBS[:, :], in_=PROW[:, :]),
                         reads=[rfull(PROW)], writes=[rfull(RCBS)])
                    for par, pav in ((0, pava), (1, pavb)):
                        p.op("v", lambda e, par=par, pav=pav, hp=hp: e.tensor_mul(
                            OT[64 * par:64 * (par + 1), 512 * hp:512 * (hp + 1)],
                            pav[0:64, :], RCBS[64 * par:64 * (par + 1), :]),
                            reads=[rfull(pav), r(RCBS, 64 * par, 64 * (par + 1), 0, 512)],
                            writes=[r(OT, 64 * par, 64 * (par + 1), 512 * hp, 512 * (hp + 1))])

                def wo_out(w):
                    t0 = w * 512
                    if l == 1:
                        def sv(e, t0=t0):
                            src = bass.AP(Z.h, tcol(0, t0 + 511), [[NCC * TC, 128], [TC, 8], [1, 1]])
                            return e.tensor_copy(out=SVCOL[:, :], in_=src)
                        p.op("v", sv, reads=[r(Z, 0, 128, 0, Z.shape[1])], writes=[rfull(SVCOL)])
                    for qh in range(4):
                        WOF = WOFr[qh % 2]
                        for j2 in range(2):
                            oc = 2 * qh + j2
                            ps = PB[oc % 4]
                            for icc in range(4):
                                p.op("t", lambda e, ps=ps, WOF=WOF, j2=j2, icc=icc: e.matmul(
                                    ps[:, :],
                                    WOF[:, 256 * icc + 128 * j2:256 * icc + 128 * (j2 + 1)],
                                    OT[:, 512 * icc:512 * (icc + 1)],
                                    start=(icc == 0), stop=(icc == 3)),
                                    reads=[r(WOF, 0, 128, 256 * icc + 128 * j2,
                                             256 * icc + 128 * (j2 + 1)),
                                           r(OT, 0, 128, 512 * icc, 512 * (icc + 1))],
                                    writes=[rfull(ps)])
                            p.op("v", lambda e, ps=ps, oc=oc, t0=t0: e.tensor_add(
                                Z[:, tcol(oc, t0):tcol(oc, t0) + 512],
                                R[:, tcol(oc, t0):tcol(oc, t0) + 512], ps[:, :]),
                                reads=[r(R, 0, 128, tcol(oc, t0), tcol(oc, t0) + 512), rfull(ps)],
                                writes=[r(Z, 0, 128, tcol(oc, t0), tcol(oc, t0) + 512)])
                        if qh + 2 < 4:
                            load_wof(qh + 2)

                for w in range(NW):
                    emit_tables(w)
                    advance(back=(w == NW - 1))
                    for cob in range(3):
                        if not (cob == 0 and w > 0):
                            load_qkv_slab(cob)
                        for tt in range(4):
                            sh_build(w, tt)
                            ps = qkv_cob(w, cob, tt)
                            if cob == 2:
                                v_evac(w, tt, ps)
                            else:
                                qk_process(w, cob, tt, ps)
                    if w + 1 < NW:
                        load_qkv_slab(0)
                    load_wof(0)
                    load_wof(1)
                    for hp in range(4):
                        pava, pavb = ((PAV, PB[0]), (PB[2], PB[3]))[hp % 2]
                        for par in range(2):
                            h = 2 * hp + par
                            head_attention(w, h, pava if par == 0 else pavb)
                        pair_out(w, hp, pava, pavb)
                    wo_out(w)
                if b in dump_batches:
                    dd = {"QTE": QKTE, "KTE0": KTE0, "VE0": VE[0], "OT": OT,
                          "CSQ": CSQr[0], "SSQ": SSQr[0], "CSK": CSKr[0], "SSK": SSKr[0]}
                    for nm in dd:
                        if nm in dump_stages:
                            dump(dd[nm])
                p.phase_barrier()


        HGROUPS = [5, 5, 4, 4, 4]

        def ffn_mlp_batch(b, l):
            with ExitStack() as ph:
                sb = lambda name, shape, dt: Buf(
                    ph.enter_context(nc.sbuf_tensor(p.uname(name), list(shape), dt)),
                    name, tuple(shape), dt)
                LNZ = sb("LNZ", (128, NCC * 2048), BF16)
                MGELr = [sb("MGEL", (128, 512), F32)] * 2
                p.bufs["LNROWV"] = ((SUMT, 0, 0), (MGELr[0], 0, 0))
                WA = sb("WA", (128, 5 * 1024), BF16)
                WG = sb("WG", (128, 5 * 1024), BF16)
                W2P = sb("W2P", (128, 5 * 1024), BF16)
                HTG = sb("HTG", (128, 5 * 512), BF16)
                LNGC = sb("LNGC", (128, 16), F32)
                B1AC = sb("B1AC", (128, NHC), F32)
                B1GC = sb("B1GC", (128, NHC), F32)
                B2C = sb("B2C", (128, 8), F32)

                def cl(dstbuf, c0, nm, ncols):
                    def fn(e, dstbuf=dstbuf, c0=c0, nm=nm, ncols=ncols):
                        src = bass.AP(dt_in[nm], 0, [[1, 128], [128, ncols]])
                        return e.dma_start(out=dstbuf[:, c0:c0 + ncols], in_=src,
                                           allow_slow_non_contiguous=True)
                    p.op_slow_dma("g", fn, writes=[r(dstbuf, 0, 128, c0, c0 + ncols)])
                cl(LNGC, 0, f"lng{l}", 8)
                cl(LNGC, 8, f"lnb{l}", 8)
                cl(B1AC, 0, f"b1a{l}", NHC)
                cl(B1GC, 0, f"b1g{l}", NHC)
                cl(B2C, 0, f"b2_{l}", 8)

                def zsh(cc, t0, n=512):
                    base = tcol(cc, t0) - (1 if cc >= 4 else 0)
                    return Z[:, base:base + n], r(Z, 0, 128, base, base + n)

                for tb in range(4):
                    t0 = tb * 512
                    for cc in range(NCC):
                        ap, rd = zsh(cc, t0)
                        cb, cb0 = (STATB, 0) if cc % 2 == 0 else (HTG, 512)
                        qb, qb0 = (HTG, 0) if cc % 2 == 0 else (HTG, 1024)
                        p.op("s", lambda e, ap=ap, cb=cb, cb0=cb0: e.copy(
                            out=cb[:, cb0:cb0 + 512], in_=ap),
                            reads=[rd], writes=[r(cb, 0, 128, cb0, cb0 + 512)])
                        p.op("t", lambda e, cc=cc, cb=cb, cb0=cb0: e.matmul(
                            PROW[0:1, :], ONE1B[:, :], cb[:, cb0:cb0 + 512],
                            start=(cc == 0), stop=(cc == NCC - 1)),
                            reads=[rfull(ONE1B), r(cb, 0, 128, cb0, cb0 + 512)],
                            writes=[r(PROW, 0, 1, 0, 512)])
                        p.op("s", lambda e, ap=ap, qb=qb, qb0=qb0: e.activation(
                            out=qb[:, qb0:qb0 + 512], in_=ap, func=AF.Square),
                            reads=[rd], writes=[r(qb, 0, 128, qb0, qb0 + 512)])
                        p.op("t", lambda e, cc=cc, qb=qb, qb0=qb0: e.matmul(
                            PAV[0:1, :], ONE1B[:, :], qb[:, qb0:qb0 + 512],
                            start=(cc == 0), stop=(cc == NCC - 1)),
                            reads=[rfull(ONE1B), r(qb, 0, 128, qb0, qb0 + 512)],
                            writes=[r(PAV, 0, 1, 0, 512)])
                    ln_rows_mid()
                    for cc in range(NCC):
                        ap, rd = zsh(cc, t0)
                        TB1 = SUMT
                        p.op("v", lambda e, ap=ap, TB1=TB1: e.tensor_mul(TB1[:, :], ap, PROW[:, :]),
                             reads=[rd, rfull(PROW)], writes=[rfull(TB1)])
                        def b2f(e, cc=cc):
                            i1 = LNGC[:, 8 + cc:9 + cc].broadcast_to([128, 512])
                            return e.scalar_tensor_tensor(
                                out=STATB[:, 0:512], in0=PTRN[:, :],
                                scalar=LNGC[:, cc:cc + 1], in1=i1,
                                op0=ALU.mult, op1=ALU.add)
                        p.op("v", b2f, reads=[rfull(PTRN), r(LNGC, 0, 128, 0, 16)],
                             writes=[r(STATB, 0, 128, 0, 512)])
                        p.op("v", lambda e, cc=cc, t0=t0, TB1=TB1: e.scalar_tensor_tensor(
                            out=LNZ[:, 2048 * cc + t0:2048 * cc + t0 + 512], in0=TB1[:, :],
                            scalar=LNGC[:, cc:cc + 1], in1=STATB[:, 0:512],
                            op0=ALU.mult, op1=ALU.add),
                            reads=[rfull(TB1), r(LNGC, 0, 128, 0, 8),
                                   r(STATB, 0, 128, 0, 512)],
                            writes=[r(LNZ, 0, 128, 2048 * cc + t0, 2048 * cc + t0 + 512)])

                hid0 = 0
                for gi, gsz in enumerate(HGROUPS):
                    for wsl, nm in ((WA, f"wa{l}"), (WG, f"wg{l}")):
                        for cc in range(NCC):
                            def ldp(e, wsl=wsl, nm=nm, hid0=hid0, gsz=gsz, cc=cc):
                                src = bass.AP(WCAST[nm][0], hid0 * 128 + cc * 128 * HIDP,
                                              [[HIDP, 128], [128, gsz], [1, 128]])
                                dst = bass.AP(wsl.h, cc * 128,
                                              [[wsl.shape[1], 128], [1024, gsz], [1, 128]])
                                return e.dma_start(out=dst, in_=src)
                            p.op("y", ldp, writes=[rfull(wsl)], dma=True)
                    def ldw2(e, hid0=hid0, gsz=gsz):
                        src = bass.AP(WCAST[f"w2_{l}"][0], hid0 * 128 * 1024,
                                      [[1024, 128], [128 * 1024, gsz], [1, 1024]])
                        dst = bass.AP(W2P.h, 0,
                                      [[W2P.shape[1], 128], [1024, gsz], [1, 1024]])
                        return e.dma_start(out=dst, in_=src)
                    p.op("y", ldw2, writes=[rfull(W2P)], dma=True)
                    for tb in range(4):
                        t0 = tb * 512
                        for j in range(gsz):
                            hidc = hid0 + j
                            MGEL = MGELr[j % 2]
                            psa = (PB[0], PB[2])[j % 2]
                            psg = (PB[1], PB[3])[j % 2]
                            for ps, wsl in ((psa, WA), (psg, WG)):
                                for cc in range(NCC):
                                    p.op("t", lambda e, ps=ps, wsl=wsl, j=j, cc=cc, t0=t0: e.matmul(
                                        ps[:, :], wsl[:, 1024 * j + 128 * cc:1024 * j + 128 * (cc + 1)],
                                        LNZ[:, 2048 * cc + t0:2048 * cc + t0 + 512],
                                        start=(cc == 0), stop=(cc == NCC - 1)),
                                        reads=[r(wsl, 0, 128, 1024 * j + 128 * cc,
                                                 1024 * j + 128 * (cc + 1)),
                                               r(LNZ, 0, 128, 2048 * cc + t0, 2048 * cc + t0 + 512)],
                                        writes=[rfull(ps)])
                            p.op("s", lambda e, hidc=hidc, MGEL=MGEL, psg=psg: e.activation(
                                out=MGEL[:, :], in_=psg[:, :], func=AF.Gelu,
                                bias=B1GC[:, hidc:hidc + 1]),
                                reads=[rfull(psg), r(B1GC, 0, 128, hidc, hidc + 1)],
                                writes=[rfull(MGEL)])
                            p.op("v", lambda e, hidc=hidc, j=j, MGEL=MGEL, psa=psa:
                                 e.scalar_tensor_tensor(
                                out=HTG[:, 512 * j:512 * (j + 1)], in0=psa[:, :],
                                scalar=B1AC[:, hidc:hidc + 1], in1=MGEL[:, :],
                                op0=ALU.add, op1=ALU.mult),
                                reads=[rfull(psa), r(B1AC, 0, 128, hidc, hidc + 1),
                                       rfull(MGEL)],
                                writes=[r(HTG, 0, 128, 512 * j, 512 * (j + 1))])
                        for och in range(2):
                            for oci in range(4):
                                oc = och * 4 + oci
                                ps = (PB[0], PB[1], PTRN, PB[2])[oci]
                                for j in range(gsz):
                                    p.op("t", lambda e, ps=ps, j=j, oc=oc: e.matmul(
                                        ps[:, :], W2P[:, 1024 * j + 128 * oc:1024 * j + 128 * (oc + 1)],
                                        HTG[:, 512 * j:512 * (j + 1)],
                                        start=(j == 0), stop=(j == gsz - 1)),
                                        reads=[r(W2P, 0, 128, 1024 * j + 128 * oc,
                                                 1024 * j + 128 * (oc + 1)),
                                               r(HTG, 0, 128, 512 * j, 512 * (j + 1))],
                                        writes=[rfull(ps)])
                                if gi == 0:
                                    p.op("v", lambda e, ps=ps, oc=oc, t0=t0: e.scalar_tensor_tensor(
                                        out=Z[:, tcol(oc, t0):tcol(oc, t0) + 512], in0=ps[:, :],
                                        scalar=B2C[:, oc:oc + 1],
                                        in1=Z[:, tcol(oc, t0):tcol(oc, t0) + 512],
                                        op0=ALU.add, op1=ALU.add),
                                        reads=[rfull(ps), r(B2C, 0, 128, oc, oc + 1),
                                               r(Z, 0, 128, tcol(oc, t0), tcol(oc, t0) + 512)],
                                        writes=[r(Z, 0, 128, tcol(oc, t0), tcol(oc, t0) + 512)])
                                else:
                                    p.op("v", lambda e, ps=ps, oc=oc, t0=t0: e.tensor_add(
                                        Z[:, tcol(oc, t0):tcol(oc, t0) + 512],
                                        Z[:, tcol(oc, t0):tcol(oc, t0) + 512], ps[:, :]),
                                        reads=[rfull(ps), r(Z, 0, 128, tcol(oc, t0),
                                                            tcol(oc, t0) + 512)],
                                        writes=[r(Z, 0, 128, tcol(oc, t0), tcol(oc, t0) + 512)])
                    hid0 += gsz
                if b in dump_batches and "LNZ" in dump_stages:
                    dump(LNZ)
                p.phase_barrier()

        def final_ln_store(b):
          with ExitStack() as ph:
            XTM = Buf(ph.enter_context(nc.sbuf_tensor(p.uname("XTM"), [128, 1024], F32)),
                      "XTM", (128, 1024), F32)
            for tt in range(T // 128):
                t0 = tt * 128
                for half in range(2):
                    for k in range(4):
                        cc = half * 4 + k
                        p.op("v", lambda e, cc=cc, k=k, t0=t0: e.tensor_add(
                            SUMT[:, 128 * k:128 * (k + 1)],
                            X1[:, tcol(cc, t0):tcol(cc, t0) + 128],
                            Z[:, tcol(cc, t0):tcol(cc, t0) + 128]),
                            reads=[r(X1, 0, 128, tcol(cc, t0), tcol(cc, t0) + 128),
                                   r(Z, 0, 128, tcol(cc, t0), tcol(cc, t0) + 128)],
                            writes=[r(SUMT, 0, 128, 128 * k, 128 * (k + 1))])
                    for k in range(4):
                        p.op("t", lambda e, k=k: e.transpose(
                            PTRN[:, 128 * k:128 * (k + 1)], SUMT[:, 128 * k:128 * (k + 1)],
                            IDNF[:, :]),
                            reads=[r(SUMT, 0, 128, 128 * k, 128 * (k + 1)), rfull(IDNF)],
                            writes=[r(PTRN, 0, 128, 128 * k, 128 * (k + 1))])
                    p.op("s", lambda e, half=half: e.copy(
                        out=XTM[:, 512 * half:512 * (half + 1)], in_=PTRN[:, :]),
                        reads=[rfull(PTRN)],
                        writes=[r(XTM, 0, 128, 512 * half, 512 * (half + 1))])
                p.op("v", lambda e: e.tensor_reduce(MUC[:, 0:1], XTM[:, :], axis=AX.X,
                                                    op=ALU.add),
                     reads=[rfull(XTM)], writes=[r(MUC, 0, 128, 0, 1)])
                p.op("s", lambda e, XTM=XTM: e.activation(
                    out=STATB[:, :], in_=XTM[:, 0:512], func=AF.Square,
                    accum_out=MUC[:, 1:2]),
                    reads=[rfull(XTM)], writes=[rfull(STATB), r(MUC, 0, 128, 1, 2)])
                p.op("s", lambda e, XTM=XTM: e.activation(
                    out=STATB[:, :], in_=XTM[:, 512:1024], func=AF.Square,
                    accum_out=MUC[:, 3:4]),
                    reads=[rfull(XTM)], writes=[rfull(STATB), r(MUC, 0, 128, 3, 4)])
                p.op("v", lambda e: e.tensor_add(MUC[:, 1:2], MUC[:, 1:2], MUC[:, 3:4]),
                     reads=[r(MUC, 0, 128, 1, 4)], writes=[r(MUC, 0, 128, 1, 2)])
                p.op("v", lambda e: e.tensor_scalar_mul(MUC[:, 0:1], MUC[:, 0:1], 1.0 / C),
                     reads=[r(MUC, 0, 128, 0, 1)], writes=[r(MUC, 0, 128, 0, 1)])
                p.op("v", lambda e: e.tensor_scalar_mul(MUC[:, 1:2], MUC[:, 1:2], 1.0 / C),
                     reads=[r(MUC, 0, 128, 1, 2)], writes=[r(MUC, 0, 128, 1, 2)])
                p.op("v", lambda e: e.tensor_mul(MUC[:, 2:3], MUC[:, 0:1], MUC[:, 0:1]),
                     reads=[r(MUC, 0, 128, 0, 1)], writes=[r(MUC, 0, 128, 2, 3)])
                p.op("v", lambda e: e.tensor_sub(MUC[:, 1:2], MUC[:, 1:2], MUC[:, 2:3]),
                     reads=[r(MUC, 0, 128, 1, 3)], writes=[r(MUC, 0, 128, 1, 2)])
                p.op("s", lambda e: e.activation(out=MUC[:, 1:2], in_=MUC[:, 1:2], func=AF.Ln,
                                                 bias=KEPS[:, 0:1]),
                     reads=[r(MUC, 0, 128, 1, 2), rfull(KEPS)], writes=[r(MUC, 0, 128, 1, 2)])
                p.op("s", lambda e: e.activation(out=MUC[:, 1:2], in_=MUC[:, 1:2], func=AF.Exp,
                                                 scale=-0.5),
                     reads=[r(MUC, 0, 128, 1, 2)], writes=[r(MUC, 0, 128, 1, 2)])
                p.op("v", lambda e: e.tensor_scalar(
                    out=XTM[:, :], in0=XTM[:, :], scalar1=MUC[:, 0:1], scalar2=MUC[:, 1:2],
                    op0=ALU.subtract, op1=ALU.mult),
                    reads=[rfull(XTM), r(MUC, 0, 128, 0, 2)], writes=[rfull(XTM)])
                tok = p.op("y", lambda e, b=b, t0=t0: e.dma_start(
                    out=out_t.ap()[b * T + t0:b * T + t0 + 128, :], in_=XTM[:, :]),
                    reads=[rfull(XTM)], dma=True)
                p.out_dma_tokens.append(tok)
            p.phase_barrier()

        dump_batches = {nbatch - 1}
        dump_stages = set(dumps)
        init_consts()
        init_rotary()
        init_wcast()
        for b in range(nbatch):
            load_input_batch(b)
            if "rwkv" in stages:
                rwkv_batch(b)
            else:
                for cc in range(NCC):
                    p.op("v", lambda e, cc=cc: e.tensor_copy(
                        out=X1[:, cc * TC:(cc + 1) * TC], in_=Z[:, cc * TC:(cc + 1) * TC]),
                        reads=[r(Z, 0, 128, cc * TC, (cc + 1) * TC)],
                        writes=[r(X1, 0, 128, cc * TC, (cc + 1) * TC)])
            for l in range(2):
                if f"att{l}" in stages or "att" in stages:
                    ffn_att_batch(b, l, X1 if l == 0 else Z)
                if f"mlp{l}" in stages or "mlp" in stages:
                    ffn_mlp_batch(b, l)
            final_ln_store(b)
        for nm in dump_stages:
            if nm in ("KH", "VH", "WSLAB", "XKB", "SUMT", "SCN", "EKS", "SBB"):
                continue
            if nm in p.bufs:
                dump(p.bufs[nm])

        p.wait_tokens("y", p.out_dma_tokens + dump_toks)

        block = ctx.enter_context(nc.Block())
        p.finalize_into_block(block)

    return nc


def _prep_weights(inputs):
    f = np.ascontiguousarray
    w = {}
    w["wkT"] = f(inputs["L0_attn_Wk"].T)
    w["wvT"] = f(inputs["L0_attn_Wv"].T)
    w["wrT"] = f(inputs["L0_attn_Wr"].T)
    w["woT"] = f(inputs["L0_attn_Wo"].T)
    w["mixk"] = f(inputs["L0_attn_mix_k"].reshape(-1))
    w["mixv"] = f(inputs["L0_attn_mix_v"].reshape(-1))
    w["mixr"] = f(inputs["L0_attn_mix_r"].reshape(-1))
    w["tdec"] = f(inputs["L0_attn_time_decay"])
    w["tfirst"] = f(inputs["L0_attn_time_first"])
    pad = HIDP - HID
    for l in range(2):
        w[f"wqkv{l}"] = f(inputs["L0_ffn_wqkv"][l])
        w[f"qsc{l}"] = f(inputs["L0_ffn_q_scale"][l])
        w[f"ksc{l}"] = f(inputs["L0_ffn_k_scale"][l])
        w[f"woffn{l}"] = f(inputs["L0_ffn_wo"][l])
        w[f"lng{l}"] = f(inputs["L0_ffn_ln_g"][l])
        w[f"lnb{l}"] = f(inputs["L0_ffn_ln_b"][l])
        w1 = inputs["L0_ffn_w1"][l]
        b1 = inputs["L0_ffn_b1"][l]
        w[f"wa{l}"] = np.pad(w1[:, :HID], ((0, 0), (0, pad)))
        w[f"wg{l}"] = np.pad(w1[:, HID:], ((0, 0), (0, pad)))
        w[f"b1a{l}"] = np.pad(b1[:HID], (0, pad))
        w[f"b1g{l}"] = np.pad(b1[HID:], (0, pad))
        w[f"w2_{l}"] = np.pad(inputs["L0_ffn_w2"][l], ((0, pad), (0, 0)))
        w[f"b2_{l}"] = f(inputs["L0_ffn_b2"][l])
    return {k: np.ascontiguousarray(np.asarray(v, dtype=np.float32)) for k, v in w.items()}


def kernel(**inputs):
    x = np.asarray(inputs["L0_attn_x"], dtype=np.float32)
    w = _prep_weights({k: np.asarray(v, dtype=np.float32) for k, v in inputs.items()})
    nc = build_program()
    in_maps = []
    for c in range(NCORES):
        m = dict(w)
        m["x"] = np.ascontiguousarray(x[c * B_LOC:(c + 1) * B_LOC].reshape(B_LOC * T, C))
        in_maps.append(m)
    res = run_bass_kernel_spmd(nc, in_maps, core_ids=list(range(NCORES)))
    out = np.concatenate([res.results[c]["out"].reshape(B_LOC, T, C) for c in range(NCORES)],
                         axis=0)
    return out
